# version 11
# baseline (speedup 1.0000x reference)
import numpy as np
import concourse.bass as bass
import concourse.mybir as mybir
from concourse.bass_utils import run_bass_kernel_spmd

F32 = mybir.dt.float32
BF16 = mybir.dt.bfloat16
I32 = mybir.dt.int32
AF = mybir.ActivationFunctionType
ALU = mybir.AluOpType

D = 1024
E = 2048
N = 64
R = 64
HALF = 4096
T = 1024
NSUP = HALF // T
G = 4
NG = N // G
NET = E // 128
NDT = D // 128
NHT = 3072 // 128
TS3 = 512
LN_EPS = 1e-5
QS = 1.0 / 127.0


def build_program(debug=False):
    nc = bass.Bass("TRN2", target_bir_lowering=False)
    skind = "ExternalOutput" if debug else "Internal"

    x_loc = nc.dram_tensor("x_loc", [D, HALF], F32, kind="ExternalInput").ap()
    W = {}
    for ph in ("p1", "p2"):
        W[ph] = dict(
            WinT=nc.dram_tensor(f"{ph}_WinT", [D, 2 * E], F32, kind="ExternalInput").ap(),
            WxpT=nc.dram_tensor(f"{ph}_WxpT", [E, 192], F32, kind="ExternalInput").ap(),
            WdtT=nc.dram_tensor(f"{ph}_WdtT", [R, E], F32, kind="ExternalInput").ap(),
            WoutT=nc.dram_tensor(f"{ph}_WoutT", [E, D], F32, kind="ExternalInput").ap(),
            Alog=nc.dram_tensor(f"{ph}_Alog", [128, NET * N], F32, kind="ExternalInput").ap(),
            convw=nc.dram_tensor(f"{ph}_convw", [128, NET * 4], F32, kind="ExternalInput").ap(),
            convb=nc.dram_tensor(f"{ph}_convb", [128, NET], F32, kind="ExternalInput").ap(),
            dtb=nc.dram_tensor(f"{ph}_dtb", [128, NET], F32, kind="ExternalInput").ap(),
            Dp=nc.dram_tensor(f"{ph}_Dp", [128, NET], F32, kind="ExternalInput").ap(),
        )
    ln1w = nc.dram_tensor("ln1w", [128, NDT], F32, kind="ExternalInput").ap()
    ln1b = nc.dram_tensor("ln1b", [128, NDT], F32, kind="ExternalInput").ap()
    ln2w = nc.dram_tensor("ln2w", [128, NDT], F32, kind="ExternalInput").ap()
    ln2b = nc.dram_tensor("ln2b", [128, NDT], F32, kind="ExternalInput").ap()
    wgT = nc.dram_tensor("wgT", [D, 3072], I32, kind="ExternalInput").ap()
    wuT = nc.dram_tensor("wuT", [D, 3072], I32, kind="ExternalInput").ap()
    wdT = nc.dram_tensor("wdT", [3072, D], I32, kind="ExternalInput").ap()
    out_feat = nc.dram_tensor("out_feat", [D, HALF], F32, kind="ExternalOutput").ap()

    xn1_fwd = nc.dram_tensor("xn1_fwd", [D, HALF], BF16, kind=skind).ap()
    xn1_rev = nc.dram_tensor("xn1_rev", [D, HALF], BF16, kind=skind).ap()
    xno_rev = nc.dram_tensor("xno_rev", [D, HALF], BF16, kind=skind).ap()
    mamba_a = nc.dram_tensor("mamba_a", [D, HALF], BF16, kind=skind).ap()
    mamba_b = nc.dram_tensor("mamba_b", [D, HALF], BF16, kind=skind).ap()
    xc_scr = nc.dram_tensor("xc_scr", [E, HALF], BF16, kind=skind).ap()
    z_scr = nc.dram_tensor("z_scr", [E, HALF], BF16, kind="Internal").ap()
    dt_scr = nc.dram_tensor("dt_scr", [E, HALF], BF16, kind=skind).ap()
    dtx_scr = nc.dram_tensor("dtx_scr", [E, HALF], BF16, kind="Internal").ap()
    bc_scr = nc.dram_tensor("bc_scr", [128, HALF], BF16, kind=skind).ap()
    h_dbg = nc.dram_tensor("h_dbg", [3072, T], BF16, kind=skind).ap()
    g_dbg = nc.dram_tensor("g_dbg", [D, T], BF16, kind=skind).ap()
    cc_in_t = nc.dram_tensor("cc_in", [128, 1072], F32)
    cc_out_t = nc.dram_tensor("cc_out", [128, 1072], F32)

    import contextlib
    with contextlib.ExitStack() as _stk:
        xn1_sb = _stk.enter_context(nc.sbuf_tensor("xn1_sb", [128, NDT * T], BF16))
        xno_sb = _stk.enter_context(nc.sbuf_tensor("xno_sb", [128, NDT * T], BF16))
        ar1 = _stk.enter_context(nc.sbuf_tensor("ar1", [128, 16384], BF16))
        ar2 = _stk.enter_context(nc.sbuf_tensor("ar2", [128, 16384], BF16))
        ar3 = _stk.enter_context(nc.sbuf_tensor("ar3", [128, 8192], BF16))
        brow = _stk.enter_context(nc.sbuf_tensor("brow", [1, 4096], BF16))
        crow = _stk.enter_context(nc.sbuf_tensor("crow", [1, 4096], BF16))
        xp_ext = _stk.enter_context(nc.sbuf_tensor("xp_ext", [128, T + 4], BF16))
        dtlo_bf = _stk.enter_context(nc.sbuf_tensor("dtlo_bf", [64, T], BF16))
        bstg = _stk.enter_context(nc.sbuf_tensor("bstg", [64, T], BF16))
        bstg2 = _stk.enter_context(nc.sbuf_tensor("bstg2", [64, T], BF16))
        state = _stk.enter_context(nc.sbuf_tensor("state", [128, 1072], F32))
        blob2 = _stk.enter_context(nc.sbuf_tensor("blob2", [128, 1072], F32))
        A_sb = _stk.enter_context(nc.sbuf_tensor("A_sb", [128, NET * N], F32))
        row_mu = brow[:].bitcast(F32)[:, 0:1024]
        row_u = brow[:].bitcast(F32)[:, 1024:2048]
        row_v = crow[:].bitcast(F32)[:, 0:1024]
        row_w = crow[:].bitcast(F32)[:, 1024:2048]
        bcst = _stk.enter_context(nc.sbuf_tensor("bcst", [128, 2048], F32))
        wstg = _stk.enter_context(nc.sbuf_tensor("wstg", [128, 2048], F32))
        wsb0 = _stk.enter_context(nc.sbuf_tensor("wsb0", [128, 2048], BF16))
        wsb1 = _stk.enter_context(nc.sbuf_tensor("wsb1", [128, 2048], BF16))
        wxp_sb = _stk.enter_context(nc.sbuf_tensor("wxp_sb", [128, NET * 192], BF16))
        wdt_sb = _stk.enter_context(nc.sbuf_tensor("wdt_sb", [64, E], BF16))
        cvt = _stk.enter_context(nc.sbuf_tensor("cvt", [128, T], BF16))
        xct = _stk.enter_context(nc.sbuf_tensor("xct", [128, T], BF16))
        zt = _stk.enter_context(nc.sbuf_tensor("zt", [128, T], BF16))
        upt = _stk.enter_context(nc.sbuf_tensor("upt", [128, T], BF16))
        sgt = _stk.enter_context(nc.sbuf_tensor("sgt", [128, TS3], BF16))
        ot = _stk.enter_context(nc.sbuf_tensor("ot", [128, TS3], F32))
        wrev = _stk.enter_context(nc.sbuf_tensor("wrev", [128, 512], BF16))
        yxc = _stk.enter_context(nc.sbuf_tensor("yxc", [128, T], BF16))
        yz = _stk.enter_context(nc.sbuf_tensor("yz", [128, T], BF16))
        wout_sb = _stk.enter_context(nc.sbuf_tensor("wout_sb", [128, 2048], BF16))
        convw_sb = _stk.enter_context(nc.sbuf_tensor("convw_sb", [128, NET * 4], F32))
        convb_sb = _stk.enter_context(nc.sbuf_tensor("convb_sb", [128, NET], F32))
        dtb_sb = _stk.enter_context(nc.sbuf_tensor("dtb_sb", [128, NET], F32))
        Dp_sb = _stk.enter_context(nc.sbuf_tensor("Dp_sb", [128, NET], F32))
        lnp = _stk.enter_context(nc.sbuf_tensor("lnp", [128, 4 * NDT], F32))
        ones_rb = _stk.enter_context(nc.sbuf_tensor("ones_rb", [1, 128], BF16))
        ones_rf = _stk.enter_context(nc.sbuf_tensor("ones_rf", [1, 128], F32))
        ones_cb = _stk.enter_context(nc.sbuf_tensor("ones_cb", [128, 1], BF16))
        ones_cf = _stk.enter_context(nc.sbuf_tensor("ones_cf", [128, 1], F32))
        epsr = _stk.enter_context(nc.sbuf_tensor("epsr", [1, 2], F32))
        pA = _stk.enter_context(nc.psum_tensor("pA", [128, T], F32))
        pB = _stk.enter_context(nc.psum_tensor("pB", [128, T], F32))
        pX = _stk.enter_context(nc.psum_tensor("pX", [128, T], F32))
        pR = _stk.enter_context(nc.psum_tensor("pR", [1, 512], F32))
        pR2 = _stk.enter_context(nc.psum_tensor("pR2", [1, 512], F32))
        s_ld = _stk.enter_context(nc.semaphore("s_ld"))
        s_st = _stk.enter_context(nc.semaphore("s_st"))
        s_pe = _stk.enter_context(nc.semaphore("s_pe"))
        s_ve = _stk.enter_context(nc.semaphore("s_ve"))
        s_ac = _stk.enter_context(nc.semaphore("s_ac"))
        s_cc = _stk.enter_context(nc.semaphore("s_cc"))
        block = _stk.enter_context(nc.Block())
        SEM = dict(ld=s_ld, st=s_st, pe=s_pe, ve=s_ve, ac=s_ac, cc=s_cc)

        def prog(role, eng):
            cnt = {k: 0 for k in SEM}
            on = lambda r: role == r
            act_done_at = []
            dve_done_at = []
            scan_it = [0]

            def bump(sem, n, instr, r):
                cnt[sem] += n
                if instr is not None and role == r:
                    instr.then_inc(SEM[sem], n)

            def wt(r, sem, val=None):
                if role == r:
                    eng.wait_ge(SEM[sem], cnt[sem] if val is None else val)

            def ld(dst, src):
                i = eng.dma_start(out=dst, in_=src) if on("y") else None
                bump("ld", 16, i, "y")

            def st(dst, src):
                i = eng.dma_start(out=dst, in_=src) if on("g") else None
                bump("st", 16, i, "g")

            def V(fn, sem=None):
                i = fn(eng) if on("v") else None
                if sem is not None:
                    bump(sem, 1, i, "v")

            def S(fn, sem=None):
                i = fn(eng) if on("s") else None
                if sem is not None:
                    bump(sem, 1, i, "s")

            def TE(fn, sem=None):
                i = fn(eng) if on("t") else None
                if sem is not None:
                    bump(sem, 1, i, "t")

            def bcast_row(src_row, hs, dst):
                wt("t", "ve", cnt["ve"])
                TE(lambda e: e.matmul(pA[:, 0:512], ones_rf[:], src_row[:, hs], start=True, stop=True), "pe")
                wt("v", "pe")
                V(lambda e: e.tensor_copy(dst[:, hs], pA[:, 0:512]), "ve")

            V(lambda e: e.memset(ones_rb[:], 1.0))
            V(lambda e: e.memset(ones_rf[:], 1.0))
            V(lambda e: e.memset(ones_cb[:], 1.0))
            V(lambda e: e.memset(ones_cf[:], 1.0))
            V(lambda e: e.memset(epsr[:, 0:1], LN_EPS))
            V(lambda e: e.memset(epsr[:, 1:2], LN_EPS * LN_EPS), "ve")
            ld(lnp[:, 0:NDT], ln1w)
            ld(lnp[:, NDT:2 * NDT], ln1b)
            ld(lnp[:, 2 * NDT:3 * NDT], ln2w)
            ld(lnp[:, 3 * NDT:4 * NDT], ln2b)
            wt("t", "ve", cnt["ve"])

            xt = ar1[:].bitcast(F32)
            sqf = ar2[:].bitcast(F32)
            for s in range(NSUP):
                sl = slice(s * T, (s + 1) * T)
                wt("y", "pe", cnt["pe"])
                wt("y", "ve", cnt["ve"])
                for dti in range(NDT):
                    ld(xt[:, dti * T:(dti + 1) * T], x_loc[dti * 128:(dti + 1) * 128, sl])
                wt("s", "ld")
                for dti in range(NDT):
                    S(lambda e, i=dti: e.activation(out=sqf[:, i * T:(i + 1) * T], in_=xt[:, i * T:(i + 1) * T], func=AF.Square),
                      "ac" if dti == NDT - 1 else None)
                wt("t", "ac")
                wt("t", "ld")
                for h in range(2):
                    hs = slice(h * 512, (h + 1) * 512)
                    wt("t", "ve", cnt["ve"])
                    for dti in range(NDT):
                        TE(lambda e, i=dti, hs=hs: e.matmul(pR[:, :], ones_cf[:], xt[:, i * T:(i + 1) * T][:, hs],
                                                           start=(i == 0), stop=(i == NDT - 1)))
                    for dti in range(NDT):
                        TE(lambda e, i=dti, hs=hs: e.matmul(pR2[:, :], ones_cf[:], sqf[:, i * T:(i + 1) * T][:, hs],
                                                            start=(i == 0), stop=(i == NDT - 1)),
                           "pe" if dti == NDT - 1 else None)
                    wt("v", "pe")
                    V(lambda e, hs=hs: e.tensor_scalar(out=row_mu[:, hs], in0=pR[:, :], scalar1=1.0 / D, scalar2=None, op0=ALU.mult))
                    V(lambda e, hs=hs: e.tensor_scalar(out=row_u[:, hs], in0=pR2[:, :], scalar1=1.0 / D, scalar2=None, op0=ALU.mult),
                      "ve")
                wt("s", "ve")
                S(lambda e: e.activation(out=row_v, in_=row_mu, func=AF.Square), "ac")
                wt("v", "ac")
                wt("v", "ve")
                V(lambda e: e.tensor_sub(row_u, row_u, row_v), "ve")
                wt("s", "ve")
                S(lambda e: e.activation(out=row_v, in_=row_u, func=AF.Ln, bias=epsr[:, 0:1]), "ac")
                wt("s", "ac")
                S(lambda e: e.activation(out=row_v, in_=row_v, func=AF.Exp, scale=-0.5), "ac")
                wt("s", "ac")
                S(lambda e: e.activation(out=row_w, in_=row_u, func=AF.Ln, scale=1.0 + LN_EPS, bias=epsr[:, 1:2]), "ac")
                wt("s", "ac")
                S(lambda e: e.activation(out=row_w, in_=row_w, func=AF.Exp, scale=-0.5), "ac")
                wt("v", "ac")
                V(lambda e: e.tensor_mul(row_u, row_mu, row_v), "ve")
                wt("v", "ve")
                V(lambda e: e.tensor_mul(row_mu, row_mu, row_w), "ve")
                for pass_i, (ra, rb_) in enumerate(((row_v, row_u), (row_w, row_mu))):
                    for h in range(2):
                        hs = slice(h * 512, (h + 1) * 512)
                        bcast_row(ra, hs, bcst[:, 0:1024])
                        bcast_row(rb_, hs, bcst[:, 1024:2048])
                    for dti in range(NDT):
                        xs_ = xt[:, dti * T:(dti + 1) * T]
                        tmp = sqf[:, dti * T:(dti + 1) * T]
                        wt("v", "st", cnt["st"])
                        V(lambda e, xs_=xs_, tmp=tmp: e.tensor_mul(tmp, xs_, bcst[:, 0:1024]), "ve")
                        wt("v", "ve")
                        if pass_i == 0:
                            V(lambda e, tmp=tmp: e.tensor_sub(cvt[:], tmp, bcst[:, 1024:2048]), "ve")
                            wt("g", "ve")
                            st(xno_rev[dti * 128:(dti + 1) * 128, sl], cvt[:])
                        else:
                            V(lambda e, tmp=tmp: e.tensor_sub(tmp, tmp, bcst[:, 1024:2048]), "ve")
                            wt("v", "ve")
                            V(lambda e, tmp=tmp, dti=dti: e.tensor_scalar(
                                out=cvt[:], in0=tmp, scalar1=lnp[:, dti:dti + 1], scalar2=lnp[:, NDT + dti:NDT + dti + 1],
                                op0=ALU.mult, op1=ALU.add), "ve")
                            wt("g", "ve")
                            st(xn1_rev[dti * 128:(dti + 1) * 128, sl], cvt[:])
                            wt("v", "ve")
                            V(lambda e: e.tensor_copy(xct[:], cvt[:, ::-1]), "ve")
                            wt("g", "ve")
                            st(xn1_fwd[dti * 128:(dti + 1) * 128, HALF - (s + 1) * T:HALF - s * T], xct[:])

            def mamba_phase(ph, xn_src, out_rev):
                w = W[ph]
                wt("y", "ac", cnt["ac"])
                wt("y", "ve", cnt["ve"])
                wt("y", "pe", cnt["pe"])
                wt("y", "st", cnt["st"])
                ld(A_sb[:], w["Alog"])
                ld(convw_sb[:], w["convw"])
                ld(convb_sb[:], w["convb"])
                ld(dtb_sb[:], w["dtb"])
                ld(Dp_sb[:], w["Dp"])
                for ke in range(NET):
                    wt("y", "ac", cnt["ac"])
                    ld(wstg[:, 0:192], w["WxpT"][ke * 128:(ke + 1) * 128, :])
                    wt("s", "ld")
                    S(lambda e, ke=ke: e.activation(out=wxp_sb[:, ke * 192:(ke + 1) * 192], in_=wstg[:, 0:192], func=AF.Copy), "ac")
                for hh in range(2):
                    wt("y", "ac", cnt["ac"])
                    ld(wstg[0:64, 0:1024], w["WdtT"][:, hh * 1024:(hh + 1) * 1024])
                    wt("s", "ld")
                    S(lambda e, hh=hh: e.activation(out=wdt_sb[:, hh * 1024:(hh + 1) * 1024], in_=wstg[0:64, 0:1024], func=AF.Copy), "ac")
                wt("s", "ld")
                S(lambda e: e.activation(out=A_sb[:], in_=A_sb[:], func=AF.Exp), "ac")
                wt("v", "ac")
                V(lambda e: e.tensor_scalar(out=A_sb[:], in0=A_sb[:], scalar1=-1.0, scalar2=None, op0=ALU.mult), "ve")
                if ph == "p1":
                    V(lambda e: e.memset(state[:], 0.0), "ve")
                wt("s", "ve", cnt["ve"])

                cap = dict(wstg_ac=cnt["ac"], wsb_pe=[], xp_evict_ve=[], zsilu_ac=[],
                           xn1_pe=cnt["pe"], bcast_pe=cnt["pe"], evict_ac=cnt["ac"],
                           projcvt_ac=[], projmm_pe=[], conv_ve=[], xcz_st=[], last_win_mm=cnt["pe"],
                           wout_ve=cnt["ve"], wout_pe=cnt["pe"], l23_pe=cnt["pe"], l23_ve=cnt["ve"])
                proj_idx = [0]

                def emit_xn1_load(s):
                    wt("y", "pe", cap["last_win_mm"])
                    for dti in range(NDT):
                        ld(xn1_sb[:, dti * T:(dti + 1) * T], xn_src[dti * 128:(dti + 1) * 128, slice(s * T, (s + 1) * T)])
                    cap["xn1_ld"] = cnt["ld"]

                def emit_proj1_pe(s, et):
                    j = proj_idx[0]
                    par = j % 2
                    wsb = wsb0 if par == 0 else wsb1
                    wt("y", "ac", cap["wstg_ac"])
                    for kd in range(NDT):
                        ld(wstg[:, kd * 128:(kd + 1) * 128], w["WinT"][kd * 128:(kd + 1) * 128, et * 128:et * 128 + 128])
                    for kd in range(NDT):
                        ld(wstg[:, 1024 + kd * 128:1024 + (kd + 1) * 128],
                           w["WinT"][kd * 128:(kd + 1) * 128, E + et * 128:E + et * 128 + 128])
                    wt("s", "ld")
                    wt("s", "pe", cap["l23_pe"])
                    wt("s", "ve", cap["l23_ve"])
                    if j >= 2:
                        wt("s", "pe", cap["wsb_pe"][j - 2])
                    S(lambda e, wsb=wsb: e.activation(out=wsb[:], in_=wstg[:], func=AF.Copy), "ac")
                    cap["wstg_ac"] = cnt["ac"]
                    cap["projcvt_ac"].append(cnt["ac"])
                    wt("t", "ac", cnt["ac"])
                    wt("t", "ld", cap["xn1_ld"])
                    wt("t", "ve", cap["wout_ve"])
                    if j >= 1:
                        wt("t", "ve", cap["xp_evict_ve"][j - 1])
                        wt("t", "ac", cap["zsilu_ac"][j - 1]) if len(cap["zsilu_ac"]) >= j else None
                    for kd in range(NDT):
                        for c2 in range(2):
                            TE(lambda e, kd=kd, c2=c2, wsb=wsb: e.matmul(
                                pB[:, c2 * 512:(c2 + 1) * 512], wsb[:, kd * 128:(kd + 1) * 128],
                                xn1_sb[:, kd * T + c2 * 512:kd * T + (c2 + 1) * 512],
                                start=(kd == 0), stop=(kd == NDT - 1)),
                               "pe" if (kd == NDT - 1 and c2 == 1) else None)
                    for kd in range(NDT):
                        for c2 in range(2):
                            TE(lambda e, kd=kd, c2=c2, wsb=wsb: e.matmul(
                                pX[:, c2 * 512:(c2 + 1) * 512], wsb[:, 1024 + kd * 128:1024 + (kd + 1) * 128],
                                xn1_sb[:, kd * T + c2 * 512:kd * T + (c2 + 1) * 512],
                                start=(kd == 0), stop=(kd == NDT - 1)),
                               "pe" if (kd == NDT - 1 and c2 == 1) else None)
                    cap["wsb_pe"].append(cnt["pe"])
                    cap["projmm_pe"].append(cnt["pe"])
                    cap["last_win_mm"] = cnt["pe"]
                    proj_idx[0] += 1

                def emit_proj1_dve(s, et):
                    j = len(cap["conv_ve"])
                    wt("v", "pe", cap["projmm_pe"][j])
                    if j >= 1:
                        wt("v", "st", cap["xcz_st"][j - 1])
                    V(lambda e, et=et: e.tensor_copy(xp_ext[:, 1:4], state[:, 1024 + et * 3:1024 + et * 3 + 3]))
                    V(lambda e: e.tensor_copy(xp_ext[:, 4:4 + T], pB[:, :]), "ve")
                    cap["xp_evict_ve"].append(cnt["ve"])
                    V(lambda e, et=et: e.tensor_scalar(out=cvt[:], in0=xp_ext[:, 1:1 + T],
                                                       scalar1=convw_sb[:, et * 4:et * 4 + 1], scalar2=None, op0=ALU.mult))
                    for k in (1, 2, 3):
                        V(lambda e, k=k, et=et: e.scalar_tensor_tensor(
                            out=cvt[:], in0=xp_ext[:, 1 + k:1 + k + T], scalar=convw_sb[:, et * 4 + k:et * 4 + k + 1],
                            in1=cvt[:], op0=ALU.mult, op1=ALU.add), "ve" if k == 3 else None)
                    wt("v", "ve")
                    V(lambda e, et=et: e.tensor_copy(state[:, 1024 + et * 3:1024 + et * 3 + 3], xp_ext[:, 1 + T:4 + T]), "ve")
                    cap["conv_ve"].append(cnt["ve"])

                def emit_proj1_act(s, et):
                    j = len(cap["zsilu_ac"])
                    sl = slice(s * T, (s + 1) * T)
                    wt("s", "ve", cap["conv_ve"][j])
                    if j >= 1:
                        wt("s", "st", cap["xcz_st"][j - 1])
                    S(lambda e, et=et: e.activation(out=xct[:], in_=cvt[:], func=AF.Silu, bias=convb_sb[:, et:et + 1]))
                    S(lambda e: e.activation(out=zt[:], in_=pX[:, :], func=AF.Silu), "ac")
                    cap["zsilu_ac"].append(cnt["ac"])
                    wt("g", "ac", cnt["ac"])
                    st(xc_scr[et * 128:(et + 1) * 128, sl], xct[:])
                    st(z_scr[et * 128:(et + 1) * 128, sl], zt[:])
                    cap["xcz_st"].append(cnt["st"])

                def emit_loop23(s):
                    sl = slice(s * T, (s + 1) * T)
                    pe_at2 = []
                    for ke in range(NET):
                        par = ke % 2
                        dst = (wsb0 if par == 0 else wsb1)[:, 0:T]
                        wt("y", "st", cap["xcz_st"][-1] if cap["xcz_st"] else cnt["st"])
                        if ke >= 2:
                            wt("y", "pe", pe_at2[ke - 2])
                        else:
                            wt("y", "pe", cap["wsb_pe"][-1] if cap["wsb_pe"] else cnt["pe"])
                        ld(dst, xc_scr[ke * 128:(ke + 1) * 128, sl])
                        wt("t", "ld")
                        for c2 in range(2):
                            TE(lambda e, ke=ke, c2=c2, dst=dst: e.matmul(
                                pX[:, c2 * 512:(c2 + 1) * 512], wxp_sb[:, ke * 192:ke * 192 + 128],
                                dst[:, c2 * 512:(c2 + 1) * 512], start=(ke == 0), stop=(ke == NET - 1)))
                        for c2 in range(2):
                            TE(lambda e, ke=ke, c2=c2, dst=dst: e.matmul(
                                pB[0:64, c2 * 512:(c2 + 1) * 512], wxp_sb[:, ke * 192 + 128:(ke + 1) * 192],
                                dst[:, c2 * 512:(c2 + 1) * 512], start=(ke == 0), stop=(ke == NET - 1)),
                               "pe" if c2 == 1 else None)
                        pe_at2.append(cnt["pe"])
                    wt("v", "pe")
                    wt("v", "st", cnt["st"])
                    V(lambda e: e.tensor_copy(dtlo_bf[:], pX[0:64, :]))
                    V(lambda e: e.tensor_copy(bstg[:], pX[64:128, :]), "ve")
                    wt("g", "ve")
                    st(bc_scr[0:64, sl], bstg[:])
                    V(lambda e: e.tensor_copy(bstg2[:], pB[0:64, :]), "ve")
                    wt("g", "ve")
                    st(bc_scr[64:128, sl], bstg2[:])
                    for et in range(NET):
                        pp = pA if et % 2 == 0 else pX
                        wt("t", "ve", cnt["ve"])
                        wt("t", "ac", cnt["ac"] - (4 if et >= 1 else 0))
                        for c2 in range(2):
                            TE(lambda e, et=et, c2=c2, pp=pp: e.matmul(
                                pp[:, c2 * 512:(c2 + 1) * 512], wdt_sb[:, et * 128:(et + 1) * 128],
                                dtlo_bf[:, c2 * 512:(c2 + 1) * 512], start=True, stop=True),
                               "pe" if c2 == 1 else None)
                        wt("s", "pe")
                        wt("s", "ve", cnt["ve"])
                        S(lambda e, et=et, pp=pp: e.activation(out=cvt[:], in_=pp[:, :], func=AF.Abs, bias=dtb_sb[:, et:et + 1]), "ac")
                        wt("s", "ac")
                        S(lambda e: e.activation(out=cvt[:], in_=cvt[:], func=AF.Exp, scale=-1.0), "ac")
                        wt("s", "ac")
                        S(lambda e: e.activation(out=cvt[:], in_=cvt[:], func=AF.Ln, bias=ones_cf[:, 0:1]), "ac")
                        wt("s", "ac")
                        S(lambda e, et=et, pp=pp: e.activation(out=zt[:], in_=pp[:, :], func=AF.Relu, bias=dtb_sb[:, et:et + 1]), "ac")
                        par = et % 2
                        dst = (wsb0 if par == 0 else wsb1)[:, 0:T]
                        wt("y", "ve", cnt["ve"])
                        ld(dst, xc_scr[et * 128:(et + 1) * 128, sl])
                        wt("v", "ac")
                        wt("v", "ld")
                        wt("v", "st", cnt["st"])
                        V(lambda e: e.tensor_add(xct[:], cvt[:], zt[:]))
                        V(lambda e, dst=dst: e.tensor_mul(upt[:], xct[:], dst), "ve")
                        wt("g", "ve")
                        st(dt_scr[et * 128:(et + 1) * 128, sl], xct[:])
                        st(dtx_scr[et * 128:(et + 1) * 128, sl], upt[:])
                    cap["l23_pe"] = cnt["pe"]
                    cap["l23_ve"] = cnt["ve"]

                emit_xn1_load(0)
                for et in range(NET):
                    emit_proj1_pe(0, et)
                    emit_proj1_dve(0, et)
                    emit_proj1_act(0, et)
                emit_loop23(0)

                for s in range(NSUP):
                    sl = slice(s * T, (s + 1) * T)
                    nxt = s + 1 if s + 1 < NSUP else None
                    dA_ = ar1[:, 0:8192]
                    u_ = ar1[:, 8192:12288]
                    h_ = ar1[:, 12288:16384]
                    Bb = ar3[:, 0:4096]
                    Cb = ar3[:, 4096:8192]
                    yacc = ar2[:]
                    if nxt is not None:
                        emit_xn1_load(nxt)
                    for ng in range(NG):
                        wt("y", "pe", cap["bcast_pe"])
                        if ng == 0:
                            wt("y", "st", cnt["st"])
                        ld(brow[:].rearrange("o (p t) -> o p t", p=G),
                           bc_scr[ng * G:(ng + 1) * G, sl].unsqueeze(0))
                        ld(crow[:].rearrange("o (p t) -> o p t", p=G),
                           bc_scr[64 + ng * G:64 + (ng + 1) * G, sl].unsqueeze(0))
                        brow_ld = cnt["ld"]
                        wt("t", "ld", brow_ld)
                        if dve_done_at:
                            wt("t", "ve", dve_done_at[-1])
                        wt("t", "ac", cap["evict_ac"])
                        for bi, (rw, BB) in enumerate(((brow, Bb), (crow, Cb))):
                            for i in range(G):
                                pp = pA
                                for c2 in range(2):
                                    TE(lambda e, i=i, c2=c2, pp=pp, rw=rw: e.matmul(
                                        pp[:, c2 * 512:(c2 + 1) * 512], ones_rb[:],
                                        rw[:, i * T + c2 * 512:i * T + (c2 + 1) * 512], start=True, stop=True),
                                       "pe" if c2 == 1 else None)
                                wt("s", "pe")
                                if bi == 0 and i == 0:
                                    if dve_done_at:
                                        wt("s", "ve", dve_done_at[-1])
                                S(lambda e, i=i, pp=pp, BB=BB: e.activation(out=BB[:, i * T:(i + 1) * T], in_=pp[:, :], func=AF.Copy), "ac")
                                if not (bi == 1 and i == G - 1):
                                    wt("t", "ac", cnt["ac"])
                        cap["bcast_pe"] = cnt["pe"]
                        cap["evict_ac"] = cnt["ac"]
                        if nxt is not None:
                            emit_proj1_pe(nxt, ng)
                        for et in range(NET):
                            it = scan_it[0]
                            par = it % 2
                            dts = xno_sb[:, par * T:(par + 1) * T]
                            dxs = xno_sb[:, 2048 + par * T:2048 + (par + 1) * T]
                            dab = dA_[:, par * 4096:(par + 1) * 4096]
                            if it >= 2:
                                wt("y", "ac", act_done_at[it - 2])
                                wt("y", "ve", dve_done_at[it - 2])
                            ld(dts, dt_scr[et * 128:(et + 1) * 128, sl])
                            ld(dxs, dtx_scr[et * 128:(et + 1) * 128, sl])
                            wt("s", "ld")
                            if it >= 2:
                                wt("s", "ve", dve_done_at[it - 2])
                            for i in range(G):
                                col = et * N + ng * G + i
                                S(lambda e, i=i, col=col, dab=dab, dts=dts: e.activation(
                                    out=dab[:, i * T:(i + 1) * T], in_=dts, func=AF.Exp,
                                    scale=A_sb[:, col:col + 1]), "ac" if i == G - 1 else None)
                            act_done_at.append(cnt["ac"])
                            wt("v", "ac", act_done_at[-1])
                            wt("v", "ld")
                            V(lambda e, dxs=dxs: e.tensor_tensor(
                                out=u_[:].rearrange("p (n t) -> p n t", n=G),
                                in0=dxs.unsqueeze(1).broadcast_to([128, G, T]),
                                in1=Bb[:].rearrange("p (n t) -> p n t", n=G), op=ALU.mult))
                            for i in range(G):
                                col = et * N + ng * G + i
                                V(lambda e, i=i, col=col, dab=dab: e.tensor_tensor_scan(
                                    out=h_[:, i * T:(i + 1) * T], data0=dab[:, i * T:(i + 1) * T],
                                    data1=u_[:, i * T:(i + 1) * T], initial=state[:, col:col + 1],
                                    op0=ALU.mult, op1=ALU.add), "ve" if i == G - 1 else None)
                            wt("v", "ve")
                            for i in range(G):
                                col = et * N + ng * G + i
                                V(lambda e, i=i, col=col: e.tensor_copy(state[:, col:col + 1], h_[:, (i + 1) * T - 1:(i + 1) * T]))
                            V(lambda e: e.tensor_mul(u_[:], h_[:], Cb[:]))
                            V(lambda e: e.tensor_add(h_[:, 0:2 * T], u_[:, 0:2 * T], u_[:, 2 * T:4 * T]))
                            ya = yacc[:, et * T:(et + 1) * T]
                            if ng == 0:
                                if et == 0:
                                    wt("v", "pe", cap["wout_pe"])
                                V(lambda e, ya=ya: e.tensor_add(ya, h_[:, 0:T], h_[:, T:2 * T]), "ve")
                            else:
                                V(lambda e: e.tensor_add(h_[:, 0:T], h_[:, 0:T], h_[:, T:2 * T]))
                                V(lambda e, ya=ya: e.tensor_add(ya, ya, h_[:, 0:T]), "ve")
                            dve_done_at.append(cnt["ve"])
                            scan_it[0] += 1
                        if nxt is not None:
                            emit_proj1_dve(nxt, ng)
                            emit_proj1_act(nxt, ng)
                    wt("y", "st", cnt["st"])
                    for et in range(NET):
                        wt("y", "ve", cnt["ve"])
                        ld(yxc[:], xc_scr[et * 128:(et + 1) * 128, sl])
                        ld(yz[:], z_scr[et * 128:(et + 1) * 128, sl])
                        ya = yacc[:, et * T:(et + 1) * T]
                        wt("v", "ld")
                        V(lambda e, et=et, ya=ya: e.scalar_tensor_tensor(
                            out=ya, in0=yxc[:], scalar=Dp_sb[:, et:et + 1], in1=ya, op0=ALU.mult, op1=ALU.add))
                        V(lambda e, ya=ya: e.tensor_mul(ya, ya, yz[:]), "ve")
                    if nxt is not None:
                        emit_loop23(nxt)
                    wcvt_ac = []
                    for dti in range(NDT):
                        stg = wstg if dti % 2 == 0 else bcst
                        wt("y", "ve", cnt["ve"])
                        if dti >= 2:
                            wt("y", "ac", wcvt_ac[dti - 2])
                        else:
                            wt("y", "ac", cap["wstg_ac"])
                        wt("y", "pe", cnt["pe"] if dti == 0 else cnt["pe"])
                        for ke in range(NET):
                            ld(stg[:, ke * 128:(ke + 1) * 128], w["WoutT"][ke * 128:(ke + 1) * 128, dti * 128:(dti + 1) * 128])
                        wt("s", "ld")
                        wt("s", "pe", cnt["pe"])
                        S(lambda e, stg=stg: e.activation(out=wout_sb[:], in_=stg[:], func=AF.Copy), "ac")
                        cap["wstg_ac"] = cnt["ac"]
                        wcvt_ac.append(cnt["ac"])
                        for ts2 in range(2):
                            wt("t", "ve")
                            wt("t", "ac", cnt["ac"])
                            wt("t", "st", cnt["st"])
                            for ke in range(NET):
                                TE(lambda e, ke=ke, ts2=ts2: e.matmul(
                                    pB[:, 0:512], wout_sb[:, ke * 128:(ke + 1) * 128],
                                    yacc[:, ke * T + ts2 * 512:ke * T + ts2 * 512 + 512],
                                    start=(ke == 0), stop=(ke == NET - 1)), "pe" if ke == NET - 1 else None)
                            wt("v", "pe")
                            wt("v", "st", cnt["st"])
                            if out_rev:
                                V(lambda e: e.tensor_copy(wrev[:], pB[:, 0:512][:, ::-1]), "ve")
                                slot = HALF - (s * T + ts2 * 512) - 512
                            else:
                                V(lambda e: e.tensor_copy(wrev[:], pB[:, 0:512]), "ve")
                                slot = s * T + ts2 * 512
                            wt("g", "ve")
                            st((mamba_a if out_rev else mamba_b)[dti * 128:(dti + 1) * 128, slot:slot + 512], wrev[:])
                    cap["wout_ve"] = cnt["ve"]
                    cap["wout_pe"] = cnt["pe"]

            mamba_phase("p1", xn1_fwd, out_rev=True)

            wt("g", "ve", cnt["ve"])
            st(cc_in_t.ap(), state[:])
            wt("g", "st")
            gcc = eng.collective_compute(
                "AllReduce", ALU.add, replica_groups=[[0, 1], [2, 3], [4, 5], [6, 7]],
                ins=[cc_in_t.ap().opt()], outs=[cc_out_t.ap().opt()],
            ) if on("g") else None
            bump("cc", 1, gcc, "g")
            wt("g", "cc")
            i2 = eng.dma_start(out=blob2[:], in_=cc_out_t.ap()) if on("g") else None
            bump("st", 16, i2, "g")
            wt("v", "st")
            V(lambda e: e.tensor_sub(state[:], blob2[:], state[:]), "ve")

            mamba_phase("p2", xn1_rev, out_rev=False)

            sslab = ar3[:]
            def hs_ap(ht):
                return ar1[:, ht * T:(ht + 1) * T] if ht < 16 else ar2[:, (ht - 16) * T:(ht - 15) * T]
            for s in range(NSUP):
                sl = slice(s * T, (s + 1) * T)
                wt("y", "ve", cnt["ve"])
                wt("y", "st", cnt["st"])
                wt("y", "pe", cnt["pe"])
                wt("y", "ac", cnt["ac"])
                for dti in range(NDT):
                    rows = slice(dti * 128, (dti + 1) * 128)
                    ld(xn1_sb[:, dti * T:(dti + 1) * T], xn1_rev[rows, sl])
                    ld(xno_sb[:, dti * T:(dti + 1) * T], xno_rev[rows, sl])
                    ld(cvt[:], mamba_a[rows, sl])
                    ld(upt[:], mamba_b[rows, sl])
                    wt("v", "ld")
                    V(lambda e, dti=dti: e.tensor_add(sslab[:, dti * T:(dti + 1) * T], cvt[:], upt[:]), "ve")
                    wt("y", "ve")
                wt("s", "ve")
                for dti in range(NDT):
                    S(lambda e, i=dti: e.activation(out=ar2[:, 8192 + i * T:8192 + (i + 1) * T],
                                                    in_=sslab[:, i * T:(i + 1) * T], func=AF.Square),
                      "ac" if dti == NDT - 1 else None)
                wt("t", "ac")
                for h in range(2):
                    hs = slice(h * 512, (h + 1) * 512)
                    wt("t", "ve", cnt["ve"])
                    for dti in range(NDT):
                        TE(lambda e, i=dti, hs=hs: e.matmul(pR[:, :], ones_cb[:], sslab[:, i * T:(i + 1) * T][:, hs],
                                                           start=(i == 0), stop=(i == NDT - 1)))
                    for dti in range(NDT):
                        TE(lambda e, i=dti, hs=hs: e.matmul(pR2[:, :], ones_cb[:],
                                                            ar2[:, 8192 + i * T:8192 + (i + 1) * T][:, hs],
                                                            start=(i == 0), stop=(i == NDT - 1)),
                           "pe" if dti == NDT - 1 else None)
                    wt("v", "pe")
                    V(lambda e, hs=hs: e.tensor_scalar(out=row_mu[:, hs], in0=pR[:, :], scalar1=1.0 / D, scalar2=None, op0=ALU.mult))
                    V(lambda e, hs=hs: e.tensor_scalar(out=row_u[:, hs], in0=pR2[:, :], scalar1=1.0 / D, scalar2=None, op0=ALU.mult),
                      "ve")
                wt("s", "ve")
                S(lambda e: e.activation(out=row_v, in_=row_mu, func=AF.Square), "ac")
                wt("v", "ac")
                V(lambda e: e.tensor_sub(row_u, row_u, row_v), "ve")
                wt("s", "ve")
                S(lambda e: e.activation(out=row_v, in_=row_u, func=AF.Ln, bias=epsr[:, 0:1]))
                S(lambda e: e.activation(out=row_v, in_=row_v, func=AF.Exp, scale=-0.5), "ac")
                wt("v", "ac")
                wt("v", "ve")
                V(lambda e: e.tensor_mul(row_u, row_mu, row_v), "ve")
                for h in range(2):
                    hs = slice(h * 512, (h + 1) * 512)
                    bcast_row(row_v, hs, bcst[:, 0:1024])
                    bcast_row(row_u, hs, bcst[:, 1024:2048])
                for dti in range(NDT):
                    ss = sslab[:, dti * T:(dti + 1) * T]
                    V(lambda e, ss=ss: e.tensor_mul(ss, ss, bcst[:, 0:1024]))
                    V(lambda e, ss=ss: e.tensor_sub(ss, ss, bcst[:, 1024:2048]))
                    V(lambda e, ss=ss, dti=dti: e.tensor_scalar(
                        out=ss, in0=ss, scalar1=lnp[:, 2 * NDT + dti:2 * NDT + dti + 1],
                        scalar2=lnp[:, 3 * NDT + dti:3 * NDT + dti + 1], op0=ALU.mult, op1=ALU.add))
                    V(lambda e, ss=ss, dti=dti: e.tensor_add(ss, ss, xn1_sb[:, dti * T:(dti + 1) * T]),
                      "ve" if dti == NDT - 1 else None)
                pe_hist = []
                ac_hist = []
                for ht in range(NHT):
                    stg = wstg if ht % 2 == 0 else bcst
                    wsb = wsb0 if ht % 2 == 0 else wsb1
                    if ht >= 2:
                        wt("y", "ve", ac_hist[ht - 2])
                    else:
                        wt("y", "ve", cnt["ve"])
                    for kd in range(NDT):
                        ld(stg[:, kd * 128:(kd + 1) * 128].bitcast(I32), wgT[kd * 128:(kd + 1) * 128, ht * 128:(ht + 1) * 128])
                    for kd in range(NDT):
                        ld(stg[:, 1024 + kd * 128:1024 + (kd + 1) * 128].bitcast(I32),
                           wuT[kd * 128:(kd + 1) * 128, ht * 128:(ht + 1) * 128])
                    wt("v", "ld")
                    if ht >= 2:
                        wt("v", "pe", pe_hist[ht - 2])
                    V(lambda e, stg=stg, wsb=wsb: e.tensor_scalar(out=wsb[:], in0=stg[:].bitcast(I32), scalar1=QS,
                                                                  scalar2=None, op0=ALU.mult), "ve")
                    ac_hist.append(cnt["ve"])
                    wt("t", "ve", cnt["ve"])
                    if ht >= 1:
                        wt("t", "ve", cnt["ve"])
                    for c2 in range(2):
                        for kd in range(NDT):
                            TE(lambda e, kd=kd, c2=c2, wsb=wsb: e.matmul(
                                pA[:, c2 * 512:(c2 + 1) * 512], wsb[:, kd * 128:(kd + 1) * 128],
                                xno_sb[:, kd * T + c2 * 512:kd * T + (c2 + 1) * 512],
                                start=(kd == 0), stop=(kd == NDT - 1)))
                    for c2 in range(2):
                        for kd in range(NDT):
                            TE(lambda e, kd=kd, c2=c2, wsb=wsb: e.matmul(
                                pB[:, c2 * 512:(c2 + 1) * 512], wsb[:, 1024 + kd * 128:1024 + (kd + 1) * 128],
                                xno_sb[:, kd * T + c2 * 512:kd * T + (c2 + 1) * 512],
                                start=(kd == 0), stop=(kd == NDT - 1)),
                               "pe" if (kd == NDT - 1 and c2 == 1) else None)
                    pe_hist.append(cnt["pe"])
                    wt("s", "pe")
                    S(lambda e: e.activation(out=cvt[:], in_=pA[:, :], func=AF.Silu), "ac")
                    wt("v", "ac")
                    V(lambda e: e.tensor_copy(upt[:], pB[:, :]))
                    V(lambda e, ht=ht: e.tensor_mul(hs_ap(ht), cvt[:], upt[:]), "ve")
                if debug and s == 0:
                    wt("g", "ve", cnt["ve"])
                    for ht in range(NHT):
                        st(h_dbg[ht * 128:(ht + 1) * 128, :], hs_ap(ht))
                for dti in range(NDT):
                    stg = wstg if dti % 2 == 0 else bcst
                    wsb = wsb0 if dti % 2 == 0 else wsb1
                    wt("y", "ve", cnt["ve"])
                    wt("y", "pe", cnt["pe"])
                    for kh in range(12):
                        ld(stg[:, kh * 128:(kh + 1) * 128].bitcast(I32),
                           wdT[kh * 128:(kh + 1) * 128, dti * 128:(dti + 1) * 128])
                    stg2 = bcst if dti % 2 == 0 else wstg
                    for kh in range(12, 24):
                        ld(stg2[:, (kh - 12) * 128:(kh - 11) * 128].bitcast(I32),
                           wdT[kh * 128:(kh + 1) * 128, dti * 128:(dti + 1) * 128])
                    wt("v", "ld")
                    V(lambda e, stg=stg, wsb=wsb: e.tensor_scalar(out=wsb[:, 0:1536], in0=stg[:, 0:1536].bitcast(I32),
                                                                  scalar1=QS, scalar2=None, op0=ALU.mult), "ve")
                    wt("t", "ve", cnt["ve"])
                    for c2 in range(2):
                        for kh in range(12):
                            TE(lambda e, kh=kh, c2=c2, wsb=wsb: e.matmul(
                                pX[:, c2 * 512:(c2 + 1) * 512], wsb[:, kh * 128:(kh + 1) * 128],
                                hs_ap(kh)[:, c2 * 512:(c2 + 1) * 512],
                                start=(kh == 0), stop=False),
                               "pe" if (kh == 11 and c2 == 1) else None)
                    wt("v", "pe")
                    V(lambda e, stg2=stg2, wsb=wsb: e.tensor_scalar(out=wsb[:, 0:1536], in0=stg2[:, 0:1536].bitcast(I32),
                                                                    scalar1=QS, scalar2=None, op0=ALU.mult), "ve")
                    wt("t", "ve", cnt["ve"])
                    for c2 in range(2):
                        for kh in range(12):
                            TE(lambda e, kh=kh, c2=c2, wsb=wsb: e.matmul(
                                pX[:, c2 * 512:(c2 + 1) * 512], wsb[:, kh * 128:(kh + 1) * 128],
                                hs_ap(12 + kh)[:, c2 * 512:(c2 + 1) * 512],
                                start=False, stop=(kh == 11)),
                               "pe" if (kh == 11 and c2 == 1) else None)
                    for ts2 in range(2):
                        tss = slice(ts2 * TS3, (ts2 + 1) * TS3)
                        wt("y", "st", cnt["st"])
                        ld(ot[:], x_loc[dti * 128:(dti + 1) * 128, sl][:, tss])
                        wt("s", "pe")
                        wt("s", "ld")
                        wt("s", "ve", cnt["ve"])
                        S(lambda e, ts2=ts2: e.activation(out=sgt[:], in_=pX[:, ts2 * TS3:(ts2 + 1) * TS3], func=AF.Silu), "ac")
                        wt("v", "ac")
                        wt("v", "st", cnt["st"])
                        if debug and s == 0:
                            wt("g", "ac", cnt["ac"])
                            st(g_dbg[dti * 128:(dti + 1) * 128, tss], sgt[:])
                        V(lambda e, dti=dti, tss=tss: e.tensor_mul(sgt[:], sgt[:], sslab[:, dti * T:(dti + 1) * T][:, tss]))
                        V(lambda e: e.tensor_add(ot[:], ot[:], sgt[:]), "ve")
                        wt("g", "ve")
                        st(out_feat[dti * 128:(dti + 1) * 128, sl][:, tss], ot[:])
                        wt("v", "st")

            wt("g", "st", cnt["st"])

        @block.vector
        def _(e):
            prog("v", e)

        @block.scalar
        def _(e):
            prog("s", e)

        @block.tensor
        def _(e):
            prog("t", e)

        @block.gpsimd
        def _(e):
            prog("g", e)

        @block.sync
        def _(e):
            prog("y", e)

    return nc


def _prep_core_inputs(inputs, b, role):
    x = np.asarray(inputs["L0_attn_x"])
    if role == 0:
        x_loc = np.ascontiguousarray(x[b, :HALF][::-1].T)
    else:
        x_loc = np.ascontiguousarray(x[b, HALF:].T)

    def mamba_set(m):
        pre = f"L0_attn_{m}_"
        return dict(
            WinT=np.ascontiguousarray(np.asarray(inputs[pre + "W_in"]).T),
            WxpT=np.ascontiguousarray(np.asarray(inputs[pre + "W_xp"]).T),
            WdtT=np.ascontiguousarray(np.asarray(inputs[pre + "W_dt"]).T),
            WoutT=np.ascontiguousarray(np.asarray(inputs[pre + "W_out"]).T),
            Alog=np.ascontiguousarray(np.asarray(inputs[pre + "A_log"]).reshape(NET, 128, N).transpose(1, 0, 2).reshape(128, NET * N)),
            convw=np.ascontiguousarray(np.asarray(inputs[pre + "conv_w"]).reshape(NET, 128, 4).transpose(1, 0, 2).reshape(128, NET * 4)),
            convb=np.ascontiguousarray(np.asarray(inputs[pre + "conv_b"]).reshape(NET, 128).T),
            dtb=np.ascontiguousarray(np.asarray(inputs[pre + "dt_b"]).reshape(NET, 128).T),
            Dp=np.ascontiguousarray(np.asarray(inputs[pre + "D"]).reshape(NET, 128).T),
        )

    m1, m2 = mamba_set("m1"), mamba_set("m2")
    p1, p2 = (m1, m2) if role == 0 else (m2, m1)
    im = {"x_loc": x_loc}
    for ph, mm in (("p1", p1), ("p2", p2)):
        for k, v in mm.items():
            im[f"{ph}_{k}"] = v
    for k, src in (("ln1w", "L0_attn_ln1_w"), ("ln1b", "L0_attn_ln1_b"),
                   ("ln2w", "L0_attn_ln2_w"), ("ln2b", "L0_attn_ln2_b")):
        im[k] = np.ascontiguousarray(np.asarray(inputs[src]).reshape(NDT, 128).T)
    im["wgT"] = np.ascontiguousarray(np.asarray(inputs["L0_ffn_w_gate"]).T)
    im["wuT"] = np.ascontiguousarray(np.asarray(inputs["L0_ffn_w_up"]).T)
    im["wdT"] = np.ascontiguousarray(np.asarray(inputs["L0_ffn_w_down"]).T)
    return im


def kernel(**inputs):
    nc = build_program()
    in_maps = [_prep_core_inputs(inputs, r // 2, r % 2) for r in range(8)]
    res = run_bass_kernel_spmd(nc, in_maps, core_ids=list(range(8)))
    out = np.empty((4, 8192, 1024), dtype=np.float32)
    for r in range(8):
        of = np.asarray(res.results[r]["out_feat"])
        b, role = r // 2, r % 2
        if role == 0:
            out[b, :HALF] = of.T[::-1]
        else:
            out[b, HALF:] = of.T
    return out


# version 12
# speedup vs baseline: 1.1819x; 1.1819x over previous
import numpy as np
import concourse.bass as bass
import concourse.mybir as mybir
from concourse.bass_utils import run_bass_kernel_spmd

F32 = mybir.dt.float32
BF16 = mybir.dt.bfloat16
I32 = mybir.dt.int32
AF = mybir.ActivationFunctionType
ALU = mybir.AluOpType

D = 1024
E = 2048
N = 64
R = 64
HALF = 4096
T = 1024
NSUP = HALF // T
G = 4
NG = N // G
NET = E // 128
NDT = D // 128
NHT = 3072 // 128
TS3 = 512
LN_EPS = 1e-5
QS = 1.0 / 127.0


def build_program(debug=False):
    nc = bass.Bass("TRN2", target_bir_lowering=False)
    skind = "ExternalOutput" if debug else "Internal"

    x_loc = nc.dram_tensor("x_loc", [D, HALF], F32, kind="ExternalInput").ap()
    W = {}
    for ph in ("p1", "p2"):
        W[ph] = dict(
            WinT=nc.dram_tensor(f"{ph}_WinT", [D, 2 * E], F32, kind="ExternalInput").ap(),
            WxpT=nc.dram_tensor(f"{ph}_WxpT", [E, 192], F32, kind="ExternalInput").ap(),
            WdtT=nc.dram_tensor(f"{ph}_WdtT", [R, E], F32, kind="ExternalInput").ap(),
            WoutT=nc.dram_tensor(f"{ph}_WoutT", [E, D], F32, kind="ExternalInput").ap(),
            Alog=nc.dram_tensor(f"{ph}_Alog", [128, NET * N], F32, kind="ExternalInput").ap(),
            convw=nc.dram_tensor(f"{ph}_convw", [128, NET * 4], F32, kind="ExternalInput").ap(),
            convb=nc.dram_tensor(f"{ph}_convb", [128, NET], F32, kind="ExternalInput").ap(),
            dtb=nc.dram_tensor(f"{ph}_dtb", [128, NET], F32, kind="ExternalInput").ap(),
            Dp=nc.dram_tensor(f"{ph}_Dp", [128, NET], F32, kind="ExternalInput").ap(),
        )
    ln1w = nc.dram_tensor("ln1w", [128, NDT], F32, kind="ExternalInput").ap()
    ln1b = nc.dram_tensor("ln1b", [128, NDT], F32, kind="ExternalInput").ap()
    ln2w = nc.dram_tensor("ln2w", [128, NDT], F32, kind="ExternalInput").ap()
    ln2b = nc.dram_tensor("ln2b", [128, NDT], F32, kind="ExternalInput").ap()
    wgT = nc.dram_tensor("wgT", [D, 3072], I32, kind="ExternalInput").ap()
    wuT = nc.dram_tensor("wuT", [D, 3072], I32, kind="ExternalInput").ap()
    wdT = nc.dram_tensor("wdT", [3072, D], I32, kind="ExternalInput").ap()
    out_feat = nc.dram_tensor("out_feat", [D, HALF], F32, kind="ExternalOutput").ap()

    xn1_fwd = nc.dram_tensor("xn1_fwd", [D, HALF], BF16, kind=skind).ap()
    xn1_rev = nc.dram_tensor("xn1_rev", [D, HALF], BF16, kind=skind).ap()
    xno_rev = nc.dram_tensor("xno_rev", [D, HALF], BF16, kind=skind).ap()
    mamba_a = nc.dram_tensor("mamba_a", [D, HALF], BF16, kind=skind).ap()
    mamba_b = nc.dram_tensor("mamba_b", [D, HALF], BF16, kind=skind).ap()
    xc_scr = nc.dram_tensor("xc_scr", [E, HALF], BF16, kind=skind).ap()
    z_scr = nc.dram_tensor("z_scr", [E, HALF], BF16, kind="Internal").ap()
    dt_scr = nc.dram_tensor("dt_scr", [E, HALF], BF16, kind=skind).ap()
    dtx_scr = nc.dram_tensor("dtx_scr", [E, HALF], BF16, kind="Internal").ap()
    bc_scr = nc.dram_tensor("bc_scr", [128, HALF], BF16, kind=skind).ap()
    h_dbg = nc.dram_tensor("h_dbg", [3072, T], BF16, kind=skind).ap()
    g_dbg = nc.dram_tensor("g_dbg", [D, T], BF16, kind=skind).ap()
    cc_in_t = nc.dram_tensor("cc_in", [128, 1072], F32)
    cc_out_t = nc.dram_tensor("cc_out", [128, 1072], F32)

    import contextlib
    with contextlib.ExitStack() as _stk:
        xn1_sb = _stk.enter_context(nc.sbuf_tensor("xn1_sb", [128, NDT * T], BF16))
        xno_sb = _stk.enter_context(nc.sbuf_tensor("xno_sb", [128, NDT * T], BF16))
        ar1 = _stk.enter_context(nc.sbuf_tensor("ar1", [128, 16384], BF16))
        ar2 = _stk.enter_context(nc.sbuf_tensor("ar2", [128, 16384], BF16))
        ar3 = _stk.enter_context(nc.sbuf_tensor("ar3", [128, 8192], BF16))
        brow = _stk.enter_context(nc.sbuf_tensor("brow", [1, 4096], BF16))
        bcD = _stk.enter_context(nc.sbuf_tensor("bcD", [128, 4096], BF16))
        xp_ext = _stk.enter_context(nc.sbuf_tensor("xp_ext", [128, T + 4], BF16))
        dtlo_bf = _stk.enter_context(nc.sbuf_tensor("dtlo_bf", [64, T], BF16))
        bstg = _stk.enter_context(nc.sbuf_tensor("bstg", [64, T], BF16))
        bstg2 = _stk.enter_context(nc.sbuf_tensor("bstg2", [64, T], BF16))
        state = _stk.enter_context(nc.sbuf_tensor("state", [128, 1072], F32))
        blob2 = _stk.enter_context(nc.sbuf_tensor("blob2", [128, 1072], F32))
        A_sb = _stk.enter_context(nc.sbuf_tensor("A_sb", [128, NET * N], F32))
        row_mu = brow[:].bitcast(F32)[:, 0:1024]
        row_u = brow[:].bitcast(F32)[:, 1024:2048]
        row_v = bcD[:, 0:2048].bitcast(F32)[0:1, 0:1024]
        row_w = bcD[:, 2048:4096].bitcast(F32)[0:1, 0:1024]
        bcst = _stk.enter_context(nc.sbuf_tensor("bcst", [128, 2048], F32))
        wstg = _stk.enter_context(nc.sbuf_tensor("wstg", [128, 2048], F32))
        wsb0 = _stk.enter_context(nc.sbuf_tensor("wsb0", [128, 2048], BF16))
        wsb1 = _stk.enter_context(nc.sbuf_tensor("wsb1", [128, 2048], BF16))
        wxp_sb = _stk.enter_context(nc.sbuf_tensor("wxp_sb", [128, NET * 192], BF16))
        wdt_sb = _stk.enter_context(nc.sbuf_tensor("wdt_sb", [64, E], BF16))
        cvt = _stk.enter_context(nc.sbuf_tensor("cvt", [128, T], BF16))
        xct = _stk.enter_context(nc.sbuf_tensor("xct", [128, T], BF16))
        zt = _stk.enter_context(nc.sbuf_tensor("zt", [128, T], BF16))
        upt = _stk.enter_context(nc.sbuf_tensor("upt", [128, T], BF16))
        sgt = _stk.enter_context(nc.sbuf_tensor("sgt", [128, TS3], BF16))
        ot = _stk.enter_context(nc.sbuf_tensor("ot", [128, TS3], F32))
        wrev = _stk.enter_context(nc.sbuf_tensor("wrev", [128, 512], BF16))
        yxc = _stk.enter_context(nc.sbuf_tensor("yxc", [128, T], BF16))
        yz = _stk.enter_context(nc.sbuf_tensor("yz", [128, T], BF16))
        wout_sb = _stk.enter_context(nc.sbuf_tensor("wout_sb", [128, 2048], BF16))
        convw_sb = _stk.enter_context(nc.sbuf_tensor("convw_sb", [128, NET * 4], F32))
        convb_sb = _stk.enter_context(nc.sbuf_tensor("convb_sb", [128, NET], F32))
        dtb_sb = _stk.enter_context(nc.sbuf_tensor("dtb_sb", [128, NET], F32))
        Dp_sb = _stk.enter_context(nc.sbuf_tensor("Dp_sb", [128, NET], F32))
        lnp = _stk.enter_context(nc.sbuf_tensor("lnp", [128, 4 * NDT], F32))
        ones_rb = _stk.enter_context(nc.sbuf_tensor("ones_rb", [1, 128], BF16))
        ones_rf = _stk.enter_context(nc.sbuf_tensor("ones_rf", [1, 128], F32))
        ones_cb = _stk.enter_context(nc.sbuf_tensor("ones_cb", [128, 1], BF16))
        ones_cf = _stk.enter_context(nc.sbuf_tensor("ones_cf", [128, 1], F32))
        epsr = _stk.enter_context(nc.sbuf_tensor("epsr", [1, 2], F32))
        pA = _stk.enter_context(nc.psum_tensor("pA", [128, T], F32))
        pB = _stk.enter_context(nc.psum_tensor("pB", [128, T], F32))
        pX = _stk.enter_context(nc.psum_tensor("pX", [128, T], F32))
        pR = _stk.enter_context(nc.psum_tensor("pR", [1, 512], F32))
        pR2 = _stk.enter_context(nc.psum_tensor("pR2", [1, 512], F32))
        s_ld = _stk.enter_context(nc.semaphore("s_ld"))
        s_st = _stk.enter_context(nc.semaphore("s_st"))
        s_pe = _stk.enter_context(nc.semaphore("s_pe"))
        s_ve = _stk.enter_context(nc.semaphore("s_ve"))
        s_ac = _stk.enter_context(nc.semaphore("s_ac"))
        s_cc = _stk.enter_context(nc.semaphore("s_cc"))
        block = _stk.enter_context(nc.Block())
        SEM = dict(ld=s_ld, st=s_st, pe=s_pe, ve=s_ve, ac=s_ac, cc=s_cc)

        def prog(role, eng):
            cnt = {k: 0 for k in SEM}
            on = lambda r: role == r
            act_done_at = []
            dve_done_at = []
            scan_it = [0]

            def bump(sem, n, instr, r):
                cnt[sem] += n
                if instr is not None and role == r:
                    instr.then_inc(SEM[sem], n)

            def wt(r, sem, val=None):
                if role == r:
                    eng.wait_ge(SEM[sem], cnt[sem] if val is None else val)

            def ld(dst, src):
                i = eng.dma_start(out=dst, in_=src) if on("y") else None
                bump("ld", 16, i, "y")

            def st(dst, src):
                i = eng.dma_start(out=dst, in_=src) if on("g") else None
                bump("st", 16, i, "g")

            def V(fn, sem=None):
                i = fn(eng) if on("v") else None
                if sem is not None:
                    bump(sem, 1, i, "v")

            def S(fn, sem=None):
                i = fn(eng) if on("s") else None
                if sem is not None:
                    bump(sem, 1, i, "s")

            def TE(fn, sem=None):
                i = fn(eng) if on("t") else None
                if sem is not None:
                    bump(sem, 1, i, "t")

            def bcast_row(src_row, hs, dst):
                wt("t", "ve", cnt["ve"])
                TE(lambda e: e.matmul(pA[:, 0:512], ones_rf[:], src_row[:, hs], start=True, stop=True), "pe")
                wt("v", "pe")
                V(lambda e: e.tensor_copy(dst[:, hs], pA[:, 0:512]), "ve")

            V(lambda e: e.memset(ones_rb[:], 1.0))
            V(lambda e: e.memset(ones_rf[:], 1.0))
            V(lambda e: e.memset(ones_cb[:], 1.0))
            V(lambda e: e.memset(ones_cf[:], 1.0))
            V(lambda e: e.memset(epsr[:, 0:1], LN_EPS))
            V(lambda e: e.memset(epsr[:, 1:2], LN_EPS * LN_EPS), "ve")
            ld(lnp[:, 0:NDT], ln1w)
            ld(lnp[:, NDT:2 * NDT], ln1b)
            ld(lnp[:, 2 * NDT:3 * NDT], ln2w)
            ld(lnp[:, 3 * NDT:4 * NDT], ln2b)
            wt("t", "ve", cnt["ve"])

            xt = ar1[:].bitcast(F32)
            sqf = ar2[:].bitcast(F32)
            for s in range(NSUP):
                sl = slice(s * T, (s + 1) * T)
                wt("y", "pe", cnt["pe"])
                wt("y", "ve", cnt["ve"])
                for dti in range(NDT):
                    ld(xt[:, dti * T:(dti + 1) * T], x_loc[dti * 128:(dti + 1) * 128, sl])
                wt("s", "ld")
                for dti in range(NDT):
                    S(lambda e, i=dti: e.activation(out=sqf[:, i * T:(i + 1) * T], in_=xt[:, i * T:(i + 1) * T], func=AF.Square),
                      "ac" if dti == NDT - 1 else None)
                wt("t", "ac")
                wt("t", "ld")
                for h in range(2):
                    hs = slice(h * 512, (h + 1) * 512)
                    wt("t", "ve", cnt["ve"])
                    for dti in range(NDT):
                        TE(lambda e, i=dti, hs=hs: e.matmul(pR[:, :], ones_cf[:], xt[:, i * T:(i + 1) * T][:, hs],
                                                           start=(i == 0), stop=(i == NDT - 1)))
                    for dti in range(NDT):
                        TE(lambda e, i=dti, hs=hs: e.matmul(pR2[:, :], ones_cf[:], sqf[:, i * T:(i + 1) * T][:, hs],
                                                            start=(i == 0), stop=(i == NDT - 1)),
                           "pe" if dti == NDT - 1 else None)
                    wt("v", "pe")
                    V(lambda e, hs=hs: e.tensor_scalar(out=row_mu[:, hs], in0=pR[:, :], scalar1=1.0 / D, scalar2=None, op0=ALU.mult))
                    V(lambda e, hs=hs: e.tensor_scalar(out=row_u[:, hs], in0=pR2[:, :], scalar1=1.0 / D, scalar2=None, op0=ALU.mult),
                      "ve")
                wt("s", "ve")
                S(lambda e: e.activation(out=row_v, in_=row_mu, func=AF.Square), "ac")
                wt("v", "ac")
                wt("v", "ve")
                V(lambda e: e.tensor_sub(row_u, row_u, row_v), "ve")
                wt("s", "ve")
                S(lambda e: e.activation(out=row_v, in_=row_u, func=AF.Ln, bias=epsr[:, 0:1]), "ac")
                wt("s", "ac")
                S(lambda e: e.activation(out=row_v, in_=row_v, func=AF.Exp, scale=-0.5), "ac")
                wt("s", "ac")
                S(lambda e: e.activation(out=row_w, in_=row_u, func=AF.Ln, scale=1.0 + LN_EPS, bias=epsr[:, 1:2]), "ac")
                wt("s", "ac")
                S(lambda e: e.activation(out=row_w, in_=row_w, func=AF.Exp, scale=-0.5), "ac")
                wt("v", "ac")
                V(lambda e: e.tensor_mul(row_u, row_mu, row_v), "ve")
                wt("v", "ve")
                V(lambda e: e.tensor_mul(row_mu, row_mu, row_w), "ve")
                for pass_i, (ra, rb_) in enumerate(((row_v, row_u), (row_w, row_mu))):
                    for h in range(2):
                        hs = slice(h * 512, (h + 1) * 512)
                        bcast_row(ra, hs, bcst[:, 0:1024])
                        bcast_row(rb_, hs, bcst[:, 1024:2048])
                    for dti in range(NDT):
                        xs_ = xt[:, dti * T:(dti + 1) * T]
                        tmp = sqf[:, dti * T:(dti + 1) * T]
                        wt("v", "st", cnt["st"])
                        V(lambda e, xs_=xs_, tmp=tmp: e.tensor_mul(tmp, xs_, bcst[:, 0:1024]), "ve")
                        wt("v", "ve")
                        if pass_i == 0:
                            V(lambda e, tmp=tmp: e.tensor_sub(cvt[:], tmp, bcst[:, 1024:2048]), "ve")
                            wt("g", "ve")
                            st(xno_rev[dti * 128:(dti + 1) * 128, sl], cvt[:])
                        else:
                            V(lambda e, tmp=tmp: e.tensor_sub(tmp, tmp, bcst[:, 1024:2048]), "ve")
                            wt("v", "ve")
                            V(lambda e, tmp=tmp, dti=dti: e.tensor_scalar(
                                out=cvt[:], in0=tmp, scalar1=lnp[:, dti:dti + 1], scalar2=lnp[:, NDT + dti:NDT + dti + 1],
                                op0=ALU.mult, op1=ALU.add), "ve")
                            wt("g", "ve")
                            st(xn1_rev[dti * 128:(dti + 1) * 128, sl], cvt[:])
                            wt("v", "ve")
                            V(lambda e: e.tensor_copy(xct[:], cvt[:, ::-1]), "ve")
                            wt("g", "ve")
                            st(xn1_fwd[dti * 128:(dti + 1) * 128, HALF - (s + 1) * T:HALF - s * T], xct[:])

            def mamba_phase(ph, xn_src, out_rev):
                w = W[ph]
                wt("y", "ac", cnt["ac"])
                wt("y", "ve", cnt["ve"])
                wt("y", "pe", cnt["pe"])
                wt("y", "st", cnt["st"])
                ld(A_sb[:], w["Alog"])
                ld(convw_sb[:], w["convw"])
                ld(convb_sb[:], w["convb"])
                ld(dtb_sb[:], w["dtb"])
                ld(Dp_sb[:], w["Dp"])
                for ke in range(NET):
                    wt("y", "ac", cnt["ac"])
                    ld(wstg[:, 0:192], w["WxpT"][ke * 128:(ke + 1) * 128, :])
                    wt("s", "ld")
                    S(lambda e, ke=ke: e.activation(out=wxp_sb[:, ke * 192:(ke + 1) * 192], in_=wstg[:, 0:192], func=AF.Copy), "ac")
                for hh in range(2):
                    wt("y", "ac", cnt["ac"])
                    ld(wstg[0:64, 0:1024], w["WdtT"][:, hh * 1024:(hh + 1) * 1024])
                    wt("s", "ld")
                    S(lambda e, hh=hh: e.activation(out=wdt_sb[:, hh * 1024:(hh + 1) * 1024], in_=wstg[0:64, 0:1024], func=AF.Copy), "ac")
                wt("s", "ld")
                S(lambda e: e.activation(out=A_sb[:], in_=A_sb[:], func=AF.Exp), "ac")
                wt("v", "ac")
                V(lambda e: e.tensor_scalar(out=A_sb[:], in0=A_sb[:], scalar1=-1.0, scalar2=None, op0=ALU.mult), "ve")
                if ph == "p1":
                    V(lambda e: e.memset(state[:], 0.0), "ve")
                wt("s", "ve", cnt["ve"])

                cap = dict(wstg_ac=cnt["ac"], wsb_pe=[], xp_evict_ve=[], zsilu_ac=[],
                           xn1_pe=cnt["pe"], bcast_pe=cnt["pe"], evict_ac=cnt["ac"],
                           projcvt_ac=[], projmm_pe=[], conv_ve=[], xcz_st=[], last_win_mm=cnt["pe"],
                           wout_ve=cnt["ve"], wout_pe=cnt["pe"], l23_pe=cnt["pe"], l23_ve=cnt["ve"])
                proj_idx = [0]

                def emit_xn1_load(s):
                    wt("y", "pe", cap["last_win_mm"])
                    for dti in range(NDT):
                        ld(xn1_sb[:, dti * T:(dti + 1) * T], xn_src[dti * 128:(dti + 1) * 128, slice(s * T, (s + 1) * T)])
                    cap["xn1_ld"] = cnt["ld"]

                def emit_proj1_pe(s, et):
                    j = proj_idx[0]
                    par = j % 2
                    wsb = wsb0 if par == 0 else wsb1
                    wt("y", "ac", cap["wstg_ac"])
                    for kd in range(NDT):
                        ld(wstg[:, kd * 128:(kd + 1) * 128], w["WinT"][kd * 128:(kd + 1) * 128, et * 128:et * 128 + 128])
                    for kd in range(NDT):
                        ld(wstg[:, 1024 + kd * 128:1024 + (kd + 1) * 128],
                           w["WinT"][kd * 128:(kd + 1) * 128, E + et * 128:E + et * 128 + 128])
                    wt("s", "ld")
                    wt("s", "pe", cap["l23_pe"])
                    wt("s", "ve", cap["l23_ve"])
                    if j >= 2:
                        wt("s", "pe", cap["wsb_pe"][j - 2])
                    S(lambda e, wsb=wsb: e.activation(out=wsb[:], in_=wstg[:], func=AF.Copy), "ac")
                    cap["wstg_ac"] = cnt["ac"]
                    cap["projcvt_ac"].append(cnt["ac"])
                    wt("t", "ac", cnt["ac"])
                    wt("t", "ld", cap["xn1_ld"])
                    wt("t", "ve", cap["wout_ve"])
                    if j >= 1:
                        wt("t", "ve", cap["xp_evict_ve"][j - 1])
                        wt("t", "ac", cap["zsilu_ac"][j - 1]) if len(cap["zsilu_ac"]) >= j else None
                    for kd in range(NDT):
                        for c2 in range(2):
                            TE(lambda e, kd=kd, c2=c2, wsb=wsb: e.matmul(
                                pB[:, c2 * 512:(c2 + 1) * 512], wsb[:, kd * 128:(kd + 1) * 128],
                                xn1_sb[:, kd * T + c2 * 512:kd * T + (c2 + 1) * 512],
                                start=(kd == 0), stop=(kd == NDT - 1)),
                               "pe" if (kd == NDT - 1 and c2 == 1) else None)
                    for kd in range(NDT):
                        for c2 in range(2):
                            TE(lambda e, kd=kd, c2=c2, wsb=wsb: e.matmul(
                                pX[:, c2 * 512:(c2 + 1) * 512], wsb[:, 1024 + kd * 128:1024 + (kd + 1) * 128],
                                xn1_sb[:, kd * T + c2 * 512:kd * T + (c2 + 1) * 512],
                                start=(kd == 0), stop=(kd == NDT - 1)),
                               "pe" if (kd == NDT - 1 and c2 == 1) else None)
                    cap["wsb_pe"].append(cnt["pe"])
                    cap["projmm_pe"].append(cnt["pe"])
                    cap["last_win_mm"] = cnt["pe"]
                    proj_idx[0] += 1

                def emit_proj1_dve(s, et):
                    j = len(cap["conv_ve"])
                    wt("v", "pe", cap["projmm_pe"][j])
                    if j >= 1:
                        wt("v", "st", cap["xcz_st"][j - 1])
                    V(lambda e, et=et: e.tensor_copy(xp_ext[:, 1:4], state[:, 1024 + et * 3:1024 + et * 3 + 3]))
                    V(lambda e: e.tensor_copy(xp_ext[:, 4:4 + T], pB[:, :]), "ve")
                    cap["xp_evict_ve"].append(cnt["ve"])
                    V(lambda e, et=et: e.tensor_scalar(out=cvt[:], in0=xp_ext[:, 1:1 + T],
                                                       scalar1=convw_sb[:, et * 4:et * 4 + 1], scalar2=None, op0=ALU.mult))
                    for k in (1, 2, 3):
                        V(lambda e, k=k, et=et: e.scalar_tensor_tensor(
                            out=cvt[:], in0=xp_ext[:, 1 + k:1 + k + T], scalar=convw_sb[:, et * 4 + k:et * 4 + k + 1],
                            in1=cvt[:], op0=ALU.mult, op1=ALU.add), "ve" if k == 3 else None)
                    wt("v", "ve")
                    V(lambda e, et=et: e.tensor_copy(state[:, 1024 + et * 3:1024 + et * 3 + 3], xp_ext[:, 1 + T:4 + T]), "ve")
                    cap["conv_ve"].append(cnt["ve"])

                def emit_proj1_act(s, et):
                    j = len(cap["zsilu_ac"])
                    sl = slice(s * T, (s + 1) * T)
                    wt("s", "ve", cap["conv_ve"][j])
                    if j >= 1:
                        wt("s", "st", cap["xcz_st"][j - 1])
                    S(lambda e, et=et: e.activation(out=xct[:], in_=cvt[:], func=AF.Silu, bias=convb_sb[:, et:et + 1]))
                    S(lambda e: e.activation(out=zt[:], in_=pX[:, :], func=AF.Silu), "ac")
                    cap["zsilu_ac"].append(cnt["ac"])
                    wt("g", "ac", cnt["ac"])
                    st(xc_scr[et * 128:(et + 1) * 128, sl], xct[:])
                    st(z_scr[et * 128:(et + 1) * 128, sl], zt[:])
                    cap["xcz_st"].append(cnt["st"])

                def emit_loop23(s):
                    sl = slice(s * T, (s + 1) * T)
                    pe_at2 = []
                    for ke in range(NET):
                        par = ke % 2
                        dst = (wsb0 if par == 0 else wsb1)[:, 0:T]
                        wt("y", "st", cap["xcz_st"][-1] if cap["xcz_st"] else cnt["st"])
                        if ke >= 2:
                            wt("y", "pe", pe_at2[ke - 2])
                        else:
                            wt("y", "pe", cap["wsb_pe"][-1] if cap["wsb_pe"] else cnt["pe"])
                        ld(dst, xc_scr[ke * 128:(ke + 1) * 128, sl])
                        wt("t", "ld")
                        for c2 in range(2):
                            TE(lambda e, ke=ke, c2=c2, dst=dst: e.matmul(
                                pX[:, c2 * 512:(c2 + 1) * 512], wxp_sb[:, ke * 192:ke * 192 + 128],
                                dst[:, c2 * 512:(c2 + 1) * 512], start=(ke == 0), stop=(ke == NET - 1)))
                        for c2 in range(2):
                            TE(lambda e, ke=ke, c2=c2, dst=dst: e.matmul(
                                pB[0:64, c2 * 512:(c2 + 1) * 512], wxp_sb[:, ke * 192 + 128:(ke + 1) * 192],
                                dst[:, c2 * 512:(c2 + 1) * 512], start=(ke == 0), stop=(ke == NET - 1)),
                               "pe" if c2 == 1 else None)
                        pe_at2.append(cnt["pe"])
                    wt("v", "pe")
                    wt("v", "st", cnt["st"])
                    V(lambda e: e.tensor_copy(dtlo_bf[:], pX[0:64, :]))
                    V(lambda e: e.tensor_copy(bstg[:], pX[64:128, :]), "ve")
                    wt("g", "ve")
                    st(bc_scr[0:64, sl], bstg[:])
                    V(lambda e: e.tensor_copy(bstg2[:], pB[0:64, :]), "ve")
                    wt("g", "ve")
                    st(bc_scr[64:128, sl], bstg2[:])
                    for et in range(NET):
                        pp = pA if et % 2 == 0 else pX
                        wt("t", "ve", cnt["ve"])
                        wt("t", "ac", cnt["ac"] - (4 if et >= 1 else 0))
                        for c2 in range(2):
                            TE(lambda e, et=et, c2=c2, pp=pp: e.matmul(
                                pp[:, c2 * 512:(c2 + 1) * 512], wdt_sb[:, et * 128:(et + 1) * 128],
                                dtlo_bf[:, c2 * 512:(c2 + 1) * 512], start=True, stop=True),
                               "pe" if c2 == 1 else None)
                        wt("s", "pe")
                        wt("s", "ve", cnt["ve"])
                        S(lambda e, et=et, pp=pp: e.activation(out=cvt[:], in_=pp[:, :], func=AF.Abs, bias=dtb_sb[:, et:et + 1]), "ac")
                        wt("s", "ac")
                        S(lambda e: e.activation(out=cvt[:], in_=cvt[:], func=AF.Exp, scale=-1.0), "ac")
                        wt("s", "ac")
                        S(lambda e: e.activation(out=cvt[:], in_=cvt[:], func=AF.Ln, bias=ones_cf[:, 0:1]), "ac")
                        wt("s", "ac")
                        S(lambda e, et=et, pp=pp: e.activation(out=zt[:], in_=pp[:, :], func=AF.Relu, bias=dtb_sb[:, et:et + 1]), "ac")
                        par = et % 2
                        dst = (wsb0 if par == 0 else wsb1)[:, 0:T]
                        wt("y", "ve", cnt["ve"])
                        ld(dst, xc_scr[et * 128:(et + 1) * 128, sl])
                        wt("v", "ac")
                        wt("v", "ld")
                        wt("v", "st", cnt["st"])
                        V(lambda e: e.tensor_add(xct[:], cvt[:], zt[:]))
                        V(lambda e, dst=dst: e.tensor_mul(upt[:], xct[:], dst), "ve")
                        wt("g", "ve")
                        st(dt_scr[et * 128:(et + 1) * 128, sl], xct[:])
                        st(dtx_scr[et * 128:(et + 1) * 128, sl], upt[:])
                    cap["l23_pe"] = cnt["pe"]
                    cap["l23_ve"] = cnt["ve"]

                emit_xn1_load(0)
                for et in range(NET):
                    emit_proj1_pe(0, et)
                    emit_proj1_dve(0, et)
                    emit_proj1_act(0, et)
                emit_loop23(0)

                for s in range(NSUP):
                    sl = slice(s * T, (s + 1) * T)
                    nxt = s + 1 if s + 1 < NSUP else None
                    dA_ = ar1[:, 0:8192]
                    u_ = ar1[:, 8192:12288]
                    h_ = ar1[:, 12288:16384]
                    BbA = ar3[:, 0:4096]
                    CbA = ar3[:, 4096:8192]
                    BbB = bcst[:].bitcast(BF16)
                    CbB = bcD[:]
                    yacc = ar2[:]
                    if nxt is not None:
                        emit_xn1_load(nxt)

                    def emit_bcast(ng):
                        BB2 = (BbA, CbA) if ng % 2 == 0 else (BbB, CbB)
                        if ng == 0:
                            wt("y", "st", cnt["st"])
                            wt("y", "ve", cnt["ve"])
                            wt("y", "ac", cnt["ac"])
                        for bi, BB in enumerate(BB2):
                            wt("y", "pe", cap["bcast_pe"])
                            ld(brow[:].rearrange("o (p t) -> o p t", p=G),
                               bc_scr[bi * 64 + ng * G:bi * 64 + (ng + 1) * G, sl].unsqueeze(0))
                            wt("t", "ld")
                            if dve_done_at:
                                wt("t", "ve", dve_done_at[-1])
                            wt("t", "ac", cap["evict_ac"])
                            for i in range(G):
                                for c2 in range(2):
                                    TE(lambda e, i=i, c2=c2: e.matmul(
                                        pA[:, c2 * 512:(c2 + 1) * 512], ones_rb[:],
                                        brow[:, i * T + c2 * 512:i * T + (c2 + 1) * 512], start=True, stop=True),
                                       "pe" if c2 == 1 else None)
                                wt("s", "pe")
                                if bi == 0 and i == 0 and dve_done_at:
                                    wt("s", "ve", dve_done_at[-1])
                                S(lambda e, i=i, BB=BB: e.activation(out=BB[:, i * T:(i + 1) * T], in_=pA[:, :], func=AF.Copy), "ac")
                                if not (bi == 1 and i == G - 1):
                                    wt("t", "ac", cnt["ac"])
                            cap["bcast_pe"] = cnt["pe"]
                        cap["evict_ac"] = cnt["ac"]

                    emit_bcast(0)
                    for ng in range(NG):
                        Bb = BbA if ng % 2 == 0 else BbB
                        Cb = CbA if ng % 2 == 0 else CbB
                        if nxt is not None:
                            emit_proj1_pe(nxt, ng)
                        for et in range(NET):
                            it = scan_it[0]
                            par = it % 2
                            dts = xno_sb[:, par * T:(par + 1) * T]
                            dxs = xno_sb[:, 2048 + par * T:2048 + (par + 1) * T]
                            dab = dA_[:, par * 4096:(par + 1) * 4096]
                            if it >= 2:
                                wt("y", "ac", act_done_at[it - 2])
                                wt("y", "ve", dve_done_at[it - 2])
                            ld(dts, dt_scr[et * 128:(et + 1) * 128, sl])
                            ld(dxs, dtx_scr[et * 128:(et + 1) * 128, sl])
                            wt("s", "ld")
                            if it >= 2:
                                wt("s", "ve", dve_done_at[it - 2])
                            for i in range(G):
                                col = et * N + ng * G + i
                                S(lambda e, i=i, col=col, dab=dab, dts=dts: e.activation(
                                    out=dab[:, i * T:(i + 1) * T], in_=dts, func=AF.Exp,
                                    scale=A_sb[:, col:col + 1]), "ac" if i == G - 1 else None)
                            act_done_at.append(cnt["ac"])
                            wt("v", "ac", act_done_at[-1])
                            wt("v", "ld")
                            V(lambda e, dxs=dxs: e.tensor_tensor(
                                out=u_[:].rearrange("p (n t) -> p n t", n=G),
                                in0=dxs.unsqueeze(1).broadcast_to([128, G, T]),
                                in1=Bb[:].rearrange("p (n t) -> p n t", n=G), op=ALU.mult))
                            for i in range(G):
                                col = et * N + ng * G + i
                                V(lambda e, i=i, col=col, dab=dab: e.tensor_tensor_scan(
                                    out=h_[:, i * T:(i + 1) * T], data0=dab[:, i * T:(i + 1) * T],
                                    data1=u_[:, i * T:(i + 1) * T], initial=state[:, col:col + 1],
                                    op0=ALU.mult, op1=ALU.add))
                            V(lambda e, Cb=Cb: e.tensor_mul(u_[:], h_[:], Cb[:]))
                            V(lambda e: e.tensor_add(u_[:, 0:2 * T], u_[:, 0:2 * T], u_[:, 2 * T:4 * T]))
                            ya = yacc[:, et * T:(et + 1) * T]
                            if ng == 0:
                                if et == 0:
                                    wt("v", "pe", cap["wout_pe"])
                                V(lambda e, ya=ya: e.tensor_add(ya, u_[:, 0:T], u_[:, T:2 * T]))
                            else:
                                V(lambda e: e.tensor_add(u_[:, 0:T], u_[:, 0:T], u_[:, T:2 * T]))
                                V(lambda e, ya=ya: e.tensor_add(ya, ya, u_[:, 0:T]))
                            for i in range(G):
                                col = et * N + ng * G + i
                                V(lambda e, i=i, col=col: e.tensor_copy(state[:, col:col + 1], h_[:, (i + 1) * T - 1:(i + 1) * T]),
                                  "ve" if i == G - 1 else None)
                            dve_done_at.append(cnt["ve"])
                            scan_it[0] += 1
                            if et == 0 and ng + 1 < NG:
                                emit_bcast(ng + 1)
                        if nxt is not None:
                            emit_proj1_dve(nxt, ng)
                            emit_proj1_act(nxt, ng)
                    wt("y", "st", cnt["st"])
                    for et in range(NET):
                        wt("y", "ve", cnt["ve"])
                        ld(yxc[:], xc_scr[et * 128:(et + 1) * 128, sl])
                        ld(yz[:], z_scr[et * 128:(et + 1) * 128, sl])
                        ya = yacc[:, et * T:(et + 1) * T]
                        wt("v", "ld")
                        V(lambda e, et=et, ya=ya: e.scalar_tensor_tensor(
                            out=ya, in0=yxc[:], scalar=Dp_sb[:, et:et + 1], in1=ya, op0=ALU.mult, op1=ALU.add))
                        V(lambda e, ya=ya: e.tensor_mul(ya, ya, yz[:]), "ve")
                    if nxt is not None:
                        emit_loop23(nxt)
                    wcvt_ac = []
                    for dti in range(NDT):
                        stg = wstg if dti % 2 == 0 else bcst
                        wt("y", "ve", cnt["ve"])
                        if dti >= 2:
                            wt("y", "ac", wcvt_ac[dti - 2])
                        else:
                            wt("y", "ac", cap["wstg_ac"])
                        wt("y", "pe", cnt["pe"] if dti == 0 else cnt["pe"])
                        for ke in range(NET):
                            ld(stg[:, ke * 128:(ke + 1) * 128], w["WoutT"][ke * 128:(ke + 1) * 128, dti * 128:(dti + 1) * 128])
                        wt("s", "ld")
                        wt("s", "pe", cnt["pe"])
                        S(lambda e, stg=stg: e.activation(out=wout_sb[:], in_=stg[:], func=AF.Copy), "ac")
                        cap["wstg_ac"] = cnt["ac"]
                        wcvt_ac.append(cnt["ac"])
                        for ts2 in range(2):
                            wt("t", "ve")
                            wt("t", "ac", cnt["ac"])
                            wt("t", "st", cnt["st"])
                            for ke in range(NET):
                                TE(lambda e, ke=ke, ts2=ts2: e.matmul(
                                    pB[:, 0:512], wout_sb[:, ke * 128:(ke + 1) * 128],
                                    yacc[:, ke * T + ts2 * 512:ke * T + ts2 * 512 + 512],
                                    start=(ke == 0), stop=(ke == NET - 1)), "pe" if ke == NET - 1 else None)
                            wt("v", "pe")
                            wt("v", "st", cnt["st"])
                            if out_rev:
                                V(lambda e: e.tensor_copy(wrev[:], pB[:, 0:512][:, ::-1]), "ve")
                                slot = HALF - (s * T + ts2 * 512) - 512
                            else:
                                V(lambda e: e.tensor_copy(wrev[:], pB[:, 0:512]), "ve")
                                slot = s * T + ts2 * 512
                            wt("g", "ve")
                            st((mamba_a if out_rev else mamba_b)[dti * 128:(dti + 1) * 128, slot:slot + 512], wrev[:])
                    cap["wout_ve"] = cnt["ve"]
                    cap["wout_pe"] = cnt["pe"]

            mamba_phase("p1", xn1_fwd, out_rev=True)

            wt("g", "ve", cnt["ve"])
            st(cc_in_t.ap(), state[:])
            wt("g", "st")
            gcc = eng.collective_compute(
                "AllReduce", ALU.add, replica_groups=[[0, 1], [2, 3], [4, 5], [6, 7]],
                ins=[cc_in_t.ap().opt()], outs=[cc_out_t.ap().opt()],
            ) if on("g") else None
            bump("cc", 1, gcc, "g")
            wt("g", "cc")
            i2 = eng.dma_start(out=blob2[:], in_=cc_out_t.ap()) if on("g") else None
            bump("st", 16, i2, "g")
            wt("v", "st")
            V(lambda e: e.tensor_sub(state[:], blob2[:], state[:]), "ve")

            mamba_phase("p2", xn1_rev, out_rev=False)

            sslab = ar3[:]
            def hs_ap(ht):
                return ar1[:, ht * T:(ht + 1) * T] if ht < 16 else ar2[:, (ht - 16) * T:(ht - 15) * T]
            for s in range(NSUP):
                sl = slice(s * T, (s + 1) * T)
                wt("y", "ve", cnt["ve"])
                wt("y", "st", cnt["st"])
                wt("y", "pe", cnt["pe"])
                wt("y", "ac", cnt["ac"])
                for dti in range(NDT):
                    rows = slice(dti * 128, (dti + 1) * 128)
                    ld(xn1_sb[:, dti * T:(dti + 1) * T], xn1_rev[rows, sl])
                    ld(xno_sb[:, dti * T:(dti + 1) * T], xno_rev[rows, sl])
                    ld(cvt[:], mamba_a[rows, sl])
                    ld(upt[:], mamba_b[rows, sl])
                    wt("v", "ld")
                    V(lambda e, dti=dti: e.tensor_add(sslab[:, dti * T:(dti + 1) * T], cvt[:], upt[:]), "ve")
                    wt("y", "ve")
                wt("s", "ve")
                for dti in range(NDT):
                    S(lambda e, i=dti: e.activation(out=ar2[:, 8192 + i * T:8192 + (i + 1) * T],
                                                    in_=sslab[:, i * T:(i + 1) * T], func=AF.Square),
                      "ac" if dti == NDT - 1 else None)
                wt("t", "ac")
                for h in range(2):
                    hs = slice(h * 512, (h + 1) * 512)
                    wt("t", "ve", cnt["ve"])
                    for dti in range(NDT):
                        TE(lambda e, i=dti, hs=hs: e.matmul(pR[:, :], ones_cb[:], sslab[:, i * T:(i + 1) * T][:, hs],
                                                           start=(i == 0), stop=(i == NDT - 1)))
                    for dti in range(NDT):
                        TE(lambda e, i=dti, hs=hs: e.matmul(pR2[:, :], ones_cb[:],
                                                            ar2[:, 8192 + i * T:8192 + (i + 1) * T][:, hs],
                                                            start=(i == 0), stop=(i == NDT - 1)),
                           "pe" if dti == NDT - 1 else None)
                    wt("v", "pe")
                    V(lambda e, hs=hs: e.tensor_scalar(out=row_mu[:, hs], in0=pR[:, :], scalar1=1.0 / D, scalar2=None, op0=ALU.mult))
                    V(lambda e, hs=hs: e.tensor_scalar(out=row_u[:, hs], in0=pR2[:, :], scalar1=1.0 / D, scalar2=None, op0=ALU.mult),
                      "ve")
                wt("s", "ve")
                S(lambda e: e.activation(out=row_v, in_=row_mu, func=AF.Square), "ac")
                wt("v", "ac")
                V(lambda e: e.tensor_sub(row_u, row_u, row_v), "ve")
                wt("s", "ve")
                S(lambda e: e.activation(out=row_v, in_=row_u, func=AF.Ln, bias=epsr[:, 0:1]))
                S(lambda e: e.activation(out=row_v, in_=row_v, func=AF.Exp, scale=-0.5), "ac")
                wt("v", "ac")
                wt("v", "ve")
                V(lambda e: e.tensor_mul(row_u, row_mu, row_v), "ve")
                for h in range(2):
                    hs = slice(h * 512, (h + 1) * 512)
                    bcast_row(row_v, hs, bcst[:, 0:1024])
                    bcast_row(row_u, hs, bcst[:, 1024:2048])
                for dti in range(NDT):
                    ss = sslab[:, dti * T:(dti + 1) * T]
                    V(lambda e, ss=ss: e.tensor_mul(ss, ss, bcst[:, 0:1024]))
                    V(lambda e, ss=ss: e.tensor_sub(ss, ss, bcst[:, 1024:2048]))
                    V(lambda e, ss=ss, dti=dti: e.tensor_scalar(
                        out=ss, in0=ss, scalar1=lnp[:, 2 * NDT + dti:2 * NDT + dti + 1],
                        scalar2=lnp[:, 3 * NDT + dti:3 * NDT + dti + 1], op0=ALU.mult, op1=ALU.add))
                    V(lambda e, ss=ss, dti=dti: e.tensor_add(ss, ss, xn1_sb[:, dti * T:(dti + 1) * T]),
                      "ve" if dti == NDT - 1 else None)
                pe_hist = []
                ac_hist = []
                for ht in range(NHT):
                    stg = wstg if ht % 2 == 0 else bcst
                    wsb = wsb0 if ht % 2 == 0 else wsb1
                    if ht >= 2:
                        wt("y", "ve", ac_hist[ht - 2])
                    else:
                        wt("y", "ve", cnt["ve"])
                    for kd in range(NDT):
                        ld(stg[:, kd * 128:(kd + 1) * 128].bitcast(I32), wgT[kd * 128:(kd + 1) * 128, ht * 128:(ht + 1) * 128])
                    for kd in range(NDT):
                        ld(stg[:, 1024 + kd * 128:1024 + (kd + 1) * 128].bitcast(I32),
                           wuT[kd * 128:(kd + 1) * 128, ht * 128:(ht + 1) * 128])
                    wt("v", "ld")
                    if ht >= 2:
                        wt("v", "pe", pe_hist[ht - 2])
                    V(lambda e, stg=stg, wsb=wsb: e.tensor_scalar(out=wsb[:], in0=stg[:].bitcast(I32), scalar1=QS,
                                                                  scalar2=None, op0=ALU.mult), "ve")
                    ac_hist.append(cnt["ve"])
                    wt("t", "ve", cnt["ve"])
                    if ht >= 1:
                        wt("t", "ve", cnt["ve"])
                    for c2 in range(2):
                        for kd in range(NDT):
                            TE(lambda e, kd=kd, c2=c2, wsb=wsb: e.matmul(
                                pA[:, c2 * 512:(c2 + 1) * 512], wsb[:, kd * 128:(kd + 1) * 128],
                                xno_sb[:, kd * T + c2 * 512:kd * T + (c2 + 1) * 512],
                                start=(kd == 0), stop=(kd == NDT - 1)))
                    for c2 in range(2):
                        for kd in range(NDT):
                            TE(lambda e, kd=kd, c2=c2, wsb=wsb: e.matmul(
                                pB[:, c2 * 512:(c2 + 1) * 512], wsb[:, 1024 + kd * 128:1024 + (kd + 1) * 128],
                                xno_sb[:, kd * T + c2 * 512:kd * T + (c2 + 1) * 512],
                                start=(kd == 0), stop=(kd == NDT - 1)),
                               "pe" if (kd == NDT - 1 and c2 == 1) else None)
                    pe_hist.append(cnt["pe"])
                    wt("s", "pe")
                    S(lambda e: e.activation(out=cvt[:], in_=pA[:, :], func=AF.Silu), "ac")
                    wt("v", "ac")
                    V(lambda e: e.tensor_copy(upt[:], pB[:, :]))
                    V(lambda e, ht=ht: e.tensor_mul(hs_ap(ht), cvt[:], upt[:]), "ve")
                if debug and s == 0:
                    wt("g", "ve", cnt["ve"])
                    for ht in range(NHT):
                        st(h_dbg[ht * 128:(ht + 1) * 128, :], hs_ap(ht))
                for dti in range(NDT):
                    stg = wstg if dti % 2 == 0 else bcst
                    wsb = wsb0 if dti % 2 == 0 else wsb1
                    wt("y", "ve", cnt["ve"])
                    wt("y", "pe", cnt["pe"])
                    for kh in range(12):
                        ld(stg[:, kh * 128:(kh + 1) * 128].bitcast(I32),
                           wdT[kh * 128:(kh + 1) * 128, dti * 128:(dti + 1) * 128])
                    stg2 = bcst if dti % 2 == 0 else wstg
                    for kh in range(12, 24):
                        ld(stg2[:, (kh - 12) * 128:(kh - 11) * 128].bitcast(I32),
                           wdT[kh * 128:(kh + 1) * 128, dti * 128:(dti + 1) * 128])
                    wt("v", "ld")
                    V(lambda e, stg=stg, wsb=wsb: e.tensor_scalar(out=wsb[:, 0:1536], in0=stg[:, 0:1536].bitcast(I32),
                                                                  scalar1=QS, scalar2=None, op0=ALU.mult), "ve")
                    wt("t", "ve", cnt["ve"])
                    for c2 in range(2):
                        for kh in range(12):
                            TE(lambda e, kh=kh, c2=c2, wsb=wsb: e.matmul(
                                pX[:, c2 * 512:(c2 + 1) * 512], wsb[:, kh * 128:(kh + 1) * 128],
                                hs_ap(kh)[:, c2 * 512:(c2 + 1) * 512],
                                start=(kh == 0), stop=False),
                               "pe" if (kh == 11 and c2 == 1) else None)
                    wt("v", "pe")
                    V(lambda e, stg2=stg2, wsb=wsb: e.tensor_scalar(out=wsb[:, 0:1536], in0=stg2[:, 0:1536].bitcast(I32),
                                                                    scalar1=QS, scalar2=None, op0=ALU.mult), "ve")
                    wt("t", "ve", cnt["ve"])
                    for c2 in range(2):
                        for kh in range(12):
                            TE(lambda e, kh=kh, c2=c2, wsb=wsb: e.matmul(
                                pX[:, c2 * 512:(c2 + 1) * 512], wsb[:, kh * 128:(kh + 1) * 128],
                                hs_ap(12 + kh)[:, c2 * 512:(c2 + 1) * 512],
                                start=False, stop=(kh == 11)),
                               "pe" if (kh == 11 and c2 == 1) else None)
                    for ts2 in range(2):
                        tss = slice(ts2 * TS3, (ts2 + 1) * TS3)
                        wt("y", "st", cnt["st"])
                        ld(ot[:], x_loc[dti * 128:(dti + 1) * 128, sl][:, tss])
                        wt("s", "pe")
                        wt("s", "ld")
                        wt("s", "ve", cnt["ve"])
                        S(lambda e, ts2=ts2: e.activation(out=sgt[:], in_=pX[:, ts2 * TS3:(ts2 + 1) * TS3], func=AF.Silu), "ac")
                        wt("v", "ac")
                        wt("v", "st", cnt["st"])
                        if debug and s == 0:
                            wt("g", "ac", cnt["ac"])
                            st(g_dbg[dti * 128:(dti + 1) * 128, tss], sgt[:])
                        V(lambda e, dti=dti, tss=tss: e.tensor_mul(sgt[:], sgt[:], sslab[:, dti * T:(dti + 1) * T][:, tss]))
                        V(lambda e: e.tensor_add(ot[:], ot[:], sgt[:]), "ve")
                        wt("g", "ve")
                        st(out_feat[dti * 128:(dti + 1) * 128, sl][:, tss], ot[:])
                        wt("v", "st")

            wt("g", "st", cnt["st"])

        @block.vector
        def _(e):
            prog("v", e)

        @block.scalar
        def _(e):
            prog("s", e)

        @block.tensor
        def _(e):
            prog("t", e)

        @block.gpsimd
        def _(e):
            prog("g", e)

        @block.sync
        def _(e):
            prog("y", e)

    return nc


def _prep_core_inputs(inputs, b, role):
    x = np.asarray(inputs["L0_attn_x"])
    if role == 0:
        x_loc = np.ascontiguousarray(x[b, :HALF][::-1].T)
    else:
        x_loc = np.ascontiguousarray(x[b, HALF:].T)

    def mamba_set(m):
        pre = f"L0_attn_{m}_"
        return dict(
            WinT=np.ascontiguousarray(np.asarray(inputs[pre + "W_in"]).T),
            WxpT=np.ascontiguousarray(np.asarray(inputs[pre + "W_xp"]).T),
            WdtT=np.ascontiguousarray(np.asarray(inputs[pre + "W_dt"]).T),
            WoutT=np.ascontiguousarray(np.asarray(inputs[pre + "W_out"]).T),
            Alog=np.ascontiguousarray(np.asarray(inputs[pre + "A_log"]).reshape(NET, 128, N).transpose(1, 0, 2).reshape(128, NET * N)),
            convw=np.ascontiguousarray(np.asarray(inputs[pre + "conv_w"]).reshape(NET, 128, 4).transpose(1, 0, 2).reshape(128, NET * 4)),
            convb=np.ascontiguousarray(np.asarray(inputs[pre + "conv_b"]).reshape(NET, 128).T),
            dtb=np.ascontiguousarray(np.asarray(inputs[pre + "dt_b"]).reshape(NET, 128).T),
            Dp=np.ascontiguousarray(np.asarray(inputs[pre + "D"]).reshape(NET, 128).T),
        )

    m1, m2 = mamba_set("m1"), mamba_set("m2")
    p1, p2 = (m1, m2) if role == 0 else (m2, m1)
    im = {"x_loc": x_loc}
    for ph, mm in (("p1", p1), ("p2", p2)):
        for k, v in mm.items():
            im[f"{ph}_{k}"] = v
    for k, src in (("ln1w", "L0_attn_ln1_w"), ("ln1b", "L0_attn_ln1_b"),
                   ("ln2w", "L0_attn_ln2_w"), ("ln2b", "L0_attn_ln2_b")):
        im[k] = np.ascontiguousarray(np.asarray(inputs[src]).reshape(NDT, 128).T)
    im["wgT"] = np.ascontiguousarray(np.asarray(inputs["L0_ffn_w_gate"]).T)
    im["wuT"] = np.ascontiguousarray(np.asarray(inputs["L0_ffn_w_up"]).T)
    im["wdT"] = np.ascontiguousarray(np.asarray(inputs["L0_ffn_w_down"]).T)
    return im


def kernel(**inputs):
    nc = build_program()
    in_maps = [_prep_core_inputs(inputs, r // 2, r % 2) for r in range(8)]
    res = run_bass_kernel_spmd(nc, in_maps, core_ids=list(range(8)))
    out = np.empty((4, 8192, 1024), dtype=np.float32)
    for r in range(8):
        of = np.asarray(res.results[r]["out_feat"])
        b, role = r // 2, r % 2
        if role == 0:
            out[b, :HALF] = of.T[::-1]
        else:
            out[b, HALF:] = of.T
    return out


# version 13
# speedup vs baseline: 1.2267x; 1.0379x over previous
import numpy as np
import concourse.bass as bass
import concourse.mybir as mybir
from concourse.bass_utils import run_bass_kernel_spmd

F32 = mybir.dt.float32
BF16 = mybir.dt.bfloat16
I32 = mybir.dt.int32
AF = mybir.ActivationFunctionType
ALU = mybir.AluOpType

D = 1024
E = 2048
N = 64
R = 64
HALF = 4096
T = 1024
NSUP = HALF // T
G = 4
NG = N // G
NET = E // 128
NDT = D // 128
NHT = 3072 // 128
TS3 = 512
LN_EPS = 1e-5
QS = 1.0 / 127.0


def build_program(debug=False):
    nc = bass.Bass("TRN2", target_bir_lowering=False)
    skind = "ExternalOutput" if debug else "Internal"

    x_loc = nc.dram_tensor("x_loc", [D, HALF], F32, kind="ExternalInput").ap()
    W = {}
    for ph in ("p1", "p2"):
        W[ph] = dict(
            WinT=nc.dram_tensor(f"{ph}_WinT", [D, 2 * E], F32, kind="ExternalInput").ap(),
            WxpT=nc.dram_tensor(f"{ph}_WxpT", [E, 192], F32, kind="ExternalInput").ap(),
            WdtT=nc.dram_tensor(f"{ph}_WdtT", [R, E], F32, kind="ExternalInput").ap(),
            WoutT=nc.dram_tensor(f"{ph}_WoutT", [E, D], F32, kind="ExternalInput").ap(),
            Alog=nc.dram_tensor(f"{ph}_Alog", [128, NET * N], F32, kind="ExternalInput").ap(),
            convw=nc.dram_tensor(f"{ph}_convw", [128, NET * 4], F32, kind="ExternalInput").ap(),
            convb=nc.dram_tensor(f"{ph}_convb", [128, NET], F32, kind="ExternalInput").ap(),
            dtb=nc.dram_tensor(f"{ph}_dtb", [128, NET], F32, kind="ExternalInput").ap(),
            Dp=nc.dram_tensor(f"{ph}_Dp", [128, NET], F32, kind="ExternalInput").ap(),
        )
    ln1w = nc.dram_tensor("ln1w", [128, NDT], F32, kind="ExternalInput").ap()
    ln1b = nc.dram_tensor("ln1b", [128, NDT], F32, kind="ExternalInput").ap()
    ln2w = nc.dram_tensor("ln2w", [128, NDT], F32, kind="ExternalInput").ap()
    ln2b = nc.dram_tensor("ln2b", [128, NDT], F32, kind="ExternalInput").ap()
    wgT = nc.dram_tensor("wgT", [D, 3072], I32, kind="ExternalInput").ap()
    wuT = nc.dram_tensor("wuT", [D, 3072], I32, kind="ExternalInput").ap()
    wdT = nc.dram_tensor("wdT", [3072, D], I32, kind="ExternalInput").ap()
    out_feat = nc.dram_tensor("out_feat", [D, HALF], F32, kind="ExternalOutput").ap()

    xn1_fwd = nc.dram_tensor("xn1_fwd", [D, HALF], BF16, kind=skind).ap()
    xn1_rev = nc.dram_tensor("xn1_rev", [D, HALF], BF16, kind=skind).ap()
    xno_rev = nc.dram_tensor("xno_rev", [D, HALF], BF16, kind=skind).ap()
    mamba_a = nc.dram_tensor("mamba_a", [D, HALF], BF16, kind=skind).ap()
    mamba_b = nc.dram_tensor("mamba_b", [D, HALF], BF16, kind=skind).ap()
    xc_scr = nc.dram_tensor("xc_scr", [E, HALF], BF16, kind=skind).ap()
    z_scr = nc.dram_tensor("z_scr", [E, HALF], BF16, kind="Internal").ap()
    dt_scr = nc.dram_tensor("dt_scr", [E, HALF], BF16, kind=skind).ap()
    dtx_scr = nc.dram_tensor("dtx_scr", [E, HALF], BF16, kind="Internal").ap()
    bc_scr = nc.dram_tensor("bc_scr", [128, HALF], BF16, kind=skind).ap()
    h_dbg = nc.dram_tensor("h_dbg", [3072, T], BF16, kind=skind).ap()
    g_dbg = nc.dram_tensor("g_dbg", [D, T], BF16, kind=skind).ap()
    cc_in_t = nc.dram_tensor("cc_in", [128, 1072], F32)
    cc_out_t = nc.dram_tensor("cc_out", [128, 1072], F32)

    import contextlib
    with contextlib.ExitStack() as _stk:
        xn1_sb = _stk.enter_context(nc.sbuf_tensor("xn1_sb", [128, NDT * T], BF16))
        xno_sb = _stk.enter_context(nc.sbuf_tensor("xno_sb", [128, NDT * T], BF16))
        ar1 = _stk.enter_context(nc.sbuf_tensor("ar1", [128, 16384], BF16))
        ar2 = _stk.enter_context(nc.sbuf_tensor("ar2", [128, 16384], BF16))
        ar3 = _stk.enter_context(nc.sbuf_tensor("ar3", [128, 8192], BF16))
        brow = _stk.enter_context(nc.sbuf_tensor("brow", [1, 4096], BF16))
        bcD = _stk.enter_context(nc.sbuf_tensor("bcD", [128, 4096], BF16))
        xp_ext = _stk.enter_context(nc.sbuf_tensor("xp_ext", [128, T + 4], BF16))
        dtlo_bf = _stk.enter_context(nc.sbuf_tensor("dtlo_bf", [64, T], BF16))
        bstg = _stk.enter_context(nc.sbuf_tensor("bstg", [64, T], BF16))
        bstg2 = _stk.enter_context(nc.sbuf_tensor("bstg2", [64, T], BF16))
        state = _stk.enter_context(nc.sbuf_tensor("state", [128, 1072], F32))
        blob2 = _stk.enter_context(nc.sbuf_tensor("blob2", [128, 1072], F32))
        A_sb = _stk.enter_context(nc.sbuf_tensor("A_sb", [128, NET * N], F32))
        row_mu = brow[:].bitcast(F32)[:, 0:1024]
        row_u = brow[:].bitcast(F32)[:, 1024:2048]
        row_v = bcD[:, 0:2048].bitcast(F32)[0:1, 0:1024]
        row_w = bcD[:, 2048:4096].bitcast(F32)[0:1, 0:1024]
        bcst = _stk.enter_context(nc.sbuf_tensor("bcst", [128, 2048], F32))
        wstg = _stk.enter_context(nc.sbuf_tensor("wstg", [128, 2048], F32))
        wsb0 = _stk.enter_context(nc.sbuf_tensor("wsb0", [128, 2048], BF16))
        wsb1 = _stk.enter_context(nc.sbuf_tensor("wsb1", [128, 2048], BF16))
        wxp_sb = _stk.enter_context(nc.sbuf_tensor("wxp_sb", [128, NET * 192], BF16))
        wdt_sb = _stk.enter_context(nc.sbuf_tensor("wdt_sb", [64, E], BF16))
        cvt = _stk.enter_context(nc.sbuf_tensor("cvt", [128, T], BF16))
        xct = _stk.enter_context(nc.sbuf_tensor("xct", [128, T], BF16))
        zt = _stk.enter_context(nc.sbuf_tensor("zt", [128, T], BF16))
        upt = _stk.enter_context(nc.sbuf_tensor("upt", [128, T], BF16))
        sgt = _stk.enter_context(nc.sbuf_tensor("sgt", [128, TS3], BF16))
        ot = _stk.enter_context(nc.sbuf_tensor("ot", [128, TS3], F32))
        wrev = _stk.enter_context(nc.sbuf_tensor("wrev", [128, 512], BF16))
        yxc = _stk.enter_context(nc.sbuf_tensor("yxc", [128, T], BF16))
        yz = _stk.enter_context(nc.sbuf_tensor("yz", [128, T], BF16))
        wout_sb = _stk.enter_context(nc.sbuf_tensor("wout_sb", [128, 2048], BF16))
        convw_sb = _stk.enter_context(nc.sbuf_tensor("convw_sb", [128, NET * 4], F32))
        convb_sb = _stk.enter_context(nc.sbuf_tensor("convb_sb", [128, NET], F32))
        dtb_sb = _stk.enter_context(nc.sbuf_tensor("dtb_sb", [128, NET], F32))
        Dp_sb = _stk.enter_context(nc.sbuf_tensor("Dp_sb", [128, NET], F32))
        lnp = _stk.enter_context(nc.sbuf_tensor("lnp", [128, 4 * NDT], F32))
        ones_rb = _stk.enter_context(nc.sbuf_tensor("ones_rb", [1, 128], BF16))
        ones_rf = _stk.enter_context(nc.sbuf_tensor("ones_rf", [1, 128], F32))
        ones_cb = _stk.enter_context(nc.sbuf_tensor("ones_cb", [128, 1], BF16))
        ones_cf = _stk.enter_context(nc.sbuf_tensor("ones_cf", [128, 1], F32))
        epsr = _stk.enter_context(nc.sbuf_tensor("epsr", [1, 2], F32))
        pA = _stk.enter_context(nc.psum_tensor("pA", [128, T], F32))
        pB = _stk.enter_context(nc.psum_tensor("pB", [128, T], F32))
        pX = _stk.enter_context(nc.psum_tensor("pX", [128, T], F32))
        pR = _stk.enter_context(nc.psum_tensor("pR", [1, 512], F32))
        pR2 = _stk.enter_context(nc.psum_tensor("pR2", [1, 512], F32))
        s_ld = _stk.enter_context(nc.semaphore("s_ld"))
        s_st = _stk.enter_context(nc.semaphore("s_st"))
        s_pe = _stk.enter_context(nc.semaphore("s_pe"))
        s_ve = _stk.enter_context(nc.semaphore("s_ve"))
        s_ac = _stk.enter_context(nc.semaphore("s_ac"))
        s_cc = _stk.enter_context(nc.semaphore("s_cc"))
        block = _stk.enter_context(nc.Block())
        SEM = dict(ld=s_ld, st=s_st, pe=s_pe, ve=s_ve, ac=s_ac, cc=s_cc)

        def prog(role, eng):
            cnt = {k: 0 for k in SEM}
            on = lambda r: role == r
            act_done_at = []
            dve_done_at = []
            scan_it = [0]

            def bump(sem, n, instr, r):
                cnt[sem] += n
                if instr is not None and role == r:
                    instr.then_inc(SEM[sem], n)

            def wt(r, sem, val=None):
                if role == r:
                    eng.wait_ge(SEM[sem], cnt[sem] if val is None else val)

            def ld(dst, src):
                i = eng.dma_start(out=dst, in_=src) if on("y") else None
                bump("ld", 16, i, "y")

            def st(dst, src):
                i = eng.dma_start(out=dst, in_=src) if on("g") else None
                bump("st", 16, i, "g")

            def V(fn, sem=None):
                i = fn(eng) if on("v") else None
                if sem is not None:
                    bump(sem, 1, i, "v")

            def S(fn, sem=None):
                i = fn(eng) if on("s") else None
                if sem is not None:
                    bump(sem, 1, i, "s")

            def TE(fn, sem=None):
                i = fn(eng) if on("t") else None
                if sem is not None:
                    bump(sem, 1, i, "t")

            def bcast_row(src_row, hs, dst):
                wt("t", "ve", cnt["ve"])
                TE(lambda e: e.matmul(pA[:, 0:512], ones_rf[:], src_row[:, hs], start=True, stop=True), "pe")
                wt("v", "pe")
                V(lambda e: e.tensor_copy(dst[:, hs], pA[:, 0:512]), "ve")

            V(lambda e: e.memset(ones_rb[:], 1.0))
            V(lambda e: e.memset(ones_rf[:], 1.0))
            V(lambda e: e.memset(ones_cb[:], 1.0))
            V(lambda e: e.memset(ones_cf[:], 1.0))
            V(lambda e: e.memset(epsr[:, 0:1], LN_EPS))
            V(lambda e: e.memset(epsr[:, 1:2], LN_EPS * LN_EPS), "ve")
            ld(lnp[:, 0:NDT], ln1w)
            ld(lnp[:, NDT:2 * NDT], ln1b)
            ld(lnp[:, 2 * NDT:3 * NDT], ln2w)
            ld(lnp[:, 3 * NDT:4 * NDT], ln2b)
            wt("t", "ve", cnt["ve"])

            xt = ar1[:].bitcast(F32)
            sqf = ar2[:].bitcast(F32)
            for s in range(NSUP):
                sl = slice(s * T, (s + 1) * T)
                wt("y", "pe", cnt["pe"])
                wt("y", "ve", cnt["ve"])
                for dti in range(NDT):
                    ld(xt[:, dti * T:(dti + 1) * T], x_loc[dti * 128:(dti + 1) * 128, sl])
                wt("s", "ld")
                for dti in range(NDT):
                    S(lambda e, i=dti: e.activation(out=sqf[:, i * T:(i + 1) * T], in_=xt[:, i * T:(i + 1) * T], func=AF.Square),
                      "ac" if dti == NDT - 1 else None)
                wt("t", "ac")
                wt("t", "ld")
                for h in range(2):
                    hs = slice(h * 512, (h + 1) * 512)
                    wt("t", "ve", cnt["ve"])
                    for dti in range(NDT):
                        TE(lambda e, i=dti, hs=hs: e.matmul(pR[:, :], ones_cf[:], xt[:, i * T:(i + 1) * T][:, hs],
                                                           start=(i == 0), stop=(i == NDT - 1)))
                    for dti in range(NDT):
                        TE(lambda e, i=dti, hs=hs: e.matmul(pR2[:, :], ones_cf[:], sqf[:, i * T:(i + 1) * T][:, hs],
                                                            start=(i == 0), stop=(i == NDT - 1)),
                           "pe" if dti == NDT - 1 else None)
                    wt("v", "pe")
                    V(lambda e, hs=hs: e.tensor_scalar(out=row_mu[:, hs], in0=pR[:, :], scalar1=1.0 / D, scalar2=None, op0=ALU.mult))
                    V(lambda e, hs=hs: e.tensor_scalar(out=row_u[:, hs], in0=pR2[:, :], scalar1=1.0 / D, scalar2=None, op0=ALU.mult),
                      "ve")
                wt("s", "ve")
                S(lambda e: e.activation(out=row_v, in_=row_mu, func=AF.Square), "ac")
                wt("v", "ac")
                wt("v", "ve")
                V(lambda e: e.tensor_sub(row_u, row_u, row_v), "ve")
                wt("s", "ve")
                S(lambda e: e.activation(out=row_v, in_=row_u, func=AF.Ln, bias=epsr[:, 0:1]), "ac")
                wt("s", "ac")
                S(lambda e: e.activation(out=row_v, in_=row_v, func=AF.Exp, scale=-0.5), "ac")
                wt("s", "ac")
                S(lambda e: e.activation(out=row_w, in_=row_u, func=AF.Ln, scale=1.0 + LN_EPS, bias=epsr[:, 1:2]), "ac")
                wt("s", "ac")
                S(lambda e: e.activation(out=row_w, in_=row_w, func=AF.Exp, scale=-0.5), "ac")
                wt("v", "ac")
                V(lambda e: e.tensor_mul(row_u, row_mu, row_v), "ve")
                wt("v", "ve")
                V(lambda e: e.tensor_mul(row_mu, row_mu, row_w), "ve")
                for pass_i, (ra, rb_) in enumerate(((row_v, row_u), (row_w, row_mu))):
                    for h in range(2):
                        hs = slice(h * 512, (h + 1) * 512)
                        bcast_row(ra, hs, bcst[:, 0:1024])
                        bcast_row(rb_, hs, bcst[:, 1024:2048])
                    for dti in range(NDT):
                        xs_ = xt[:, dti * T:(dti + 1) * T]
                        tmp = sqf[:, dti * T:(dti + 1) * T]
                        wt("v", "st", cnt["st"])
                        V(lambda e, xs_=xs_, tmp=tmp: e.tensor_mul(tmp, xs_, bcst[:, 0:1024]), "ve")
                        wt("v", "ve")
                        if pass_i == 0:
                            V(lambda e, tmp=tmp: e.tensor_sub(cvt[:], tmp, bcst[:, 1024:2048]), "ve")
                            wt("g", "ve")
                            st(xno_rev[dti * 128:(dti + 1) * 128, sl], cvt[:])
                        else:
                            V(lambda e, tmp=tmp: e.tensor_sub(tmp, tmp, bcst[:, 1024:2048]), "ve")
                            wt("v", "ve")
                            V(lambda e, tmp=tmp, dti=dti: e.tensor_scalar(
                                out=cvt[:], in0=tmp, scalar1=lnp[:, dti:dti + 1], scalar2=lnp[:, NDT + dti:NDT + dti + 1],
                                op0=ALU.mult, op1=ALU.add), "ve")
                            wt("g", "ve")
                            st(xn1_rev[dti * 128:(dti + 1) * 128, sl], cvt[:])
                            wt("v", "ve")
                            V(lambda e: e.tensor_copy(xct[:], cvt[:, ::-1]), "ve")
                            wt("g", "ve")
                            st(xn1_fwd[dti * 128:(dti + 1) * 128, HALF - (s + 1) * T:HALF - s * T], xct[:])

            def mamba_phase(ph, xn_src, out_rev):
                w = W[ph]
                wt("y", "ac", cnt["ac"])
                wt("y", "ve", cnt["ve"])
                wt("y", "pe", cnt["pe"])
                wt("y", "st", cnt["st"])
                ld(A_sb[:], w["Alog"])
                ld(convw_sb[:], w["convw"])
                ld(convb_sb[:], w["convb"])
                ld(dtb_sb[:], w["dtb"])
                ld(Dp_sb[:], w["Dp"])
                for ke in range(NET):
                    wt("y", "ac", cnt["ac"])
                    ld(wstg[:, 0:192], w["WxpT"][ke * 128:(ke + 1) * 128, :])
                    wt("s", "ld")
                    S(lambda e, ke=ke: e.activation(out=wxp_sb[:, ke * 192:(ke + 1) * 192], in_=wstg[:, 0:192], func=AF.Copy), "ac")
                for hh in range(2):
                    wt("y", "ac", cnt["ac"])
                    ld(wstg[0:64, 0:1024], w["WdtT"][:, hh * 1024:(hh + 1) * 1024])
                    wt("s", "ld")
                    S(lambda e, hh=hh: e.activation(out=wdt_sb[:, hh * 1024:(hh + 1) * 1024], in_=wstg[0:64, 0:1024], func=AF.Copy), "ac")
                wt("s", "ld")
                S(lambda e: e.activation(out=A_sb[:], in_=A_sb[:], func=AF.Exp), "ac")
                wt("v", "ac")
                V(lambda e: e.tensor_scalar(out=A_sb[:], in0=A_sb[:], scalar1=-1.0, scalar2=None, op0=ALU.mult), "ve")
                if ph == "p1":
                    V(lambda e: e.memset(state[:], 0.0), "ve")
                wt("s", "ve", cnt["ve"])

                cap = dict(wstg_ac=cnt["ac"], wsb_pe=[], xp_evict_ve=[], zsilu_ac=[],
                           xn1_pe=cnt["pe"], bcast_pe=cnt["pe"], evict_ac=cnt["ac"],
                           projcvt_ac=[], projmm_pe=[], conv_ve=[], xcz_st=[], last_win_mm=cnt["pe"],
                           wout_ve=cnt["ve"], wout_pe=cnt["pe"], l23_pe=cnt["pe"], l23_ve=cnt["ve"])
                proj_idx = [0]

                def emit_xn1_load(s):
                    wt("y", "pe", cap["last_win_mm"])
                    for dti in range(NDT):
                        ld(xn1_sb[:, dti * T:(dti + 1) * T], xn_src[dti * 128:(dti + 1) * 128, slice(s * T, (s + 1) * T)])
                    cap["xn1_ld"] = cnt["ld"]

                def emit_proj1_pe(s, et):
                    j = proj_idx[0]
                    par = j % 2
                    wsb = wsb0 if par == 0 else wsb1
                    wt("y", "ac", cap["wstg_ac"])
                    for kd in range(NDT):
                        ld(wstg[:, kd * 128:(kd + 1) * 128], w["WinT"][kd * 128:(kd + 1) * 128, et * 128:et * 128 + 128])
                    for kd in range(NDT):
                        ld(wstg[:, 1024 + kd * 128:1024 + (kd + 1) * 128],
                           w["WinT"][kd * 128:(kd + 1) * 128, E + et * 128:E + et * 128 + 128])
                    wt("s", "ld")
                    wt("s", "pe", cap["l23_pe"])
                    wt("s", "ve", cap["l23_ve"])
                    if j >= 2:
                        wt("s", "pe", cap["wsb_pe"][j - 2])
                    S(lambda e, wsb=wsb: e.activation(out=wsb[:], in_=wstg[:], func=AF.Copy), "ac")
                    cap["wstg_ac"] = cnt["ac"]
                    cap["projcvt_ac"].append(cnt["ac"])
                    wt("t", "ac", cnt["ac"])
                    wt("t", "ld", cap["xn1_ld"])
                    wt("t", "ve", cap["wout_ve"])
                    if j >= 1:
                        wt("t", "ve", cap["xp_evict_ve"][j - 1])
                        wt("t", "ac", cap["zsilu_ac"][j - 1]) if len(cap["zsilu_ac"]) >= j else None
                    for kd in range(NDT):
                        for c2 in range(2):
                            TE(lambda e, kd=kd, c2=c2, wsb=wsb: e.matmul(
                                pB[:, c2 * 512:(c2 + 1) * 512], wsb[:, kd * 128:(kd + 1) * 128],
                                xn1_sb[:, kd * T + c2 * 512:kd * T + (c2 + 1) * 512],
                                start=(kd == 0), stop=(kd == NDT - 1)),
                               "pe" if (kd == NDT - 1 and c2 == 1) else None)
                    for kd in range(NDT):
                        for c2 in range(2):
                            TE(lambda e, kd=kd, c2=c2, wsb=wsb: e.matmul(
                                pX[:, c2 * 512:(c2 + 1) * 512], wsb[:, 1024 + kd * 128:1024 + (kd + 1) * 128],
                                xn1_sb[:, kd * T + c2 * 512:kd * T + (c2 + 1) * 512],
                                start=(kd == 0), stop=(kd == NDT - 1)),
                               "pe" if (kd == NDT - 1 and c2 == 1) else None)
                    cap["wsb_pe"].append(cnt["pe"])
                    cap["projmm_pe"].append(cnt["pe"])
                    cap["last_win_mm"] = cnt["pe"]
                    proj_idx[0] += 1

                def emit_proj1_dve(s, et):
                    j = len(cap["conv_ve"])
                    wt("v", "pe", cap["projmm_pe"][j])
                    if j >= 1:
                        wt("v", "st", cap["xcz_st"][j - 1])
                    V(lambda e, et=et: e.tensor_copy(xp_ext[:, 1:4], state[:, 1024 + et * 3:1024 + et * 3 + 3]))
                    V(lambda e: e.tensor_copy(xp_ext[:, 4:4 + T], pB[:, :]), "ve")
                    cap["xp_evict_ve"].append(cnt["ve"])
                    V(lambda e, et=et: e.tensor_scalar(out=cvt[:], in0=xp_ext[:, 1:1 + T],
                                                       scalar1=convw_sb[:, et * 4:et * 4 + 1], scalar2=None, op0=ALU.mult))
                    for k in (1, 2, 3):
                        V(lambda e, k=k, et=et: e.scalar_tensor_tensor(
                            out=cvt[:], in0=xp_ext[:, 1 + k:1 + k + T], scalar=convw_sb[:, et * 4 + k:et * 4 + k + 1],
                            in1=cvt[:], op0=ALU.mult, op1=ALU.add), "ve" if k == 3 else None)
                    wt("v", "ve")
                    V(lambda e, et=et: e.tensor_copy(state[:, 1024 + et * 3:1024 + et * 3 + 3], xp_ext[:, 1 + T:4 + T]), "ve")
                    cap["conv_ve"].append(cnt["ve"])

                def emit_proj1_act(s, et):
                    j = len(cap["zsilu_ac"])
                    sl = slice(s * T, (s + 1) * T)
                    wt("s", "ve", cap["conv_ve"][j])
                    if j >= 1:
                        wt("s", "st", cap["xcz_st"][j - 1])
                    S(lambda e, et=et: e.activation(out=xct[:], in_=cvt[:], func=AF.Silu, bias=convb_sb[:, et:et + 1]))
                    S(lambda e: e.activation(out=zt[:], in_=pX[:, :], func=AF.Silu), "ac")
                    cap["zsilu_ac"].append(cnt["ac"])
                    wt("g", "ac", cnt["ac"])
                    st(xc_scr[et * 128:(et + 1) * 128, sl], xct[:])
                    st(z_scr[et * 128:(et + 1) * 128, sl], zt[:])
                    cap["xcz_st"].append(cnt["st"])

                def emit_loop23(s):
                    sl = slice(s * T, (s + 1) * T)
                    pe_at2 = []
                    for ke in range(NET):
                        par = ke % 2
                        dst = (wsb0 if par == 0 else wsb1)[:, 0:T]
                        wt("y", "st", cap["xcz_st"][-1] if cap["xcz_st"] else cnt["st"])
                        if ke >= 2:
                            wt("y", "pe", pe_at2[ke - 2])
                        else:
                            wt("y", "pe", cap["wsb_pe"][-1] if cap["wsb_pe"] else cnt["pe"])
                        ld(dst, xc_scr[ke * 128:(ke + 1) * 128, sl])
                        wt("t", "ld")
                        for c2 in range(2):
                            TE(lambda e, ke=ke, c2=c2, dst=dst: e.matmul(
                                pX[:, c2 * 512:(c2 + 1) * 512], wxp_sb[:, ke * 192:ke * 192 + 128],
                                dst[:, c2 * 512:(c2 + 1) * 512], start=(ke == 0), stop=(ke == NET - 1)))
                        for c2 in range(2):
                            TE(lambda e, ke=ke, c2=c2, dst=dst: e.matmul(
                                pB[0:64, c2 * 512:(c2 + 1) * 512], wxp_sb[:, ke * 192 + 128:(ke + 1) * 192],
                                dst[:, c2 * 512:(c2 + 1) * 512], start=(ke == 0), stop=(ke == NET - 1)),
                               "pe" if c2 == 1 else None)
                        pe_at2.append(cnt["pe"])
                    wt("v", "pe")
                    wt("v", "st", cnt["st"])
                    V(lambda e: e.tensor_copy(dtlo_bf[:], pX[0:64, :]))
                    V(lambda e: e.tensor_copy(bstg[:], pX[64:128, :]), "ve")
                    wt("g", "ve")
                    st(bc_scr[0:64, sl], bstg[:])
                    V(lambda e: e.tensor_copy(bstg2[:], pB[0:64, :]), "ve")
                    wt("g", "ve")
                    st(bc_scr[64:128, sl], bstg2[:])
                    for et in range(NET):
                        pp = pA if et % 2 == 0 else pX
                        wt("t", "ve", cnt["ve"])
                        wt("t", "ac", cnt["ac"] - (4 if et >= 1 else 0))
                        for c2 in range(2):
                            TE(lambda e, et=et, c2=c2, pp=pp: e.matmul(
                                pp[:, c2 * 512:(c2 + 1) * 512], wdt_sb[:, et * 128:(et + 1) * 128],
                                dtlo_bf[:, c2 * 512:(c2 + 1) * 512], start=True, stop=True),
                               "pe" if c2 == 1 else None)
                        wt("s", "pe")
                        wt("s", "ve", cnt["ve"])
                        S(lambda e, et=et, pp=pp: e.activation(out=cvt[:], in_=pp[:, :], func=AF.Abs, bias=dtb_sb[:, et:et + 1]), "ac")
                        wt("s", "ac")
                        S(lambda e: e.activation(out=cvt[:], in_=cvt[:], func=AF.Exp, scale=-1.0), "ac")
                        wt("s", "ac")
                        S(lambda e: e.activation(out=cvt[:], in_=cvt[:], func=AF.Ln, bias=ones_cf[:, 0:1]), "ac")
                        wt("s", "ac")
                        S(lambda e, et=et, pp=pp: e.activation(out=zt[:], in_=pp[:, :], func=AF.Relu, bias=dtb_sb[:, et:et + 1]), "ac")
                        par = et % 2
                        dst = (wsb0 if par == 0 else wsb1)[:, 0:T]
                        wt("y", "ve", cnt["ve"])
                        ld(dst, xc_scr[et * 128:(et + 1) * 128, sl])
                        wt("v", "ac")
                        wt("v", "ld")
                        wt("v", "st", cnt["st"])
                        V(lambda e: e.tensor_add(xct[:], cvt[:], zt[:]))
                        V(lambda e, dst=dst: e.tensor_mul(upt[:], xct[:], dst), "ve")
                        wt("g", "ve")
                        st(dt_scr[et * 128:(et + 1) * 128, sl], xct[:])
                        st(dtx_scr[et * 128:(et + 1) * 128, sl], upt[:])
                    cap["l23_pe"] = cnt["pe"]
                    cap["l23_ve"] = cnt["ve"]

                emit_xn1_load(0)
                for et in range(NET):
                    emit_proj1_pe(0, et)
                    emit_proj1_dve(0, et)
                    emit_proj1_act(0, et)
                emit_loop23(0)

                for s in range(NSUP):
                    sl = slice(s * T, (s + 1) * T)
                    nxt = s + 1 if s + 1 < NSUP else None
                    dA_ = ar1[:, 0:8192]
                    u_ = ar1[:, 8192:12288]
                    h_ = ar1[:, 12288:16384]
                    BbA = ar3[:, 0:4096]
                    CbA = ar3[:, 4096:8192]
                    BbB = bcst[:].bitcast(BF16)
                    CbB = bcD[:]
                    yacc = ar2[:]
                    if nxt is not None:
                        emit_xn1_load(nxt)

                    def emit_bcast(ng, prev_parity_ve):
                        BB2 = (BbA, CbA) if ng % 2 == 0 else (BbB, CbB)
                        if ng == 0:
                            wt("y", "st", cnt["st"])
                            wt("y", "ve", cnt["ve"])
                            wt("y", "ac", cnt["ac"])
                        for bi, BB in enumerate(BB2):
                            wt("y", "pe", cap["bcast_pe"])
                            ld(brow[:].rearrange("o (p t) -> o p t", p=G),
                               bc_scr[bi * 64 + ng * G:bi * 64 + (ng + 1) * G, sl].unsqueeze(0))
                            wt("t", "ld")
                            if prev_parity_ve is not None:
                                wt("t", "ve", prev_parity_ve)
                            wt("t", "ac", cap["evict_ac"])
                            for i in range(G):
                                for c2 in range(2):
                                    TE(lambda e, i=i, c2=c2: e.matmul(
                                        pA[:, c2 * 512:(c2 + 1) * 512], ones_rb[:],
                                        brow[:, i * T + c2 * 512:i * T + (c2 + 1) * 512], start=True, stop=True),
                                       "pe" if c2 == 1 else None)
                                wt("s", "pe")
                                if bi == 0 and i == 0 and prev_parity_ve is not None:
                                    wt("s", "ve", prev_parity_ve)
                                S(lambda e, i=i, BB=BB: e.activation(out=BB[:, i * T:(i + 1) * T], in_=pA[:, :], func=AF.Copy), "ac")
                                if not (bi == 1 and i == G - 1):
                                    wt("t", "ac", cnt["ac"])
                            cap["bcast_pe"] = cnt["pe"]
                        cap["evict_ac"] = cnt["ac"]
                    ng_end_ve = {}

                    emit_bcast(0, dve_done_at[-1] if dve_done_at else None)
                    for ng in range(NG):
                        Bb = BbA if ng % 2 == 0 else BbB
                        Cb = CbA if ng % 2 == 0 else CbB
                        if nxt is not None:
                            emit_proj1_pe(nxt, ng)
                        for et in range(NET):
                            it = scan_it[0]
                            par = it % 2
                            dts = xno_sb[:, par * T:(par + 1) * T]
                            dxs = xno_sb[:, 2048 + par * T:2048 + (par + 1) * T]
                            dab = dA_[:, par * 4096:(par + 1) * 4096]
                            if it >= 2:
                                wt("y", "ac", act_done_at[it - 2])
                                wt("y", "ve", dve_done_at[it - 2])
                            ld(dts, dt_scr[et * 128:(et + 1) * 128, sl])
                            ld(dxs, dtx_scr[et * 128:(et + 1) * 128, sl])
                            wt("s", "ld")
                            if it >= 2:
                                wt("s", "ve", dve_done_at[it - 2])
                            for i in range(G):
                                col = et * N + ng * G + i
                                S(lambda e, i=i, col=col, dab=dab, dts=dts: e.activation(
                                    out=dab[:, i * T:(i + 1) * T], in_=dts, func=AF.Exp,
                                    scale=A_sb[:, col:col + 1]), "ac" if i == G - 1 else None)
                            act_done_at.append(cnt["ac"])
                            wt("v", "ac", act_done_at[-1])
                            wt("v", "ld")
                            V(lambda e, dxs=dxs: e.tensor_tensor(
                                out=u_[:].rearrange("p (n t) -> p n t", n=G),
                                in0=dxs.unsqueeze(1).broadcast_to([128, G, T]),
                                in1=Bb[:].rearrange("p (n t) -> p n t", n=G), op=ALU.mult))
                            for i in range(G):
                                col = et * N + ng * G + i
                                V(lambda e, i=i, col=col, dab=dab: e.tensor_tensor_scan(
                                    out=h_[:, i * T:(i + 1) * T], data0=dab[:, i * T:(i + 1) * T],
                                    data1=u_[:, i * T:(i + 1) * T], initial=state[:, col:col + 1],
                                    op0=ALU.mult, op1=ALU.add))
                            V(lambda e, Cb=Cb: e.tensor_mul(u_[:], h_[:], Cb[:]))
                            V(lambda e: e.tensor_add(u_[:, 0:2 * T], u_[:, 0:2 * T], u_[:, 2 * T:4 * T]))
                            ya = yacc[:, et * T:(et + 1) * T]
                            if ng == 0:
                                if et == 0:
                                    wt("v", "pe", cap["wout_pe"])
                                V(lambda e, ya=ya: e.tensor_add(ya, u_[:, 0:T], u_[:, T:2 * T]))
                            else:
                                V(lambda e: e.tensor_add(u_[:, 0:T], u_[:, 0:T], u_[:, T:2 * T]))
                                V(lambda e, ya=ya: e.tensor_add(ya, ya, u_[:, 0:T]))
                            for i in range(G):
                                col = et * N + ng * G + i
                                V(lambda e, i=i, col=col: e.tensor_copy(state[:, col:col + 1], h_[:, (i + 1) * T - 1:(i + 1) * T]),
                                  "ve" if i == G - 1 else None)
                            dve_done_at.append(cnt["ve"])
                            scan_it[0] += 1
                            if et == 0 and ng + 1 < NG:
                                emit_bcast(ng + 1, ng_end_ve.get(ng - 1, dve_done_at[-1] if ng == 0 else None))
                        ng_end_ve[ng] = dve_done_at[-1]
                        if nxt is not None:
                            emit_proj1_dve(nxt, ng)
                            emit_proj1_act(nxt, ng)
                    wt("y", "st", cnt["st"])
                    for et in range(NET):
                        wt("y", "ve", cnt["ve"])
                        ld(yxc[:], xc_scr[et * 128:(et + 1) * 128, sl])
                        ld(yz[:], z_scr[et * 128:(et + 1) * 128, sl])
                        ya = yacc[:, et * T:(et + 1) * T]
                        wt("v", "ld")
                        V(lambda e, et=et, ya=ya: e.scalar_tensor_tensor(
                            out=ya, in0=yxc[:], scalar=Dp_sb[:, et:et + 1], in1=ya, op0=ALU.mult, op1=ALU.add))
                        V(lambda e, ya=ya: e.tensor_mul(ya, ya, yz[:]), "ve")
                    if nxt is not None:
                        emit_loop23(nxt)
                    wcvt_ac = []
                    for dti in range(NDT):
                        stg = wstg if dti % 2 == 0 else bcst
                        wt("y", "ve", cnt["ve"])
                        if dti >= 2:
                            wt("y", "ac", wcvt_ac[dti - 2])
                        else:
                            wt("y", "ac", cap["wstg_ac"])
                        wt("y", "pe", cnt["pe"] if dti == 0 else cnt["pe"])
                        for ke in range(NET):
                            ld(stg[:, ke * 128:(ke + 1) * 128], w["WoutT"][ke * 128:(ke + 1) * 128, dti * 128:(dti + 1) * 128])
                        wt("s", "ld")
                        wt("s", "pe", cnt["pe"])
                        S(lambda e, stg=stg: e.activation(out=wout_sb[:], in_=stg[:], func=AF.Copy), "ac")
                        cap["wstg_ac"] = cnt["ac"]
                        wcvt_ac.append(cnt["ac"])
                        for ts2 in range(2):
                            wt("t", "ve")
                            wt("t", "ac", cnt["ac"])
                            wt("t", "st", cnt["st"])
                            for ke in range(NET):
                                TE(lambda e, ke=ke, ts2=ts2: e.matmul(
                                    pB[:, 0:512], wout_sb[:, ke * 128:(ke + 1) * 128],
                                    yacc[:, ke * T + ts2 * 512:ke * T + ts2 * 512 + 512],
                                    start=(ke == 0), stop=(ke == NET - 1)), "pe" if ke == NET - 1 else None)
                            wt("v", "pe")
                            wt("v", "st", cnt["st"])
                            if out_rev:
                                V(lambda e: e.tensor_copy(wrev[:], pB[:, 0:512][:, ::-1]), "ve")
                                slot = HALF - (s * T + ts2 * 512) - 512
                            else:
                                V(lambda e: e.tensor_copy(wrev[:], pB[:, 0:512]), "ve")
                                slot = s * T + ts2 * 512
                            wt("g", "ve")
                            st((mamba_a if out_rev else mamba_b)[dti * 128:(dti + 1) * 128, slot:slot + 512], wrev[:])
                    cap["wout_ve"] = cnt["ve"]
                    cap["wout_pe"] = cnt["pe"]

            mamba_phase("p1", xn1_fwd, out_rev=True)

            wt("g", "ve", cnt["ve"])
            st(cc_in_t.ap(), state[:])
            wt("g", "st")
            gcc = eng.collective_compute(
                "AllReduce", ALU.add, replica_groups=[[0, 1], [2, 3], [4, 5], [6, 7]],
                ins=[cc_in_t.ap().opt()], outs=[cc_out_t.ap().opt()],
            ) if on("g") else None
            bump("cc", 1, gcc, "g")
            wt("g", "cc")
            i2 = eng.dma_start(out=blob2[:], in_=cc_out_t.ap()) if on("g") else None
            bump("st", 16, i2, "g")
            wt("v", "st")
            V(lambda e: e.tensor_sub(state[:], blob2[:], state[:]), "ve")

            mamba_phase("p2", xn1_rev, out_rev=False)

            sslab = ar3[:]
            def hs_ap(ht):
                return ar1[:, ht * T:(ht + 1) * T] if ht < 16 else ar2[:, (ht - 16) * T:(ht - 15) * T]
            for s in range(NSUP):
                sl = slice(s * T, (s + 1) * T)
                wt("y", "ve", cnt["ve"])
                wt("y", "st", cnt["st"])
                wt("y", "pe", cnt["pe"])
                wt("y", "ac", cnt["ac"])
                for dti in range(NDT):
                    rows = slice(dti * 128, (dti + 1) * 128)
                    ld(xn1_sb[:, dti * T:(dti + 1) * T], xn1_rev[rows, sl])
                    ld(xno_sb[:, dti * T:(dti + 1) * T], xno_rev[rows, sl])
                    ld(cvt[:], mamba_a[rows, sl])
                    ld(upt[:], mamba_b[rows, sl])
                    wt("v", "ld")
                    V(lambda e, dti=dti: e.tensor_add(sslab[:, dti * T:(dti + 1) * T], cvt[:], upt[:]), "ve")
                    wt("y", "ve")
                wt("s", "ve")
                for dti in range(NDT):
                    S(lambda e, i=dti: e.activation(out=ar2[:, 8192 + i * T:8192 + (i + 1) * T],
                                                    in_=sslab[:, i * T:(i + 1) * T], func=AF.Square),
                      "ac" if dti == NDT - 1 else None)
                wt("t", "ac")
                for h in range(2):
                    hs = slice(h * 512, (h + 1) * 512)
                    wt("t", "ve", cnt["ve"])
                    for dti in range(NDT):
                        TE(lambda e, i=dti, hs=hs: e.matmul(pR[:, :], ones_cb[:], sslab[:, i * T:(i + 1) * T][:, hs],
                                                           start=(i == 0), stop=(i == NDT - 1)))
                    for dti in range(NDT):
                        TE(lambda e, i=dti, hs=hs: e.matmul(pR2[:, :], ones_cb[:],
                                                            ar2[:, 8192 + i * T:8192 + (i + 1) * T][:, hs],
                                                            start=(i == 0), stop=(i == NDT - 1)),
                           "pe" if dti == NDT - 1 else None)
                    wt("v", "pe")
                    V(lambda e, hs=hs: e.tensor_scalar(out=row_mu[:, hs], in0=pR[:, :], scalar1=1.0 / D, scalar2=None, op0=ALU.mult))
                    V(lambda e, hs=hs: e.tensor_scalar(out=row_u[:, hs], in0=pR2[:, :], scalar1=1.0 / D, scalar2=None, op0=ALU.mult),
                      "ve")
                wt("s", "ve")
                S(lambda e: e.activation(out=row_v, in_=row_mu, func=AF.Square), "ac")
                wt("v", "ac")
                V(lambda e: e.tensor_sub(row_u, row_u, row_v), "ve")
                wt("s", "ve")
                S(lambda e: e.activation(out=row_v, in_=row_u, func=AF.Ln, bias=epsr[:, 0:1]))
                S(lambda e: e.activation(out=row_v, in_=row_v, func=AF.Exp, scale=-0.5), "ac")
                wt("v", "ac")
                wt("v", "ve")
                V(lambda e: e.tensor_mul(row_u, row_mu, row_v), "ve")
                for h in range(2):
                    hs = slice(h * 512, (h + 1) * 512)
                    bcast_row(row_v, hs, bcst[:, 0:1024])
                    bcast_row(row_u, hs, bcst[:, 1024:2048])
                for dti in range(NDT):
                    ss = sslab[:, dti * T:(dti + 1) * T]
                    V(lambda e, ss=ss: e.tensor_mul(ss, ss, bcst[:, 0:1024]))
                    V(lambda e, ss=ss: e.tensor_sub(ss, ss, bcst[:, 1024:2048]))
                    V(lambda e, ss=ss, dti=dti: e.tensor_scalar(
                        out=ss, in0=ss, scalar1=lnp[:, 2 * NDT + dti:2 * NDT + dti + 1],
                        scalar2=lnp[:, 3 * NDT + dti:3 * NDT + dti + 1], op0=ALU.mult, op1=ALU.add))
                    V(lambda e, ss=ss, dti=dti: e.tensor_add(ss, ss, xn1_sb[:, dti * T:(dti + 1) * T]),
                      "ve" if dti == NDT - 1 else None)
                pe_hist = []
                ac_hist = []
                for ht in range(NHT):
                    stg = wstg if ht % 2 == 0 else bcst
                    wsb = wsb0 if ht % 2 == 0 else wsb1
                    if ht >= 2:
                        wt("y", "ve", ac_hist[ht - 2])
                    else:
                        wt("y", "ve", cnt["ve"])
                    for kd in range(NDT):
                        ld(stg[:, kd * 128:(kd + 1) * 128].bitcast(I32), wgT[kd * 128:(kd + 1) * 128, ht * 128:(ht + 1) * 128])
                    for kd in range(NDT):
                        ld(stg[:, 1024 + kd * 128:1024 + (kd + 1) * 128].bitcast(I32),
                           wuT[kd * 128:(kd + 1) * 128, ht * 128:(ht + 1) * 128])
                    wt("v", "ld")
                    if ht >= 2:
                        wt("v", "pe", pe_hist[ht - 2])
                    V(lambda e, stg=stg, wsb=wsb: e.tensor_scalar(out=wsb[:], in0=stg[:].bitcast(I32), scalar1=QS,
                                                                  scalar2=None, op0=ALU.mult), "ve")
                    ac_hist.append(cnt["ve"])
                    wt("t", "ve", cnt["ve"])
                    if ht >= 1:
                        wt("t", "ve", cnt["ve"])
                    for c2 in range(2):
                        for kd in range(NDT):
                            TE(lambda e, kd=kd, c2=c2, wsb=wsb: e.matmul(
                                pA[:, c2 * 512:(c2 + 1) * 512], wsb[:, kd * 128:(kd + 1) * 128],
                                xno_sb[:, kd * T + c2 * 512:kd * T + (c2 + 1) * 512],
                                start=(kd == 0), stop=(kd == NDT - 1)))
                    for c2 in range(2):
                        for kd in range(NDT):
                            TE(lambda e, kd=kd, c2=c2, wsb=wsb: e.matmul(
                                pB[:, c2 * 512:(c2 + 1) * 512], wsb[:, 1024 + kd * 128:1024 + (kd + 1) * 128],
                                xno_sb[:, kd * T + c2 * 512:kd * T + (c2 + 1) * 512],
                                start=(kd == 0), stop=(kd == NDT - 1)),
                               "pe" if (kd == NDT - 1 and c2 == 1) else None)
                    pe_hist.append(cnt["pe"])
                    wt("s", "pe")
                    S(lambda e: e.activation(out=cvt[:], in_=pA[:, :], func=AF.Silu), "ac")
                    wt("v", "ac")
                    V(lambda e: e.tensor_copy(upt[:], pB[:, :]))
                    V(lambda e, ht=ht: e.tensor_mul(hs_ap(ht), cvt[:], upt[:]), "ve")
                if debug and s == 0:
                    wt("g", "ve", cnt["ve"])
                    for ht in range(NHT):
                        st(h_dbg[ht * 128:(ht + 1) * 128, :], hs_ap(ht))
                for dti in range(NDT):
                    stg = wstg if dti % 2 == 0 else bcst
                    wsb = wsb0 if dti % 2 == 0 else wsb1
                    wt("y", "ve", cnt["ve"])
                    wt("y", "pe", cnt["pe"])
                    for kh in range(12):
                        ld(stg[:, kh * 128:(kh + 1) * 128].bitcast(I32),
                           wdT[kh * 128:(kh + 1) * 128, dti * 128:(dti + 1) * 128])
                    stg2 = bcst if dti % 2 == 0 else wstg
                    for kh in range(12, 24):
                        ld(stg2[:, (kh - 12) * 128:(kh - 11) * 128].bitcast(I32),
                           wdT[kh * 128:(kh + 1) * 128, dti * 128:(dti + 1) * 128])
                    wt("v", "ld")
                    V(lambda e, stg=stg, wsb=wsb: e.tensor_scalar(out=wsb[:, 0:1536], in0=stg[:, 0:1536].bitcast(I32),
                                                                  scalar1=QS, scalar2=None, op0=ALU.mult), "ve")
                    wt("t", "ve", cnt["ve"])
                    for c2 in range(2):
                        for kh in range(12):
                            TE(lambda e, kh=kh, c2=c2, wsb=wsb: e.matmul(
                                pX[:, c2 * 512:(c2 + 1) * 512], wsb[:, kh * 128:(kh + 1) * 128],
                                hs_ap(kh)[:, c2 * 512:(c2 + 1) * 512],
                                start=(kh == 0), stop=False),
                               "pe" if (kh == 11 and c2 == 1) else None)
                    wt("v", "pe")
                    V(lambda e, stg2=stg2, wsb=wsb: e.tensor_scalar(out=wsb[:, 0:1536], in0=stg2[:, 0:1536].bitcast(I32),
                                                                    scalar1=QS, scalar2=None, op0=ALU.mult), "ve")
                    wt("t", "ve", cnt["ve"])
                    for c2 in range(2):
                        for kh in range(12):
                            TE(lambda e, kh=kh, c2=c2, wsb=wsb: e.matmul(
                                pX[:, c2 * 512:(c2 + 1) * 512], wsb[:, kh * 128:(kh + 1) * 128],
                                hs_ap(12 + kh)[:, c2 * 512:(c2 + 1) * 512],
                                start=False, stop=(kh == 11)),
                               "pe" if (kh == 11 and c2 == 1) else None)
                    for ts2 in range(2):
                        tss = slice(ts2 * TS3, (ts2 + 1) * TS3)
                        wt("y", "st", cnt["st"])
                        ld(ot[:], x_loc[dti * 128:(dti + 1) * 128, sl][:, tss])
                        wt("s", "pe")
                        wt("s", "ld")
                        wt("s", "ve", cnt["ve"])
                        S(lambda e, ts2=ts2: e.activation(out=sgt[:], in_=pX[:, ts2 * TS3:(ts2 + 1) * TS3], func=AF.Silu), "ac")
                        wt("v", "ac")
                        wt("v", "st", cnt["st"])
                        if debug and s == 0:
                            wt("g", "ac", cnt["ac"])
                            st(g_dbg[dti * 128:(dti + 1) * 128, tss], sgt[:])
                        V(lambda e, dti=dti, tss=tss: e.tensor_mul(sgt[:], sgt[:], sslab[:, dti * T:(dti + 1) * T][:, tss]))
                        V(lambda e: e.tensor_add(ot[:], ot[:], sgt[:]), "ve")
                        wt("g", "ve")
                        st(out_feat[dti * 128:(dti + 1) * 128, sl][:, tss], ot[:])
                        wt("v", "st")

            wt("g", "st", cnt["st"])

        @block.vector
        def _(e):
            prog("v", e)

        @block.scalar
        def _(e):
            prog("s", e)

        @block.tensor
        def _(e):
            prog("t", e)

        @block.gpsimd
        def _(e):
            prog("g", e)

        @block.sync
        def _(e):
            prog("y", e)

    return nc


def _prep_core_inputs(inputs, b, role):
    x = np.asarray(inputs["L0_attn_x"])
    if role == 0:
        x_loc = np.ascontiguousarray(x[b, :HALF][::-1].T)
    else:
        x_loc = np.ascontiguousarray(x[b, HALF:].T)

    def mamba_set(m):
        pre = f"L0_attn_{m}_"
        return dict(
            WinT=np.ascontiguousarray(np.asarray(inputs[pre + "W_in"]).T),
            WxpT=np.ascontiguousarray(np.asarray(inputs[pre + "W_xp"]).T),
            WdtT=np.ascontiguousarray(np.asarray(inputs[pre + "W_dt"]).T),
            WoutT=np.ascontiguousarray(np.asarray(inputs[pre + "W_out"]).T),
            Alog=np.ascontiguousarray(np.asarray(inputs[pre + "A_log"]).reshape(NET, 128, N).transpose(1, 0, 2).reshape(128, NET * N)),
            convw=np.ascontiguousarray(np.asarray(inputs[pre + "conv_w"]).reshape(NET, 128, 4).transpose(1, 0, 2).reshape(128, NET * 4)),
            convb=np.ascontiguousarray(np.asarray(inputs[pre + "conv_b"]).reshape(NET, 128).T),
            dtb=np.ascontiguousarray(np.asarray(inputs[pre + "dt_b"]).reshape(NET, 128).T),
            Dp=np.ascontiguousarray(np.asarray(inputs[pre + "D"]).reshape(NET, 128).T),
        )

    m1, m2 = mamba_set("m1"), mamba_set("m2")
    p1, p2 = (m1, m2) if role == 0 else (m2, m1)
    im = {"x_loc": x_loc}
    for ph, mm in (("p1", p1), ("p2", p2)):
        for k, v in mm.items():
            im[f"{ph}_{k}"] = v
    for k, src in (("ln1w", "L0_attn_ln1_w"), ("ln1b", "L0_attn_ln1_b"),
                   ("ln2w", "L0_attn_ln2_w"), ("ln2b", "L0_attn_ln2_b")):
        im[k] = np.ascontiguousarray(np.asarray(inputs[src]).reshape(NDT, 128).T)
    im["wgT"] = np.ascontiguousarray(np.asarray(inputs["L0_ffn_w_gate"]).T)
    im["wuT"] = np.ascontiguousarray(np.asarray(inputs["L0_ffn_w_up"]).T)
    im["wdT"] = np.ascontiguousarray(np.asarray(inputs["L0_ffn_w_down"]).T)
    return im


def kernel(**inputs):
    nc = build_program()
    in_maps = [_prep_core_inputs(inputs, r // 2, r % 2) for r in range(8)]
    res = run_bass_kernel_spmd(nc, in_maps, core_ids=list(range(8)))
    out = np.empty((4, 8192, 1024), dtype=np.float32)
    for r in range(8):
        of = np.asarray(res.results[r]["out_feat"])
        b, role = r // 2, r % 2
        if role == 0:
            out[b, :HALF] = of.T[::-1]
        else:
            out[b, HALF:] = of.T
    return out


# version 14
# speedup vs baseline: 1.2560x; 1.0239x over previous
import numpy as np
import concourse.bass as bass
import concourse.mybir as mybir
from concourse.bass_utils import run_bass_kernel_spmd

F32 = mybir.dt.float32
BF16 = mybir.dt.bfloat16
I32 = mybir.dt.int32
AF = mybir.ActivationFunctionType
ALU = mybir.AluOpType

D = 1024
E = 2048
N = 64
R = 64
HALF = 4096
T = 1024
NSUP = HALF // T
G = 4
NG = N // G
NET = E // 128
NDT = D // 128
NHT = 3072 // 128
TS3 = 512
LN_EPS = 1e-5
QS = 1.0 / 127.0


def build_program(debug=False):
    nc = bass.Bass("TRN2", target_bir_lowering=False)
    skind = "ExternalOutput" if debug else "Internal"

    x_loc = nc.dram_tensor("x_loc", [D, HALF], F32, kind="ExternalInput").ap()
    W = {}
    for ph in ("p1", "p2"):
        W[ph] = dict(
            WinT=nc.dram_tensor(f"{ph}_WinT", [D, 2 * E], F32, kind="ExternalInput").ap(),
            WxpT=nc.dram_tensor(f"{ph}_WxpT", [E, 192], F32, kind="ExternalInput").ap(),
            WdtT=nc.dram_tensor(f"{ph}_WdtT", [R, E], F32, kind="ExternalInput").ap(),
            WoutT=nc.dram_tensor(f"{ph}_WoutT", [E, D], F32, kind="ExternalInput").ap(),
            Alog=nc.dram_tensor(f"{ph}_Alog", [128, NET * N], F32, kind="ExternalInput").ap(),
            convw=nc.dram_tensor(f"{ph}_convw", [128, NET * 4], F32, kind="ExternalInput").ap(),
            convb=nc.dram_tensor(f"{ph}_convb", [128, NET], F32, kind="ExternalInput").ap(),
            dtb=nc.dram_tensor(f"{ph}_dtb", [128, NET], F32, kind="ExternalInput").ap(),
            Dp=nc.dram_tensor(f"{ph}_Dp", [128, NET], F32, kind="ExternalInput").ap(),
        )
    ln1w = nc.dram_tensor("ln1w", [128, NDT], F32, kind="ExternalInput").ap()
    ln1b = nc.dram_tensor("ln1b", [128, NDT], F32, kind="ExternalInput").ap()
    ln2w = nc.dram_tensor("ln2w", [128, NDT], F32, kind="ExternalInput").ap()
    ln2b = nc.dram_tensor("ln2b", [128, NDT], F32, kind="ExternalInput").ap()
    wgT = nc.dram_tensor("wgT", [D, 3072], I32, kind="ExternalInput").ap()
    wuT = nc.dram_tensor("wuT", [D, 3072], I32, kind="ExternalInput").ap()
    wdT = nc.dram_tensor("wdT", [3072, D], I32, kind="ExternalInput").ap()
    out_feat = nc.dram_tensor("out_feat", [D, HALF], F32, kind="ExternalOutput").ap()

    xn1_fwd = nc.dram_tensor("xn1_fwd", [D, HALF], BF16, kind=skind).ap()
    xn1_rev = nc.dram_tensor("xn1_rev", [D, HALF], BF16, kind=skind).ap()
    xno_rev = nc.dram_tensor("xno_rev", [D, HALF], BF16, kind=skind).ap()
    mamba_a = nc.dram_tensor("mamba_a", [D, HALF], BF16, kind=skind).ap()
    mamba_b = nc.dram_tensor("mamba_b", [D, HALF], BF16, kind=skind).ap()
    xc_scr = nc.dram_tensor("xc_scr", [E, HALF], BF16, kind=skind).ap()
    z_scr = nc.dram_tensor("z_scr", [E, HALF], BF16, kind="Internal").ap()
    dt_scr = nc.dram_tensor("dt_scr", [E, HALF], BF16, kind=skind).ap()
    dtx_scr = nc.dram_tensor("dtx_scr", [E, HALF], BF16, kind="Internal").ap()
    bc_scr = nc.dram_tensor("bc_scr", [128, HALF], BF16, kind=skind).ap()
    h_dbg = nc.dram_tensor("h_dbg", [3072, T], BF16, kind=skind).ap()
    g_dbg = nc.dram_tensor("g_dbg", [D, T], BF16, kind=skind).ap()
    cc_in_t = nc.dram_tensor("cc_in", [128, 1072], F32)
    cc_out_t = nc.dram_tensor("cc_out", [128, 1072], F32)

    import contextlib
    with contextlib.ExitStack() as _stk:
        xn1_sb = _stk.enter_context(nc.sbuf_tensor("xn1_sb", [128, NDT * T], BF16))
        xno_sb = _stk.enter_context(nc.sbuf_tensor("xno_sb", [128, NDT * T], BF16))
        ar1 = _stk.enter_context(nc.sbuf_tensor("ar1", [128, 16384], BF16))
        ar2 = _stk.enter_context(nc.sbuf_tensor("ar2", [128, 16384], BF16))
        ar3 = _stk.enter_context(nc.sbuf_tensor("ar3", [128, 8192], BF16))
        brow = _stk.enter_context(nc.sbuf_tensor("brow", [1, 4096], BF16))
        bcD = _stk.enter_context(nc.sbuf_tensor("bcD", [128, 4096], BF16))
        xp_ext = _stk.enter_context(nc.sbuf_tensor("xp_ext", [128, T + 4], BF16))
        dtlo_bf = _stk.enter_context(nc.sbuf_tensor("dtlo_bf", [64, T], BF16))
        bstg = _stk.enter_context(nc.sbuf_tensor("bstg", [64, T], BF16))
        bstg2 = _stk.enter_context(nc.sbuf_tensor("bstg2", [64, T], BF16))
        state = _stk.enter_context(nc.sbuf_tensor("state", [128, 1072], F32))
        blob2 = _stk.enter_context(nc.sbuf_tensor("blob2", [128, 1072], F32))
        A_sb = _stk.enter_context(nc.sbuf_tensor("A_sb", [128, NET * N], F32))
        row_mu = brow[:].bitcast(F32)[:, 0:1024]
        row_u = brow[:].bitcast(F32)[:, 1024:2048]
        row_v = bcD[:, 0:2048].bitcast(F32)[0:1, 0:1024]
        row_w = bcD[:, 2048:4096].bitcast(F32)[0:1, 0:1024]
        bcst = _stk.enter_context(nc.sbuf_tensor("bcst", [128, 2048], F32))
        wstg = _stk.enter_context(nc.sbuf_tensor("wstg", [128, 2048], F32))
        wsb0 = _stk.enter_context(nc.sbuf_tensor("wsb0", [128, 2048], BF16))
        wsb1 = _stk.enter_context(nc.sbuf_tensor("wsb1", [128, 2048], BF16))
        wxp_sb = _stk.enter_context(nc.sbuf_tensor("wxp_sb", [128, NET * 192], BF16))
        wdt_sb = _stk.enter_context(nc.sbuf_tensor("wdt_sb", [64, E], BF16))
        cvt = _stk.enter_context(nc.sbuf_tensor("cvt", [128, T], BF16))
        xct = _stk.enter_context(nc.sbuf_tensor("xct", [128, T], BF16))
        zt = _stk.enter_context(nc.sbuf_tensor("zt", [128, T], BF16))
        upt = _stk.enter_context(nc.sbuf_tensor("upt", [128, T], BF16))
        sgt = _stk.enter_context(nc.sbuf_tensor("sgt", [128, TS3], BF16))
        ot = _stk.enter_context(nc.sbuf_tensor("ot", [128, TS3], F32))
        wrev = _stk.enter_context(nc.sbuf_tensor("wrev", [128, 512], BF16))
        yxc = _stk.enter_context(nc.sbuf_tensor("yxc", [128, T], BF16))
        yz = _stk.enter_context(nc.sbuf_tensor("yz", [128, T], BF16))
        wout_sb = _stk.enter_context(nc.sbuf_tensor("wout_sb", [128, 2048], BF16))
        convw_sb = _stk.enter_context(nc.sbuf_tensor("convw_sb", [128, NET * 4], F32))
        convb_sb = _stk.enter_context(nc.sbuf_tensor("convb_sb", [128, NET], F32))
        dtb_sb = _stk.enter_context(nc.sbuf_tensor("dtb_sb", [128, NET], F32))
        Dp_sb = _stk.enter_context(nc.sbuf_tensor("Dp_sb", [128, NET], F32))
        lnp = _stk.enter_context(nc.sbuf_tensor("lnp", [128, 4 * NDT], F32))
        ones_rb = _stk.enter_context(nc.sbuf_tensor("ones_rb", [1, 128], BF16))
        ones_rf = _stk.enter_context(nc.sbuf_tensor("ones_rf", [1, 128], F32))
        ones_cb = _stk.enter_context(nc.sbuf_tensor("ones_cb", [128, 1], BF16))
        ones_cf = _stk.enter_context(nc.sbuf_tensor("ones_cf", [128, 1], F32))
        epsr = _stk.enter_context(nc.sbuf_tensor("epsr", [1, 2], F32))
        pA = _stk.enter_context(nc.psum_tensor("pA", [128, T], F32))
        pB = _stk.enter_context(nc.psum_tensor("pB", [128, T], F32))
        pX = _stk.enter_context(nc.psum_tensor("pX", [128, T], F32))
        pR = _stk.enter_context(nc.psum_tensor("pR", [1, 512], F32))
        pR2 = _stk.enter_context(nc.psum_tensor("pR2", [1, 512], F32))
        s_ld = _stk.enter_context(nc.semaphore("s_ld"))
        s_st = _stk.enter_context(nc.semaphore("s_st"))
        s_pe = _stk.enter_context(nc.semaphore("s_pe"))
        s_ve = _stk.enter_context(nc.semaphore("s_ve"))
        s_ac = _stk.enter_context(nc.semaphore("s_ac"))
        s_cc = _stk.enter_context(nc.semaphore("s_cc"))
        block = _stk.enter_context(nc.Block())
        SEM = dict(ld=s_ld, st=s_st, pe=s_pe, ve=s_ve, ac=s_ac, cc=s_cc)

        def prog(role, eng):
            cnt = {k: 0 for k in SEM}
            on = lambda r: role == r
            act_done_at = []
            dve_done_at = []
            scan_it = [0]

            def bump(sem, n, instr, r):
                cnt[sem] += n
                if instr is not None and role == r:
                    instr.then_inc(SEM[sem], n)

            def wt(r, sem, val=None):
                if role == r:
                    eng.wait_ge(SEM[sem], cnt[sem] if val is None else val)

            def ld(dst, src):
                i = eng.dma_start(out=dst, in_=src) if on("y") else None
                bump("ld", 16, i, "y")

            def st(dst, src):
                i = eng.dma_start(out=dst, in_=src) if on("g") else None
                bump("st", 16, i, "g")

            def V(fn, sem=None):
                i = fn(eng) if on("v") else None
                if sem is not None:
                    bump(sem, 1, i, "v")

            def S(fn, sem=None):
                i = fn(eng) if on("s") else None
                if sem is not None:
                    bump(sem, 1, i, "s")

            def TE(fn, sem=None):
                i = fn(eng) if on("t") else None
                if sem is not None:
                    bump(sem, 1, i, "t")

            def bcast_row(src_row, hs, dst):
                wt("t", "ve", cnt["ve"])
                TE(lambda e: e.matmul(pA[:, 0:512], ones_rf[:], src_row[:, hs], start=True, stop=True), "pe")
                wt("v", "pe")
                V(lambda e: e.tensor_copy(dst[:, hs], pA[:, 0:512]), "ve")

            V(lambda e: e.memset(ones_rb[:], 1.0))
            V(lambda e: e.memset(ones_rf[:], 1.0))
            V(lambda e: e.memset(ones_cb[:], 1.0))
            V(lambda e: e.memset(ones_cf[:], 1.0))
            V(lambda e: e.memset(epsr[:, 0:1], LN_EPS))
            V(lambda e: e.memset(epsr[:, 1:2], LN_EPS * LN_EPS), "ve")
            ld(lnp[:, 0:NDT], ln1w)
            ld(lnp[:, NDT:2 * NDT], ln1b)
            ld(lnp[:, 2 * NDT:3 * NDT], ln2w)
            ld(lnp[:, 3 * NDT:4 * NDT], ln2b)
            wt("t", "ve", cnt["ve"])

            xt = ar1[:].bitcast(F32)
            sqf = ar2[:].bitcast(F32)
            for s in range(NSUP):
                sl = slice(s * T, (s + 1) * T)
                wt("y", "pe", cnt["pe"])
                wt("y", "ve", cnt["ve"])
                for dti in range(NDT):
                    ld(xt[:, dti * T:(dti + 1) * T], x_loc[dti * 128:(dti + 1) * 128, sl])
                wt("s", "ld")
                for dti in range(NDT):
                    S(lambda e, i=dti: e.activation(out=sqf[:, i * T:(i + 1) * T], in_=xt[:, i * T:(i + 1) * T], func=AF.Square),
                      "ac" if dti == NDT - 1 else None)
                wt("t", "ac")
                wt("t", "ld")
                for h in range(2):
                    hs = slice(h * 512, (h + 1) * 512)
                    wt("t", "ve", cnt["ve"])
                    for dti in range(NDT):
                        TE(lambda e, i=dti, hs=hs: e.matmul(pR[:, :], ones_cf[:], xt[:, i * T:(i + 1) * T][:, hs],
                                                           start=(i == 0), stop=(i == NDT - 1)))
                    for dti in range(NDT):
                        TE(lambda e, i=dti, hs=hs: e.matmul(pR2[:, :], ones_cf[:], sqf[:, i * T:(i + 1) * T][:, hs],
                                                            start=(i == 0), stop=(i == NDT - 1)),
                           "pe" if dti == NDT - 1 else None)
                    wt("v", "pe")
                    V(lambda e, hs=hs: e.tensor_scalar(out=row_mu[:, hs], in0=pR[:, :], scalar1=1.0 / D, scalar2=None, op0=ALU.mult))
                    V(lambda e, hs=hs: e.tensor_scalar(out=row_u[:, hs], in0=pR2[:, :], scalar1=1.0 / D, scalar2=None, op0=ALU.mult),
                      "ve")
                wt("s", "ve")
                S(lambda e: e.activation(out=row_v, in_=row_mu, func=AF.Square), "ac")
                wt("v", "ac")
                wt("v", "ve")
                V(lambda e: e.tensor_sub(row_u, row_u, row_v), "ve")
                wt("s", "ve")
                S(lambda e: e.activation(out=row_v, in_=row_u, func=AF.Ln, bias=epsr[:, 0:1]), "ac")
                wt("s", "ac")
                S(lambda e: e.activation(out=row_v, in_=row_v, func=AF.Exp, scale=-0.5), "ac")
                wt("s", "ac")
                S(lambda e: e.activation(out=row_w, in_=row_u, func=AF.Ln, scale=1.0 + LN_EPS, bias=epsr[:, 1:2]), "ac")
                wt("s", "ac")
                S(lambda e: e.activation(out=row_w, in_=row_w, func=AF.Exp, scale=-0.5), "ac")
                wt("v", "ac")
                V(lambda e: e.tensor_mul(row_u, row_mu, row_v), "ve")
                wt("v", "ve")
                V(lambda e: e.tensor_mul(row_mu, row_mu, row_w), "ve")
                for pass_i, (ra, rb_) in enumerate(((row_v, row_u), (row_w, row_mu))):
                    for h in range(2):
                        hs = slice(h * 512, (h + 1) * 512)
                        bcast_row(ra, hs, bcst[:, 0:1024])
                        bcast_row(rb_, hs, bcst[:, 1024:2048])
                    for dti in range(NDT):
                        xs_ = xt[:, dti * T:(dti + 1) * T]
                        tmp = sqf[:, dti * T:(dti + 1) * T]
                        wt("v", "st", cnt["st"])
                        V(lambda e, xs_=xs_, tmp=tmp: e.tensor_mul(tmp, xs_, bcst[:, 0:1024]), "ve")
                        wt("v", "ve")
                        if pass_i == 0:
                            V(lambda e, tmp=tmp: e.tensor_sub(cvt[:], tmp, bcst[:, 1024:2048]), "ve")
                            wt("g", "ve")
                            st(xno_rev[dti * 128:(dti + 1) * 128, sl], cvt[:])
                        else:
                            V(lambda e, tmp=tmp: e.tensor_sub(tmp, tmp, bcst[:, 1024:2048]), "ve")
                            wt("v", "ve")
                            V(lambda e, tmp=tmp, dti=dti: e.tensor_scalar(
                                out=cvt[:], in0=tmp, scalar1=lnp[:, dti:dti + 1], scalar2=lnp[:, NDT + dti:NDT + dti + 1],
                                op0=ALU.mult, op1=ALU.add), "ve")
                            wt("g", "ve")
                            st(xn1_rev[dti * 128:(dti + 1) * 128, sl], cvt[:])
                            wt("v", "ve")
                            V(lambda e: e.tensor_copy(xct[:], cvt[:, ::-1]), "ve")
                            wt("g", "ve")
                            st(xn1_fwd[dti * 128:(dti + 1) * 128, HALF - (s + 1) * T:HALF - s * T], xct[:])

            def mamba_phase(ph, xn_src, out_rev):
                w = W[ph]
                wt("y", "ac", cnt["ac"])
                wt("y", "ve", cnt["ve"])
                wt("y", "pe", cnt["pe"])
                wt("y", "st", cnt["st"])
                ld(A_sb[:], w["Alog"])
                ld(convw_sb[:], w["convw"])
                ld(convb_sb[:], w["convb"])
                ld(dtb_sb[:], w["dtb"])
                ld(Dp_sb[:], w["Dp"])
                for ke in range(NET):
                    wt("y", "ac", cnt["ac"])
                    ld(wstg[:, 0:192], w["WxpT"][ke * 128:(ke + 1) * 128, :])
                    wt("s", "ld")
                    S(lambda e, ke=ke: e.activation(out=wxp_sb[:, ke * 192:(ke + 1) * 192], in_=wstg[:, 0:192], func=AF.Copy), "ac")
                for hh in range(2):
                    wt("y", "ac", cnt["ac"])
                    ld(wstg[0:64, 0:1024], w["WdtT"][:, hh * 1024:(hh + 1) * 1024])
                    wt("s", "ld")
                    S(lambda e, hh=hh: e.activation(out=wdt_sb[:, hh * 1024:(hh + 1) * 1024], in_=wstg[0:64, 0:1024], func=AF.Copy), "ac")
                wt("s", "ld")
                S(lambda e: e.activation(out=A_sb[:], in_=A_sb[:], func=AF.Exp), "ac")
                wt("v", "ac")
                V(lambda e: e.tensor_scalar(out=A_sb[:], in0=A_sb[:], scalar1=-1.0, scalar2=None, op0=ALU.mult), "ve")
                if ph == "p1":
                    V(lambda e: e.memset(state[:], 0.0), "ve")
                wt("s", "ve", cnt["ve"])

                cap = dict(wstg_ac=cnt["ac"], wsb_pe=[], xp_evict_ve=[], zsilu_ac=[],
                           xn1_pe=cnt["pe"], bcast_pe=cnt["pe"], evict_ac=cnt["ac"],
                           projcvt_ac=[], projmm_pe=[], conv_ve=[], xcz_st=[], last_win_mm=cnt["pe"],
                           wout_ve=cnt["ve"], wout_pe=cnt["pe"], l23_pe=cnt["pe"], l23_ve=cnt["ve"])
                proj_idx = [0]

                def emit_xn1_load(s):
                    wt("y", "pe", cap["last_win_mm"])
                    for dti in range(NDT):
                        ld(xn1_sb[:, dti * T:(dti + 1) * T], xn_src[dti * 128:(dti + 1) * 128, slice(s * T, (s + 1) * T)])
                    cap["xn1_ld"] = cnt["ld"]

                def emit_proj1_pe(s, et):
                    j = proj_idx[0]
                    par = j % 2
                    wsb = wsb0 if par == 0 else wsb1
                    wt("y", "ac", cap["wstg_ac"])
                    for kd in range(NDT):
                        ld(wstg[:, kd * 128:(kd + 1) * 128], w["WinT"][kd * 128:(kd + 1) * 128, et * 128:et * 128 + 128])
                    for kd in range(NDT):
                        ld(wstg[:, 1024 + kd * 128:1024 + (kd + 1) * 128],
                           w["WinT"][kd * 128:(kd + 1) * 128, E + et * 128:E + et * 128 + 128])
                    wt("s", "ld")
                    wt("s", "pe", cap["l23_pe"])
                    wt("s", "ve", cap["l23_ve"])
                    if j >= 2:
                        wt("s", "pe", cap["wsb_pe"][j - 2])
                    S(lambda e, wsb=wsb: e.activation(out=wsb[:], in_=wstg[:], func=AF.Copy), "ac")
                    cap["wstg_ac"] = cnt["ac"]
                    cap["projcvt_ac"].append(cnt["ac"])
                    wt("t", "ac", cnt["ac"])
                    wt("t", "ld", cap["xn1_ld"])
                    wt("t", "ve", cap["wout_ve"])
                    if j >= 1:
                        wt("t", "ve", cap["xp_evict_ve"][j - 1])
                        wt("t", "ac", cap["zsilu_ac"][j - 1]) if len(cap["zsilu_ac"]) >= j else None
                    for kd in range(NDT):
                        for c2 in range(2):
                            TE(lambda e, kd=kd, c2=c2, wsb=wsb: e.matmul(
                                pB[:, c2 * 512:(c2 + 1) * 512], wsb[:, kd * 128:(kd + 1) * 128],
                                xn1_sb[:, kd * T + c2 * 512:kd * T + (c2 + 1) * 512],
                                start=(kd == 0), stop=(kd == NDT - 1)),
                               "pe" if (kd == NDT - 1 and c2 == 1) else None)
                    for kd in range(NDT):
                        for c2 in range(2):
                            TE(lambda e, kd=kd, c2=c2, wsb=wsb: e.matmul(
                                pX[:, c2 * 512:(c2 + 1) * 512], wsb[:, 1024 + kd * 128:1024 + (kd + 1) * 128],
                                xn1_sb[:, kd * T + c2 * 512:kd * T + (c2 + 1) * 512],
                                start=(kd == 0), stop=(kd == NDT - 1)),
                               "pe" if (kd == NDT - 1 and c2 == 1) else None)
                    cap["wsb_pe"].append(cnt["pe"])
                    cap["projmm_pe"].append(cnt["pe"])
                    cap["last_win_mm"] = cnt["pe"]
                    proj_idx[0] += 1

                def emit_proj1_dve(s, et):
                    j = len(cap["conv_ve"])
                    wt("v", "pe", cap["projmm_pe"][j])
                    if j >= 1:
                        wt("v", "st", cap["xcz_st"][j - 1])
                    V(lambda e, et=et: e.tensor_copy(xp_ext[:, 1:4], state[:, 1024 + et * 3:1024 + et * 3 + 3]))
                    V(lambda e: e.tensor_copy(xp_ext[:, 4:4 + T], pB[:, :]), "ve")
                    cap["xp_evict_ve"].append(cnt["ve"])
                    V(lambda e, et=et: e.tensor_scalar(out=cvt[:], in0=xp_ext[:, 1:1 + T],
                                                       scalar1=convw_sb[:, et * 4:et * 4 + 1], scalar2=None, op0=ALU.mult))
                    for k in (1, 2, 3):
                        V(lambda e, k=k, et=et: e.scalar_tensor_tensor(
                            out=cvt[:], in0=xp_ext[:, 1 + k:1 + k + T], scalar=convw_sb[:, et * 4 + k:et * 4 + k + 1],
                            in1=cvt[:], op0=ALU.mult, op1=ALU.add), "ve" if k == 3 else None)
                    wt("v", "ve")
                    V(lambda e, et=et: e.tensor_copy(state[:, 1024 + et * 3:1024 + et * 3 + 3], xp_ext[:, 1 + T:4 + T]), "ve")
                    cap["conv_ve"].append(cnt["ve"])

                def emit_proj1_act(s, et):
                    j = len(cap["zsilu_ac"])
                    sl = slice(s * T, (s + 1) * T)
                    wt("s", "ve", cap["conv_ve"][j])
                    if j >= 1:
                        wt("s", "st", cap["xcz_st"][j - 1])
                    S(lambda e, et=et: e.activation(out=xct[:], in_=cvt[:], func=AF.Silu, bias=convb_sb[:, et:et + 1]))
                    S(lambda e: e.activation(out=zt[:], in_=pX[:, :], func=AF.Silu), "ac")
                    cap["zsilu_ac"].append(cnt["ac"])
                    wt("g", "ac", cnt["ac"])
                    st(xc_scr[et * 128:(et + 1) * 128, sl], xct[:])
                    st(z_scr[et * 128:(et + 1) * 128, sl], zt[:])
                    cap["xcz_st"].append(cnt["st"])

                def emit_loop23(s):
                    sl = slice(s * T, (s + 1) * T)
                    pe_at2 = []
                    for ke in range(NET):
                        par = ke % 2
                        dst = (wsb0 if par == 0 else wsb1)[:, 0:T]
                        wt("y", "st", cap["xcz_st"][-1] if cap["xcz_st"] else cnt["st"])
                        if ke >= 2:
                            wt("y", "pe", pe_at2[ke - 2])
                        else:
                            wt("y", "pe", cap["wsb_pe"][-1] if cap["wsb_pe"] else cnt["pe"])
                        ld(dst, xc_scr[ke * 128:(ke + 1) * 128, sl])
                        wt("t", "ld")
                        for c2 in range(2):
                            TE(lambda e, ke=ke, c2=c2, dst=dst: e.matmul(
                                pX[:, c2 * 512:(c2 + 1) * 512], wxp_sb[:, ke * 192:ke * 192 + 128],
                                dst[:, c2 * 512:(c2 + 1) * 512], start=(ke == 0), stop=(ke == NET - 1)))
                        for c2 in range(2):
                            TE(lambda e, ke=ke, c2=c2, dst=dst: e.matmul(
                                pB[0:64, c2 * 512:(c2 + 1) * 512], wxp_sb[:, ke * 192 + 128:(ke + 1) * 192],
                                dst[:, c2 * 512:(c2 + 1) * 512], start=(ke == 0), stop=(ke == NET - 1)),
                               "pe" if c2 == 1 else None)
                        pe_at2.append(cnt["pe"])
                    wt("v", "pe")
                    wt("v", "st", cnt["st"])
                    V(lambda e: e.tensor_copy(dtlo_bf[:], pX[0:64, :]))
                    V(lambda e: e.tensor_copy(bstg[:], pX[64:128, :]), "ve")
                    wt("g", "ve")
                    st(bc_scr[0:64, sl], bstg[:])
                    V(lambda e: e.tensor_copy(bstg2[:], pB[0:64, :]), "ve")
                    wt("g", "ve")
                    st(bc_scr[64:128, sl], bstg2[:])
                    for et in range(NET):
                        pp = pA if et % 2 == 0 else pX
                        wt("t", "ve", cnt["ve"])
                        wt("t", "ac", cnt["ac"] - (4 if et >= 1 else 0))
                        for c2 in range(2):
                            TE(lambda e, et=et, c2=c2, pp=pp: e.matmul(
                                pp[:, c2 * 512:(c2 + 1) * 512], wdt_sb[:, et * 128:(et + 1) * 128],
                                dtlo_bf[:, c2 * 512:(c2 + 1) * 512], start=True, stop=True),
                               "pe" if c2 == 1 else None)
                        wt("s", "pe")
                        wt("s", "ve", cnt["ve"])
                        S(lambda e, et=et, pp=pp: e.activation(out=cvt[:], in_=pp[:, :], func=AF.Abs, bias=dtb_sb[:, et:et + 1]), "ac")
                        wt("s", "ac")
                        S(lambda e: e.activation(out=cvt[:], in_=cvt[:], func=AF.Exp, scale=-1.0), "ac")
                        wt("s", "ac")
                        S(lambda e: e.activation(out=cvt[:], in_=cvt[:], func=AF.Ln, bias=ones_cf[:, 0:1]), "ac")
                        wt("s", "ac")
                        S(lambda e, et=et, pp=pp: e.activation(out=zt[:], in_=pp[:, :], func=AF.Relu, bias=dtb_sb[:, et:et + 1]), "ac")
                        par = et % 2
                        dst = (wsb0 if par == 0 else wsb1)[:, 0:T]
                        wt("y", "ve", cnt["ve"])
                        ld(dst, xc_scr[et * 128:(et + 1) * 128, sl])
                        wt("v", "ac")
                        wt("v", "ld")
                        wt("v", "st", cnt["st"])
                        V(lambda e: e.tensor_add(xct[:], cvt[:], zt[:]))
                        V(lambda e, dst=dst: e.tensor_mul(upt[:], xct[:], dst), "ve")
                        wt("g", "ve")
                        st(dt_scr[et * 128:(et + 1) * 128, sl], xct[:])
                        st(dtx_scr[et * 128:(et + 1) * 128, sl], upt[:])
                    cap["l23_pe"] = cnt["pe"]
                    cap["l23_ve"] = cnt["ve"]

                emit_xn1_load(0)
                for et in range(NET):
                    emit_proj1_pe(0, et)
                    emit_proj1_dve(0, et)
                    emit_proj1_act(0, et)
                emit_loop23(0)

                for s in range(NSUP):
                    sl = slice(s * T, (s + 1) * T)
                    nxt = s + 1 if s + 1 < NSUP else None
                    dA_ = ar1[:, 0:8192]
                    u_ = ar1[:, 8192:12288]
                    h_ = ar1[:, 12288:16384]
                    BbA = ar3[:, 0:4096]
                    CbA = ar3[:, 4096:8192]
                    BbB = bcst[:].bitcast(BF16)
                    CbB = bcD[:]
                    yacc = ar2[:]
                    if nxt is not None:
                        emit_xn1_load(nxt)

                    def emit_bcast(ng, prev_parity_ve):
                        BB2 = (BbA, CbA) if ng % 2 == 0 else (BbB, CbB)
                        if ng == 0:
                            wt("y", "st", cnt["st"])
                            wt("y", "ve", cnt["ve"])
                            wt("y", "ac", cnt["ac"])
                        for bi, BB in enumerate(BB2):
                            wt("y", "pe", cap["bcast_pe"])
                            ld(brow[:].rearrange("o (p t) -> o p t", p=G),
                               bc_scr[bi * 64 + ng * G:bi * 64 + (ng + 1) * G, sl].unsqueeze(0))
                            wt("t", "ld")
                            if prev_parity_ve is not None:
                                wt("t", "ve", prev_parity_ve)
                            wt("t", "ac", cap["evict_ac"])
                            for i in range(G):
                                for c2 in range(2):
                                    TE(lambda e, i=i, c2=c2: e.matmul(
                                        pA[:, c2 * 512:(c2 + 1) * 512], ones_rb[:],
                                        brow[:, i * T + c2 * 512:i * T + (c2 + 1) * 512], start=True, stop=True),
                                       "pe" if c2 == 1 else None)
                                wt("s", "pe")
                                if bi == 0 and i == 0 and prev_parity_ve is not None:
                                    wt("s", "ve", prev_parity_ve)
                                S(lambda e, i=i, BB=BB: e.activation(out=BB[:, i * T:(i + 1) * T], in_=pA[:, :], func=AF.Copy), "ac")
                                if not (bi == 1 and i == G - 1):
                                    wt("t", "ac", cnt["ac"])
                            cap["bcast_pe"] = cnt["pe"]
                        cap["evict_ac"] = cnt["ac"]
                    ng_end_ve = {}

                    emit_bcast(0, dve_done_at[-1] if dve_done_at else None)
                    for ng in range(NG):
                        Bb = BbA if ng % 2 == 0 else BbB
                        Cb = CbA if ng % 2 == 0 else CbB
                        if nxt is not None:
                            emit_proj1_pe(nxt, ng)
                        for et in range(NET):
                            it = scan_it[0]
                            par = it % 2
                            dts = xno_sb[:, par * T:(par + 1) * T]
                            dxs = xno_sb[:, 2048 + par * T:2048 + (par + 1) * T]
                            dab = dA_[:, par * 4096:(par + 1) * 4096]
                            if it >= 2:
                                wt("y", "ac", act_done_at[it - 2])
                                wt("y", "ve", dve_done_at[it - 2])
                            ld(dts, dt_scr[et * 128:(et + 1) * 128, sl])
                            ld(dxs, dtx_scr[et * 128:(et + 1) * 128, sl])
                            wt("s", "ld")
                            if it >= 2:
                                wt("s", "ve", dve_done_at[it - 2])
                            for i in range(G):
                                col = et * N + ng * G + i
                                S(lambda e, i=i, col=col, dab=dab, dts=dts: e.activation(
                                    out=dab[:, i * T:(i + 1) * T], in_=dts, func=AF.Exp,
                                    scale=A_sb[:, col:col + 1]), "ac" if i == G - 1 else None)
                            act_done_at.append(cnt["ac"])
                            wt("v", "ac", act_done_at[-1])
                            wt("v", "ld")
                            V(lambda e, dxs=dxs: e.tensor_tensor(
                                out=u_[:].rearrange("p (n t) -> p n t", n=G),
                                in0=dxs.unsqueeze(1).broadcast_to([128, G, T]),
                                in1=Bb[:].rearrange("p (n t) -> p n t", n=G), op=ALU.mult))
                            for i in range(G):
                                col = et * N + ng * G + i
                                V(lambda e, i=i, col=col, dab=dab: e.tensor_tensor_scan(
                                    out=h_[:, i * T:(i + 1) * T], data0=dab[:, i * T:(i + 1) * T],
                                    data1=u_[:, i * T:(i + 1) * T], initial=state[:, col:col + 1],
                                    op0=ALU.mult, op1=ALU.add))
                            V(lambda e, Cb=Cb: e.tensor_mul(u_[:], h_[:], Cb[:]))
                            V(lambda e: e.tensor_add(u_[:, 0:2 * T], u_[:, 0:2 * T], u_[:, 2 * T:4 * T]))
                            ya = yacc[:, et * T:(et + 1) * T]
                            if ng == 0:
                                if et == 0:
                                    wt("v", "pe", cap["wout_pe"])
                                V(lambda e, ya=ya: e.tensor_add(ya, u_[:, 0:T], u_[:, T:2 * T]))
                            else:
                                V(lambda e: e.tensor_add(u_[:, 0:T], u_[:, 0:T], u_[:, T:2 * T]))
                                V(lambda e, ya=ya: e.tensor_add(ya, ya, u_[:, 0:T]))
                            for i in range(G):
                                col = et * N + ng * G + i
                                V(lambda e, i=i, col=col: e.tensor_copy(state[:, col:col + 1], h_[:, (i + 1) * T - 1:(i + 1) * T]),
                                  "ve" if i == G - 1 else None)
                            dve_done_at.append(cnt["ve"])
                            scan_it[0] += 1
                            if et == 2 and ng + 1 < NG:
                                emit_bcast(ng + 1, ng_end_ve.get(ng - 1, dve_done_at[-1] if ng == 0 else None))
                        ng_end_ve[ng] = dve_done_at[-1]
                        if nxt is not None:
                            emit_proj1_dve(nxt, ng)
                            emit_proj1_act(nxt, ng)
                    wt("y", "st", cnt["st"])
                    for et in range(NET):
                        wt("y", "ve", cnt["ve"])
                        ld(yxc[:], xc_scr[et * 128:(et + 1) * 128, sl])
                        ld(yz[:], z_scr[et * 128:(et + 1) * 128, sl])
                        ya = yacc[:, et * T:(et + 1) * T]
                        wt("v", "ld")
                        V(lambda e, et=et, ya=ya: e.scalar_tensor_tensor(
                            out=ya, in0=yxc[:], scalar=Dp_sb[:, et:et + 1], in1=ya, op0=ALU.mult, op1=ALU.add))
                        V(lambda e, ya=ya: e.tensor_mul(ya, ya, yz[:]), "ve")
                    if nxt is not None:
                        emit_loop23(nxt)
                    wcvt_ac = []
                    for dti in range(NDT):
                        stg = wstg if dti % 2 == 0 else bcst
                        wt("y", "ve", cnt["ve"])
                        if dti >= 2:
                            wt("y", "ac", wcvt_ac[dti - 2])
                        else:
                            wt("y", "ac", cap["wstg_ac"])
                        wt("y", "pe", cnt["pe"] if dti == 0 else cnt["pe"])
                        for ke in range(NET):
                            ld(stg[:, ke * 128:(ke + 1) * 128], w["WoutT"][ke * 128:(ke + 1) * 128, dti * 128:(dti + 1) * 128])
                        wt("s", "ld")
                        wt("s", "pe", cnt["pe"])
                        S(lambda e, stg=stg: e.activation(out=wout_sb[:], in_=stg[:], func=AF.Copy), "ac")
                        cap["wstg_ac"] = cnt["ac"]
                        wcvt_ac.append(cnt["ac"])
                        for ts2 in range(2):
                            wt("t", "ve")
                            wt("t", "ac", cnt["ac"])
                            wt("t", "st", cnt["st"])
                            for ke in range(NET):
                                TE(lambda e, ke=ke, ts2=ts2: e.matmul(
                                    pB[:, 0:512], wout_sb[:, ke * 128:(ke + 1) * 128],
                                    yacc[:, ke * T + ts2 * 512:ke * T + ts2 * 512 + 512],
                                    start=(ke == 0), stop=(ke == NET - 1)), "pe" if ke == NET - 1 else None)
                            wt("v", "pe")
                            wt("v", "st", cnt["st"])
                            if out_rev:
                                V(lambda e: e.tensor_copy(wrev[:], pB[:, 0:512][:, ::-1]), "ve")
                                slot = HALF - (s * T + ts2 * 512) - 512
                            else:
                                V(lambda e: e.tensor_copy(wrev[:], pB[:, 0:512]), "ve")
                                slot = s * T + ts2 * 512
                            wt("g", "ve")
                            st((mamba_a if out_rev else mamba_b)[dti * 128:(dti + 1) * 128, slot:slot + 512], wrev[:])
                    cap["wout_ve"] = cnt["ve"]
                    cap["wout_pe"] = cnt["pe"]

            mamba_phase("p1", xn1_fwd, out_rev=True)

            wt("g", "ve", cnt["ve"])
            st(cc_in_t.ap(), state[:])
            wt("g", "st")
            gcc = eng.collective_compute(
                "AllReduce", ALU.add, replica_groups=[[0, 1], [2, 3], [4, 5], [6, 7]],
                ins=[cc_in_t.ap().opt()], outs=[cc_out_t.ap().opt()],
            ) if on("g") else None
            bump("cc", 1, gcc, "g")
            wt("g", "cc")
            i2 = eng.dma_start(out=blob2[:], in_=cc_out_t.ap()) if on("g") else None
            bump("st", 16, i2, "g")
            wt("v", "st")
            V(lambda e: e.tensor_sub(state[:], blob2[:], state[:]), "ve")

            mamba_phase("p2", xn1_rev, out_rev=False)

            sslab = ar3[:]
            def hs_ap(ht):
                return ar1[:, ht * T:(ht + 1) * T] if ht < 16 else ar2[:, (ht - 16) * T:(ht - 15) * T]
            for s in range(NSUP):
                sl = slice(s * T, (s + 1) * T)
                wt("y", "ve", cnt["ve"])
                wt("y", "st", cnt["st"])
                wt("y", "pe", cnt["pe"])
                wt("y", "ac", cnt["ac"])
                for dti in range(NDT):
                    rows = slice(dti * 128, (dti + 1) * 128)
                    ld(xn1_sb[:, dti * T:(dti + 1) * T], xn1_rev[rows, sl])
                    ld(xno_sb[:, dti * T:(dti + 1) * T], xno_rev[rows, sl])
                    ld(cvt[:], mamba_a[rows, sl])
                    ld(upt[:], mamba_b[rows, sl])
                    wt("v", "ld")
                    V(lambda e, dti=dti: e.tensor_add(sslab[:, dti * T:(dti + 1) * T], cvt[:], upt[:]), "ve")
                    wt("y", "ve")
                wt("s", "ve")
                for dti in range(NDT):
                    S(lambda e, i=dti: e.activation(out=ar2[:, 8192 + i * T:8192 + (i + 1) * T],
                                                    in_=sslab[:, i * T:(i + 1) * T], func=AF.Square),
                      "ac" if dti == NDT - 1 else None)
                wt("t", "ac")
                for h in range(2):
                    hs = slice(h * 512, (h + 1) * 512)
                    wt("t", "ve", cnt["ve"])
                    for dti in range(NDT):
                        TE(lambda e, i=dti, hs=hs: e.matmul(pR[:, :], ones_cb[:], sslab[:, i * T:(i + 1) * T][:, hs],
                                                           start=(i == 0), stop=(i == NDT - 1)))
                    for dti in range(NDT):
                        TE(lambda e, i=dti, hs=hs: e.matmul(pR2[:, :], ones_cb[:],
                                                            ar2[:, 8192 + i * T:8192 + (i + 1) * T][:, hs],
                                                            start=(i == 0), stop=(i == NDT - 1)),
                           "pe" if dti == NDT - 1 else None)
                    wt("v", "pe")
                    V(lambda e, hs=hs: e.tensor_scalar(out=row_mu[:, hs], in0=pR[:, :], scalar1=1.0 / D, scalar2=None, op0=ALU.mult))
                    V(lambda e, hs=hs: e.tensor_scalar(out=row_u[:, hs], in0=pR2[:, :], scalar1=1.0 / D, scalar2=None, op0=ALU.mult),
                      "ve")
                wt("s", "ve")
                S(lambda e: e.activation(out=row_v, in_=row_mu, func=AF.Square), "ac")
                wt("v", "ac")
                V(lambda e: e.tensor_sub(row_u, row_u, row_v), "ve")
                wt("s", "ve")
                S(lambda e: e.activation(out=row_v, in_=row_u, func=AF.Ln, bias=epsr[:, 0:1]))
                S(lambda e: e.activation(out=row_v, in_=row_v, func=AF.Exp, scale=-0.5), "ac")
                wt("v", "ac")
                wt("v", "ve")
                V(lambda e: e.tensor_mul(row_u, row_mu, row_v), "ve")
                for h in range(2):
                    hs = slice(h * 512, (h + 1) * 512)
                    bcast_row(row_v, hs, bcst[:, 0:1024])
                    bcast_row(row_u, hs, bcst[:, 1024:2048])
                for dti in range(NDT):
                    ss = sslab[:, dti * T:(dti + 1) * T]
                    V(lambda e, ss=ss: e.tensor_mul(ss, ss, bcst[:, 0:1024]))
                    V(lambda e, ss=ss: e.tensor_sub(ss, ss, bcst[:, 1024:2048]))
                    V(lambda e, ss=ss, dti=dti: e.tensor_scalar(
                        out=ss, in0=ss, scalar1=lnp[:, 2 * NDT + dti:2 * NDT + dti + 1],
                        scalar2=lnp[:, 3 * NDT + dti:3 * NDT + dti + 1], op0=ALU.mult, op1=ALU.add))
                    V(lambda e, ss=ss, dti=dti: e.tensor_add(ss, ss, xn1_sb[:, dti * T:(dti + 1) * T]),
                      "ve" if dti == NDT - 1 else None)
                pe_hist = []
                ac_hist = []
                for ht in range(NHT):
                    stg = wstg if ht % 2 == 0 else bcst
                    wsb = wsb0 if ht % 2 == 0 else wsb1
                    if ht >= 2:
                        wt("y", "ve", ac_hist[ht - 2])
                    else:
                        wt("y", "ve", cnt["ve"])
                    for kd in range(NDT):
                        ld(stg[:, kd * 128:(kd + 1) * 128].bitcast(I32), wgT[kd * 128:(kd + 1) * 128, ht * 128:(ht + 1) * 128])
                    for kd in range(NDT):
                        ld(stg[:, 1024 + kd * 128:1024 + (kd + 1) * 128].bitcast(I32),
                           wuT[kd * 128:(kd + 1) * 128, ht * 128:(ht + 1) * 128])
                    wt("v", "ld")
                    if ht >= 2:
                        wt("v", "pe", pe_hist[ht - 2])
                    V(lambda e, stg=stg, wsb=wsb: e.tensor_scalar(out=wsb[:], in0=stg[:].bitcast(I32), scalar1=QS,
                                                                  scalar2=None, op0=ALU.mult), "ve")
                    ac_hist.append(cnt["ve"])
                    wt("t", "ve", cnt["ve"])
                    if ht >= 1:
                        wt("t", "ve", cnt["ve"])
                    for c2 in range(2):
                        for kd in range(NDT):
                            TE(lambda e, kd=kd, c2=c2, wsb=wsb: e.matmul(
                                pA[:, c2 * 512:(c2 + 1) * 512], wsb[:, kd * 128:(kd + 1) * 128],
                                xno_sb[:, kd * T + c2 * 512:kd * T + (c2 + 1) * 512],
                                start=(kd == 0), stop=(kd == NDT - 1)))
                    for c2 in range(2):
                        for kd in range(NDT):
                            TE(lambda e, kd=kd, c2=c2, wsb=wsb: e.matmul(
                                pB[:, c2 * 512:(c2 + 1) * 512], wsb[:, 1024 + kd * 128:1024 + (kd + 1) * 128],
                                xno_sb[:, kd * T + c2 * 512:kd * T + (c2 + 1) * 512],
                                start=(kd == 0), stop=(kd == NDT - 1)),
                               "pe" if (kd == NDT - 1 and c2 == 1) else None)
                    pe_hist.append(cnt["pe"])
                    wt("s", "pe")
                    S(lambda e: e.activation(out=cvt[:], in_=pA[:, :], func=AF.Silu), "ac")
                    wt("v", "ac")
                    V(lambda e: e.tensor_copy(upt[:], pB[:, :]))
                    V(lambda e, ht=ht: e.tensor_mul(hs_ap(ht), cvt[:], upt[:]), "ve")
                if debug and s == 0:
                    wt("g", "ve", cnt["ve"])
                    for ht in range(NHT):
                        st(h_dbg[ht * 128:(ht + 1) * 128, :], hs_ap(ht))
                for dti in range(NDT):
                    stg = wstg if dti % 2 == 0 else bcst
                    wsb = wsb0 if dti % 2 == 0 else wsb1
                    wt("y", "ve", cnt["ve"])
                    wt("y", "pe", cnt["pe"])
                    for kh in range(12):
                        ld(stg[:, kh * 128:(kh + 1) * 128].bitcast(I32),
                           wdT[kh * 128:(kh + 1) * 128, dti * 128:(dti + 1) * 128])
                    stg2 = bcst if dti % 2 == 0 else wstg
                    for kh in range(12, 24):
                        ld(stg2[:, (kh - 12) * 128:(kh - 11) * 128].bitcast(I32),
                           wdT[kh * 128:(kh + 1) * 128, dti * 128:(dti + 1) * 128])
                    wt("v", "ld")
                    V(lambda e, stg=stg, wsb=wsb: e.tensor_scalar(out=wsb[:, 0:1536], in0=stg[:, 0:1536].bitcast(I32),
                                                                  scalar1=QS, scalar2=None, op0=ALU.mult), "ve")
                    wt("t", "ve", cnt["ve"])
                    for c2 in range(2):
                        for kh in range(12):
                            TE(lambda e, kh=kh, c2=c2, wsb=wsb: e.matmul(
                                pX[:, c2 * 512:(c2 + 1) * 512], wsb[:, kh * 128:(kh + 1) * 128],
                                hs_ap(kh)[:, c2 * 512:(c2 + 1) * 512],
                                start=(kh == 0), stop=False),
                               "pe" if (kh == 11 and c2 == 1) else None)
                    wt("v", "pe")
                    V(lambda e, stg2=stg2, wsb=wsb: e.tensor_scalar(out=wsb[:, 0:1536], in0=stg2[:, 0:1536].bitcast(I32),
                                                                    scalar1=QS, scalar2=None, op0=ALU.mult), "ve")
                    wt("t", "ve", cnt["ve"])
                    for c2 in range(2):
                        for kh in range(12):
                            TE(lambda e, kh=kh, c2=c2, wsb=wsb: e.matmul(
                                pX[:, c2 * 512:(c2 + 1) * 512], wsb[:, kh * 128:(kh + 1) * 128],
                                hs_ap(12 + kh)[:, c2 * 512:(c2 + 1) * 512],
                                start=False, stop=(kh == 11)),
                               "pe" if (kh == 11 and c2 == 1) else None)
                    for ts2 in range(2):
                        tss = slice(ts2 * TS3, (ts2 + 1) * TS3)
                        wt("y", "st", cnt["st"])
                        ld(ot[:], x_loc[dti * 128:(dti + 1) * 128, sl][:, tss])
                        wt("s", "pe")
                        wt("s", "ld")
                        wt("s", "ve", cnt["ve"])
                        S(lambda e, ts2=ts2: e.activation(out=sgt[:], in_=pX[:, ts2 * TS3:(ts2 + 1) * TS3], func=AF.Silu), "ac")
                        wt("v", "ac")
                        wt("v", "st", cnt["st"])
                        if debug and s == 0:
                            wt("g", "ac", cnt["ac"])
                            st(g_dbg[dti * 128:(dti + 1) * 128, tss], sgt[:])
                        V(lambda e, dti=dti, tss=tss: e.tensor_mul(sgt[:], sgt[:], sslab[:, dti * T:(dti + 1) * T][:, tss]))
                        V(lambda e: e.tensor_add(ot[:], ot[:], sgt[:]), "ve")
                        wt("g", "ve")
                        st(out_feat[dti * 128:(dti + 1) * 128, sl][:, tss], ot[:])
                        wt("v", "st")

            wt("g", "st", cnt["st"])

        @block.vector
        def _(e):
            prog("v", e)

        @block.scalar
        def _(e):
            prog("s", e)

        @block.tensor
        def _(e):
            prog("t", e)

        @block.gpsimd
        def _(e):
            prog("g", e)

        @block.sync
        def _(e):
            prog("y", e)

    return nc


def _prep_core_inputs(inputs, b, role):
    x = np.asarray(inputs["L0_attn_x"])
    if role == 0:
        x_loc = np.ascontiguousarray(x[b, :HALF][::-1].T)
    else:
        x_loc = np.ascontiguousarray(x[b, HALF:].T)

    def mamba_set(m):
        pre = f"L0_attn_{m}_"
        return dict(
            WinT=np.ascontiguousarray(np.asarray(inputs[pre + "W_in"]).T),
            WxpT=np.ascontiguousarray(np.asarray(inputs[pre + "W_xp"]).T),
            WdtT=np.ascontiguousarray(np.asarray(inputs[pre + "W_dt"]).T),
            WoutT=np.ascontiguousarray(np.asarray(inputs[pre + "W_out"]).T),
            Alog=np.ascontiguousarray(np.asarray(inputs[pre + "A_log"]).reshape(NET, 128, N).transpose(1, 0, 2).reshape(128, NET * N)),
            convw=np.ascontiguousarray(np.asarray(inputs[pre + "conv_w"]).reshape(NET, 128, 4).transpose(1, 0, 2).reshape(128, NET * 4)),
            convb=np.ascontiguousarray(np.asarray(inputs[pre + "conv_b"]).reshape(NET, 128).T),
            dtb=np.ascontiguousarray(np.asarray(inputs[pre + "dt_b"]).reshape(NET, 128).T),
            Dp=np.ascontiguousarray(np.asarray(inputs[pre + "D"]).reshape(NET, 128).T),
        )

    m1, m2 = mamba_set("m1"), mamba_set("m2")
    p1, p2 = (m1, m2) if role == 0 else (m2, m1)
    im = {"x_loc": x_loc}
    for ph, mm in (("p1", p1), ("p2", p2)):
        for k, v in mm.items():
            im[f"{ph}_{k}"] = v
    for k, src in (("ln1w", "L0_attn_ln1_w"), ("ln1b", "L0_attn_ln1_b"),
                   ("ln2w", "L0_attn_ln2_w"), ("ln2b", "L0_attn_ln2_b")):
        im[k] = np.ascontiguousarray(np.asarray(inputs[src]).reshape(NDT, 128).T)
    im["wgT"] = np.ascontiguousarray(np.asarray(inputs["L0_ffn_w_gate"]).T)
    im["wuT"] = np.ascontiguousarray(np.asarray(inputs["L0_ffn_w_up"]).T)
    im["wdT"] = np.ascontiguousarray(np.asarray(inputs["L0_ffn_w_down"]).T)
    return im


def kernel(**inputs):
    nc = build_program()
    in_maps = [_prep_core_inputs(inputs, r // 2, r % 2) for r in range(8)]
    res = run_bass_kernel_spmd(nc, in_maps, core_ids=list(range(8)))
    out = np.empty((4, 8192, 1024), dtype=np.float32)
    for r in range(8):
        of = np.asarray(res.results[r]["out_feat"])
        b, role = r // 2, r % 2
        if role == 0:
            out[b, :HALF] = of.T[::-1]
        else:
            out[b, HALF:] = of.T
    return out


# version 16
# speedup vs baseline: 1.2653x; 1.0074x over previous
import numpy as np
import concourse.bass as bass
import concourse.mybir as mybir
from concourse.bass_utils import run_bass_kernel_spmd

F32 = mybir.dt.float32
BF16 = mybir.dt.bfloat16
I32 = mybir.dt.int32
AF = mybir.ActivationFunctionType
ALU = mybir.AluOpType

D = 1024
E = 2048
N = 64
R = 64
HALF = 4096
T = 1024
NSUP = HALF // T
G = 4
NG = N // G
NET = E // 128
NDT = D // 128
NHT = 3072 // 128
TS3 = 512
LN_EPS = 1e-5
QS = 1.0 / 127.0


def build_program(debug=False):
    nc = bass.Bass("TRN2", target_bir_lowering=False)
    skind = "ExternalOutput" if debug else "Internal"

    x_loc = nc.dram_tensor("x_loc", [D, HALF], F32, kind="ExternalInput").ap()
    W = {}
    for ph in ("p1", "p2"):
        W[ph] = dict(
            WinT=nc.dram_tensor(f"{ph}_WinT", [D, 2 * E], F32, kind="ExternalInput").ap(),
            WxpT=nc.dram_tensor(f"{ph}_WxpT", [E, 192], F32, kind="ExternalInput").ap(),
            WdtT=nc.dram_tensor(f"{ph}_WdtT", [R, E], F32, kind="ExternalInput").ap(),
            WoutT=nc.dram_tensor(f"{ph}_WoutT", [E, D], F32, kind="ExternalInput").ap(),
            Alog=nc.dram_tensor(f"{ph}_Alog", [128, NET * N], F32, kind="ExternalInput").ap(),
            convw=nc.dram_tensor(f"{ph}_convw", [128, NET * 4], F32, kind="ExternalInput").ap(),
            convb=nc.dram_tensor(f"{ph}_convb", [128, NET], F32, kind="ExternalInput").ap(),
            dtb=nc.dram_tensor(f"{ph}_dtb", [128, NET], F32, kind="ExternalInput").ap(),
            Dp=nc.dram_tensor(f"{ph}_Dp", [128, NET], F32, kind="ExternalInput").ap(),
        )
    ln1w = nc.dram_tensor("ln1w", [128, NDT], F32, kind="ExternalInput").ap()
    ln1b = nc.dram_tensor("ln1b", [128, NDT], F32, kind="ExternalInput").ap()
    ln2w = nc.dram_tensor("ln2w", [128, NDT], F32, kind="ExternalInput").ap()
    ln2b = nc.dram_tensor("ln2b", [128, NDT], F32, kind="ExternalInput").ap()
    wgT = nc.dram_tensor("wgT", [D, 3072], I32, kind="ExternalInput").ap()
    wuT = nc.dram_tensor("wuT", [D, 3072], I32, kind="ExternalInput").ap()
    wdT = nc.dram_tensor("wdT", [3072, D], I32, kind="ExternalInput").ap()
    out_feat = nc.dram_tensor("out_feat", [D, HALF], F32, kind="ExternalOutput").ap()

    xn1_fwd = nc.dram_tensor("xn1_fwd", [D, HALF], BF16, kind=skind).ap()
    xn1_rev = nc.dram_tensor("xn1_rev", [D, HALF], BF16, kind=skind).ap()
    xno_rev = nc.dram_tensor("xno_rev", [D, HALF], BF16, kind=skind).ap()
    mamba_a = nc.dram_tensor("mamba_a", [D, HALF], BF16, kind=skind).ap()
    mamba_b = nc.dram_tensor("mamba_b", [D, HALF], BF16, kind=skind).ap()
    xc_scr = nc.dram_tensor("xc_scr", [E, HALF], BF16, kind=skind).ap()
    z_scr = nc.dram_tensor("z_scr", [E, HALF], BF16, kind="Internal").ap()
    dt_scr = nc.dram_tensor("dt_scr", [E, HALF], BF16, kind=skind).ap()
    dtx_scr = nc.dram_tensor("dtx_scr", [E, HALF], BF16, kind="Internal").ap()
    bc_scr = nc.dram_tensor("bc_scr", [128, HALF], BF16, kind=skind).ap()
    h_dbg = nc.dram_tensor("h_dbg", [3072, T], BF16, kind=skind).ap()
    g_dbg = nc.dram_tensor("g_dbg", [D, T], BF16, kind=skind).ap()
    cc_in_t = nc.dram_tensor("cc_in", [128, 1072], F32)
    cc_out_t = nc.dram_tensor("cc_out", [128, 1072], F32)

    import contextlib
    with contextlib.ExitStack() as _stk:
        xn1_sb = _stk.enter_context(nc.sbuf_tensor("xn1_sb", [128, NDT * T], BF16))
        xno_sb = _stk.enter_context(nc.sbuf_tensor("xno_sb", [128, NDT * T], BF16))
        ar1 = _stk.enter_context(nc.sbuf_tensor("ar1", [128, 16384], BF16))
        ar2 = _stk.enter_context(nc.sbuf_tensor("ar2", [128, 16384], BF16))
        ar3 = _stk.enter_context(nc.sbuf_tensor("ar3", [128, 8192], BF16))
        brow = _stk.enter_context(nc.sbuf_tensor("brow", [1, 4096], BF16))
        bcD = _stk.enter_context(nc.sbuf_tensor("bcD", [128, 4096], BF16))
        xp_ext = _stk.enter_context(nc.sbuf_tensor("xp_ext", [128, T + 4], BF16))
        dtlo_bf = _stk.enter_context(nc.sbuf_tensor("dtlo_bf", [64, T], BF16))
        bstg = _stk.enter_context(nc.sbuf_tensor("bstg", [64, T], BF16))
        bstg2 = _stk.enter_context(nc.sbuf_tensor("bstg2", [64, T], BF16))
        state = _stk.enter_context(nc.sbuf_tensor("state", [128, 1072], F32))
        blob2 = _stk.enter_context(nc.sbuf_tensor("blob2", [128, 1072], F32))
        A_sb = _stk.enter_context(nc.sbuf_tensor("A_sb", [128, NET * N], F32))
        row_mu = brow[:].bitcast(F32)[:, 0:1024]
        row_u = brow[:].bitcast(F32)[:, 1024:2048]
        row_v = bcD[:, 0:2048].bitcast(F32)[0:1, 0:1024]
        row_w = bcD[:, 2048:4096].bitcast(F32)[0:1, 0:1024]
        bcst = _stk.enter_context(nc.sbuf_tensor("bcst", [128, 2048], F32))
        wstg = _stk.enter_context(nc.sbuf_tensor("wstg", [128, 2048], F32))
        wsb0 = _stk.enter_context(nc.sbuf_tensor("wsb0", [128, 2048], BF16))
        wsb1 = _stk.enter_context(nc.sbuf_tensor("wsb1", [128, 2048], BF16))
        wxp_sb = _stk.enter_context(nc.sbuf_tensor("wxp_sb", [128, NET * 192], BF16))
        wdt_sb = _stk.enter_context(nc.sbuf_tensor("wdt_sb", [64, E], BF16))
        cvt = _stk.enter_context(nc.sbuf_tensor("cvt", [128, T], BF16))
        xct = _stk.enter_context(nc.sbuf_tensor("xct", [128, T], BF16))
        zt = _stk.enter_context(nc.sbuf_tensor("zt", [128, T], BF16))
        upt = _stk.enter_context(nc.sbuf_tensor("upt", [128, T], BF16))
        sgt = _stk.enter_context(nc.sbuf_tensor("sgt", [128, TS3], BF16))
        ot = _stk.enter_context(nc.sbuf_tensor("ot", [128, TS3], F32))
        wrev = _stk.enter_context(nc.sbuf_tensor("wrev", [128, 512], BF16))
        yxc = _stk.enter_context(nc.sbuf_tensor("yxc", [128, T], BF16))
        yz = _stk.enter_context(nc.sbuf_tensor("yz", [128, T], BF16))
        wout_sb = _stk.enter_context(nc.sbuf_tensor("wout_sb", [128, 2048], BF16))
        convw_sb = _stk.enter_context(nc.sbuf_tensor("convw_sb", [128, NET * 4], F32))
        convb_sb = _stk.enter_context(nc.sbuf_tensor("convb_sb", [128, NET], F32))
        dtb_sb = _stk.enter_context(nc.sbuf_tensor("dtb_sb", [128, NET], F32))
        Dp_sb = _stk.enter_context(nc.sbuf_tensor("Dp_sb", [128, NET], F32))
        lnp = _stk.enter_context(nc.sbuf_tensor("lnp", [128, 4 * NDT], F32))
        ones_rb = _stk.enter_context(nc.sbuf_tensor("ones_rb", [1, 128], BF16))
        ones_rf = _stk.enter_context(nc.sbuf_tensor("ones_rf", [1, 128], F32))
        ones_cb = _stk.enter_context(nc.sbuf_tensor("ones_cb", [128, 1], BF16))
        ones_cf = _stk.enter_context(nc.sbuf_tensor("ones_cf", [128, 1], F32))
        epsr = _stk.enter_context(nc.sbuf_tensor("epsr", [1, 2], F32))
        pA = _stk.enter_context(nc.psum_tensor("pA", [128, T], F32))
        pB = _stk.enter_context(nc.psum_tensor("pB", [128, T], F32))
        pX = _stk.enter_context(nc.psum_tensor("pX", [128, T], F32))
        pR = _stk.enter_context(nc.psum_tensor("pR", [1, 512], F32))
        pR2 = _stk.enter_context(nc.psum_tensor("pR2", [1, 512], F32))
        s_ld = _stk.enter_context(nc.semaphore("s_ld"))
        s_st = _stk.enter_context(nc.semaphore("s_st"))
        s_pe = _stk.enter_context(nc.semaphore("s_pe"))
        s_ve = _stk.enter_context(nc.semaphore("s_ve"))
        s_ac = _stk.enter_context(nc.semaphore("s_ac"))
        s_cc = _stk.enter_context(nc.semaphore("s_cc"))
        block = _stk.enter_context(nc.Block())
        SEM = dict(ld=s_ld, st=s_st, pe=s_pe, ve=s_ve, ac=s_ac, cc=s_cc)

        def prog(role, eng):
            cnt = {k: 0 for k in SEM}
            on = lambda r: role == r
            act_done_at = []
            dve_done_at = []
            scan_it = [0]

            def bump(sem, n, instr, r):
                cnt[sem] += n
                if instr is not None and role == r:
                    instr.then_inc(SEM[sem], n)

            def wt(r, sem, val=None):
                if role == r:
                    eng.wait_ge(SEM[sem], cnt[sem] if val is None else val)

            def ld(dst, src):
                i = eng.dma_start(out=dst, in_=src) if on("y") else None
                bump("ld", 16, i, "y")

            def st(dst, src):
                i = eng.dma_start(out=dst, in_=src) if on("g") else None
                bump("st", 16, i, "g")

            def V(fn, sem=None):
                i = fn(eng) if on("v") else None
                if sem is not None:
                    bump(sem, 1, i, "v")

            def S(fn, sem=None):
                i = fn(eng) if on("s") else None
                if sem is not None:
                    bump(sem, 1, i, "s")

            def TE(fn, sem=None):
                i = fn(eng) if on("t") else None
                if sem is not None:
                    bump(sem, 1, i, "t")

            def bcast_row(src_row, hs, dst):
                wt("t", "ve", cnt["ve"])
                TE(lambda e: e.matmul(pA[:, 0:512], ones_rf[:], src_row[:, hs], start=True, stop=True), "pe")
                wt("v", "pe")
                V(lambda e: e.tensor_copy(dst[:, hs], pA[:, 0:512]), "ve")

            V(lambda e: e.memset(ones_rb[:], 1.0))
            V(lambda e: e.memset(ones_rf[:], 1.0))
            V(lambda e: e.memset(ones_cb[:], 1.0))
            V(lambda e: e.memset(ones_cf[:], 1.0))
            V(lambda e: e.memset(epsr[:, 0:1], LN_EPS))
            V(lambda e: e.memset(epsr[:, 1:2], LN_EPS * LN_EPS), "ve")
            ld(lnp[:, 0:NDT], ln1w)
            ld(lnp[:, NDT:2 * NDT], ln1b)
            ld(lnp[:, 2 * NDT:3 * NDT], ln2w)
            ld(lnp[:, 3 * NDT:4 * NDT], ln2b)
            wt("t", "ve", cnt["ve"])

            xt = ar1[:].bitcast(F32)
            sqf = ar2[:].bitcast(F32)
            for s in range(NSUP):
                sl = slice(s * T, (s + 1) * T)
                wt("y", "pe", cnt["pe"])
                wt("y", "ve", cnt["ve"])
                for dti in range(NDT):
                    ld(xt[:, dti * T:(dti + 1) * T], x_loc[dti * 128:(dti + 1) * 128, sl])
                wt("s", "ld")
                for dti in range(NDT):
                    S(lambda e, i=dti: e.activation(out=sqf[:, i * T:(i + 1) * T], in_=xt[:, i * T:(i + 1) * T], func=AF.Square),
                      "ac" if dti == NDT - 1 else None)
                wt("t", "ac")
                wt("t", "ld")
                for h in range(2):
                    hs = slice(h * 512, (h + 1) * 512)
                    wt("t", "ve", cnt["ve"])
                    for dti in range(NDT):
                        TE(lambda e, i=dti, hs=hs: e.matmul(pR[:, :], ones_cf[:], xt[:, i * T:(i + 1) * T][:, hs],
                                                           start=(i == 0), stop=(i == NDT - 1)))
                    for dti in range(NDT):
                        TE(lambda e, i=dti, hs=hs: e.matmul(pR2[:, :], ones_cf[:], sqf[:, i * T:(i + 1) * T][:, hs],
                                                            start=(i == 0), stop=(i == NDT - 1)),
                           "pe" if dti == NDT - 1 else None)
                    wt("v", "pe")
                    V(lambda e, hs=hs: e.tensor_scalar(out=row_mu[:, hs], in0=pR[:, :], scalar1=1.0 / D, scalar2=None, op0=ALU.mult))
                    V(lambda e, hs=hs: e.tensor_scalar(out=row_u[:, hs], in0=pR2[:, :], scalar1=1.0 / D, scalar2=None, op0=ALU.mult),
                      "ve")
                wt("s", "ve")
                S(lambda e: e.activation(out=row_v, in_=row_mu, func=AF.Square), "ac")
                wt("v", "ac")
                wt("v", "ve")
                V(lambda e: e.tensor_sub(row_u, row_u, row_v), "ve")
                wt("s", "ve")
                S(lambda e: e.activation(out=row_v, in_=row_u, func=AF.Ln, bias=epsr[:, 0:1]), "ac")
                wt("s", "ac")
                S(lambda e: e.activation(out=row_v, in_=row_v, func=AF.Exp, scale=-0.5), "ac")
                wt("s", "ac")
                S(lambda e: e.activation(out=row_w, in_=row_u, func=AF.Ln, scale=1.0 + LN_EPS, bias=epsr[:, 1:2]), "ac")
                wt("s", "ac")
                S(lambda e: e.activation(out=row_w, in_=row_w, func=AF.Exp, scale=-0.5), "ac")
                wt("v", "ac")
                V(lambda e: e.tensor_mul(row_u, row_mu, row_v), "ve")
                wt("v", "ve")
                V(lambda e: e.tensor_mul(row_mu, row_mu, row_w), "ve")
                for pass_i, (ra, rb_) in enumerate(((row_v, row_u), (row_w, row_mu))):
                    for h in range(2):
                        hs = slice(h * 512, (h + 1) * 512)
                        bcast_row(ra, hs, bcst[:, 0:1024])
                        bcast_row(rb_, hs, bcst[:, 1024:2048])
                    for dti in range(NDT):
                        xs_ = xt[:, dti * T:(dti + 1) * T]
                        tmp = sqf[:, dti * T:(dti + 1) * T]
                        wt("v", "st", cnt["st"])
                        V(lambda e, xs_=xs_, tmp=tmp: e.tensor_mul(tmp, xs_, bcst[:, 0:1024]), "ve")
                        wt("v", "ve")
                        if pass_i == 0:
                            V(lambda e, tmp=tmp: e.tensor_sub(cvt[:], tmp, bcst[:, 1024:2048]), "ve")
                            wt("g", "ve")
                            st(xno_rev[dti * 128:(dti + 1) * 128, sl], cvt[:])
                        else:
                            V(lambda e, tmp=tmp: e.tensor_sub(tmp, tmp, bcst[:, 1024:2048]), "ve")
                            wt("v", "ve")
                            V(lambda e, tmp=tmp, dti=dti: e.tensor_scalar(
                                out=cvt[:], in0=tmp, scalar1=lnp[:, dti:dti + 1], scalar2=lnp[:, NDT + dti:NDT + dti + 1],
                                op0=ALU.mult, op1=ALU.add), "ve")
                            wt("g", "ve")
                            st(xn1_rev[dti * 128:(dti + 1) * 128, sl], cvt[:])
                            wt("v", "ve")
                            V(lambda e: e.tensor_copy(xct[:], cvt[:, ::-1]), "ve")
                            wt("g", "ve")
                            st(xn1_fwd[dti * 128:(dti + 1) * 128, HALF - (s + 1) * T:HALF - s * T], xct[:])

            def mamba_phase(ph, xn_src, out_rev):
                w = W[ph]
                wt("y", "ac", cnt["ac"])
                wt("y", "ve", cnt["ve"])
                wt("y", "pe", cnt["pe"])
                wt("y", "st", cnt["st"])
                ld(A_sb[:], w["Alog"])
                ld(convw_sb[:], w["convw"])
                ld(convb_sb[:], w["convb"])
                ld(dtb_sb[:], w["dtb"])
                ld(Dp_sb[:], w["Dp"])
                for ke in range(NET):
                    wt("y", "ac", cnt["ac"])
                    ld(wstg[:, 0:192], w["WxpT"][ke * 128:(ke + 1) * 128, :])
                    wt("s", "ld")
                    S(lambda e, ke=ke: e.activation(out=wxp_sb[:, ke * 192:(ke + 1) * 192], in_=wstg[:, 0:192], func=AF.Copy), "ac")
                for hh in range(2):
                    wt("y", "ac", cnt["ac"])
                    ld(wstg[0:64, 0:1024], w["WdtT"][:, hh * 1024:(hh + 1) * 1024])
                    wt("s", "ld")
                    S(lambda e, hh=hh: e.activation(out=wdt_sb[:, hh * 1024:(hh + 1) * 1024], in_=wstg[0:64, 0:1024], func=AF.Copy), "ac")
                wt("s", "ld")
                S(lambda e: e.activation(out=A_sb[:], in_=A_sb[:], func=AF.Exp), "ac")
                wt("v", "ac")
                V(lambda e: e.tensor_scalar(out=A_sb[:], in0=A_sb[:], scalar1=-1.0, scalar2=None, op0=ALU.mult), "ve")
                if ph == "p1":
                    V(lambda e: e.memset(state[:], 0.0), "ve")
                wt("s", "ve", cnt["ve"])

                cap = dict(wstg_ac=cnt["ac"], wsb_pe=[], xp_evict_ve=[], zsilu_ac=[],
                           xn1_pe=cnt["pe"], bcast_pe=cnt["pe"], evict_ac=cnt["ac"],
                           projcvt_ac=[], projmm_pe=[], conv_ve=[], xcz_st=[], last_win_mm=cnt["pe"],
                           wout_ve=cnt["ve"], wout_pe=cnt["pe"], l23_pe=cnt["pe"], l23_ve=cnt["ve"])
                proj_idx = [0]

                def emit_xn1_load(s):
                    wt("y", "pe", cap["last_win_mm"])
                    for dti in range(NDT):
                        ld(xn1_sb[:, dti * T:(dti + 1) * T], xn_src[dti * 128:(dti + 1) * 128, slice(s * T, (s + 1) * T)])
                    cap["xn1_ld"] = cnt["ld"]

                def emit_proj1_pe(s, et):
                    j = proj_idx[0]
                    par = j % 2
                    wsb = wsb0 if par == 0 else wsb1
                    wt("y", "ac", cap["wstg_ac"])
                    for kd in range(NDT):
                        ld(wstg[:, kd * 128:(kd + 1) * 128], w["WinT"][kd * 128:(kd + 1) * 128, et * 128:et * 128 + 128])
                    for kd in range(NDT):
                        ld(wstg[:, 1024 + kd * 128:1024 + (kd + 1) * 128],
                           w["WinT"][kd * 128:(kd + 1) * 128, E + et * 128:E + et * 128 + 128])
                    wt("s", "ld")
                    wt("s", "pe", cap["l23_pe"])
                    wt("s", "ve", cap["l23_ve"])
                    if j >= 2:
                        wt("s", "pe", cap["wsb_pe"][j - 2])
                    S(lambda e, wsb=wsb: e.activation(out=wsb[:], in_=wstg[:], func=AF.Copy), "ac")
                    cap["wstg_ac"] = cnt["ac"]
                    cap["projcvt_ac"].append(cnt["ac"])
                    wt("t", "ac", cnt["ac"])
                    wt("t", "ld", cap["xn1_ld"])
                    wt("t", "ve", cap["wout_ve"])
                    if j >= 1:
                        wt("t", "ve", cap["xp_evict_ve"][j - 1])
                        wt("t", "ac", cap["zsilu_ac"][j - 1]) if len(cap["zsilu_ac"]) >= j else None
                    for kd in range(NDT):
                        for c2 in range(2):
                            TE(lambda e, kd=kd, c2=c2, wsb=wsb: e.matmul(
                                pB[:, c2 * 512:(c2 + 1) * 512], wsb[:, kd * 128:(kd + 1) * 128],
                                xn1_sb[:, kd * T + c2 * 512:kd * T + (c2 + 1) * 512],
                                start=(kd == 0), stop=(kd == NDT - 1)),
                               "pe" if (kd == NDT - 1 and c2 == 1) else None)
                    for kd in range(NDT):
                        for c2 in range(2):
                            TE(lambda e, kd=kd, c2=c2, wsb=wsb: e.matmul(
                                pX[:, c2 * 512:(c2 + 1) * 512], wsb[:, 1024 + kd * 128:1024 + (kd + 1) * 128],
                                xn1_sb[:, kd * T + c2 * 512:kd * T + (c2 + 1) * 512],
                                start=(kd == 0), stop=(kd == NDT - 1)),
                               "pe" if (kd == NDT - 1 and c2 == 1) else None)
                    cap["wsb_pe"].append(cnt["pe"])
                    cap["projmm_pe"].append(cnt["pe"])
                    cap["last_win_mm"] = cnt["pe"]
                    proj_idx[0] += 1

                def emit_proj1_dve(s, et):
                    j = len(cap["conv_ve"])
                    wt("v", "pe", cap["projmm_pe"][j])
                    if j >= 1:
                        wt("v", "st", cap["xcz_st"][j - 1])
                    V(lambda e, et=et: e.tensor_copy(xp_ext[:, 1:4], state[:, 1024 + et * 3:1024 + et * 3 + 3]))
                    V(lambda e: e.tensor_copy(xp_ext[:, 4:4 + T], pB[:, :]), "ve")
                    cap["xp_evict_ve"].append(cnt["ve"])
                    V(lambda e, et=et: e.tensor_scalar(out=cvt[:], in0=xp_ext[:, 1:1 + T],
                                                       scalar1=convw_sb[:, et * 4:et * 4 + 1], scalar2=None, op0=ALU.mult))
                    for k in (1, 2, 3):
                        V(lambda e, k=k, et=et: e.scalar_tensor_tensor(
                            out=cvt[:], in0=xp_ext[:, 1 + k:1 + k + T], scalar=convw_sb[:, et * 4 + k:et * 4 + k + 1],
                            in1=cvt[:], op0=ALU.mult, op1=ALU.add), "ve" if k == 3 else None)
                    wt("v", "ve")
                    V(lambda e, et=et: e.tensor_copy(state[:, 1024 + et * 3:1024 + et * 3 + 3], xp_ext[:, 1 + T:4 + T]), "ve")
                    cap["conv_ve"].append(cnt["ve"])

                def emit_proj1_act(s, et):
                    j = len(cap["zsilu_ac"])
                    sl = slice(s * T, (s + 1) * T)
                    wt("s", "ve", cap["conv_ve"][j])
                    if j >= 1:
                        wt("s", "st", cap["xcz_st"][j - 1])
                    S(lambda e, et=et: e.activation(out=xct[:], in_=cvt[:], func=AF.Silu, bias=convb_sb[:, et:et + 1]))
                    S(lambda e: e.activation(out=zt[:], in_=pX[:, :], func=AF.Silu), "ac")
                    cap["zsilu_ac"].append(cnt["ac"])
                    wt("g", "ac", cnt["ac"])
                    st(xc_scr[et * 128:(et + 1) * 128, sl], xct[:])
                    st(z_scr[et * 128:(et + 1) * 128, sl], zt[:])
                    cap["xcz_st"].append(cnt["st"])

                def emit_loop23(s):
                    sl = slice(s * T, (s + 1) * T)
                    pe_at2 = []
                    for ke in range(NET):
                        par = ke % 2
                        dst = (wsb0 if par == 0 else wsb1)[:, 0:T]
                        wt("y", "st", cap["xcz_st"][-1] if cap["xcz_st"] else cnt["st"])
                        if ke >= 2:
                            wt("y", "pe", pe_at2[ke - 2])
                        else:
                            wt("y", "pe", cap["wsb_pe"][-1] if cap["wsb_pe"] else cnt["pe"])
                        ld(dst, xc_scr[ke * 128:(ke + 1) * 128, sl])
                        wt("t", "ld")
                        for c2 in range(2):
                            TE(lambda e, ke=ke, c2=c2, dst=dst: e.matmul(
                                pX[:, c2 * 512:(c2 + 1) * 512], wxp_sb[:, ke * 192:ke * 192 + 128],
                                dst[:, c2 * 512:(c2 + 1) * 512], start=(ke == 0), stop=(ke == NET - 1)))
                        for c2 in range(2):
                            TE(lambda e, ke=ke, c2=c2, dst=dst: e.matmul(
                                pB[0:64, c2 * 512:(c2 + 1) * 512], wxp_sb[:, ke * 192 + 128:(ke + 1) * 192],
                                dst[:, c2 * 512:(c2 + 1) * 512], start=(ke == 0), stop=(ke == NET - 1)),
                               "pe" if c2 == 1 else None)
                        pe_at2.append(cnt["pe"])
                    wt("v", "pe")
                    wt("v", "st", cnt["st"])
                    V(lambda e: e.tensor_copy(dtlo_bf[:], pX[0:64, :]))
                    V(lambda e: e.tensor_copy(bstg[:], pX[64:128, :]), "ve")
                    wt("g", "ve")
                    st(bc_scr[0:64, sl], bstg[:])
                    V(lambda e: e.tensor_copy(bstg2[:], pB[0:64, :]), "ve")
                    wt("g", "ve")
                    st(bc_scr[64:128, sl], bstg2[:])
                    for et in range(NET):
                        pp = pA if et % 2 == 0 else pX
                        wt("t", "ve", cnt["ve"])
                        wt("t", "ac", cnt["ac"] - (4 if et >= 1 else 0))
                        for c2 in range(2):
                            TE(lambda e, et=et, c2=c2, pp=pp: e.matmul(
                                pp[:, c2 * 512:(c2 + 1) * 512], wdt_sb[:, et * 128:(et + 1) * 128],
                                dtlo_bf[:, c2 * 512:(c2 + 1) * 512], start=True, stop=True),
                               "pe" if c2 == 1 else None)
                        wt("s", "pe")
                        wt("s", "ve", cnt["ve"])
                        S(lambda e, et=et, pp=pp: e.activation(out=cvt[:], in_=pp[:, :], func=AF.Abs, bias=dtb_sb[:, et:et + 1]), "ac")
                        wt("s", "ac")
                        S(lambda e: e.activation(out=cvt[:], in_=cvt[:], func=AF.Exp, scale=-1.0), "ac")
                        wt("s", "ac")
                        S(lambda e: e.activation(out=cvt[:], in_=cvt[:], func=AF.Ln, bias=ones_cf[:, 0:1]), "ac")
                        wt("s", "ac")
                        S(lambda e, et=et, pp=pp: e.activation(out=zt[:], in_=pp[:, :], func=AF.Relu, bias=dtb_sb[:, et:et + 1]), "ac")
                        par = et % 2
                        dst = (wsb0 if par == 0 else wsb1)[:, 0:T]
                        wt("y", "ve", cnt["ve"])
                        ld(dst, xc_scr[et * 128:(et + 1) * 128, sl])
                        wt("v", "ac")
                        wt("v", "ld")
                        wt("v", "st", cnt["st"])
                        V(lambda e: e.tensor_add(xct[:], cvt[:], zt[:]))
                        V(lambda e, dst=dst: e.tensor_mul(upt[:], xct[:], dst), "ve")
                        wt("g", "ve")
                        st(dt_scr[et * 128:(et + 1) * 128, sl], xct[:])
                        st(dtx_scr[et * 128:(et + 1) * 128, sl], upt[:])
                    cap["l23_pe"] = cnt["pe"]
                    cap["l23_ve"] = cnt["ve"]

                emit_xn1_load(0)
                for et in range(NET):
                    emit_proj1_pe(0, et)
                    emit_proj1_dve(0, et)
                    emit_proj1_act(0, et)
                emit_loop23(0)

                for s in range(NSUP):
                    sl = slice(s * T, (s + 1) * T)
                    nxt = s + 1 if s + 1 < NSUP else None
                    dA_ = ar1[:, 0:8192]
                    u_ = ar1[:, 8192:12288]
                    h_ = ar1[:, 12288:16384]
                    BbA = ar3[:, 0:4096]
                    CbA = ar3[:, 4096:8192]
                    BbB = bcst[:].bitcast(BF16)
                    CbB = bcD[:]
                    yacc = ar2[:]
                    if nxt is not None:
                        emit_xn1_load(nxt)

                    def emit_bcast(ng, prev_parity_ve):
                        BB2 = (BbA, CbA) if ng % 2 == 0 else (BbB, CbB)
                        if ng == 0:
                            wt("y", "st", cnt["st"])
                            wt("y", "ve", cnt["ve"])
                            wt("y", "ac", cnt["ac"])
                        for bi, BB in enumerate(BB2):
                            wt("y", "pe", cap["bcast_pe"])
                            ld(brow[:].rearrange("o (p t) -> o p t", p=G),
                               bc_scr[bi * 64 + ng * G:bi * 64 + (ng + 1) * G, sl].unsqueeze(0))
                            wt("t", "ld")
                            if prev_parity_ve is not None:
                                wt("t", "ve", prev_parity_ve)
                            wt("t", "ac", cap["evict_ac"])
                            for i in range(G):
                                for c2 in range(2):
                                    TE(lambda e, i=i, c2=c2: e.matmul(
                                        pA[:, c2 * 512:(c2 + 1) * 512], ones_rb[:],
                                        brow[:, i * T + c2 * 512:i * T + (c2 + 1) * 512], start=True, stop=True),
                                       "pe" if c2 == 1 else None)
                                wt("s", "pe")
                                if bi == 0 and i == 0 and prev_parity_ve is not None:
                                    wt("s", "ve", prev_parity_ve)
                                S(lambda e, i=i, BB=BB: e.activation(out=BB[:, i * T:(i + 1) * T], in_=pA[:, :], func=AF.Copy), "ac")
                                if not (bi == 1 and i == G - 1):
                                    wt("t", "ac", cnt["ac"])
                            cap["bcast_pe"] = cnt["pe"]
                        cap["evict_ac"] = cnt["ac"]
                    ng_end_ve = {}

                    emit_bcast(0, dve_done_at[-1] if dve_done_at else None)
                    for ng in range(NG):
                        Bb = BbA if ng % 2 == 0 else BbB
                        Cb = CbA if ng % 2 == 0 else CbB
                        if nxt is not None:
                            emit_proj1_pe(nxt, ng)
                        for et in range(NET):
                            it = scan_it[0]
                            par = it % 2
                            dts = xno_sb[:, par * T:(par + 1) * T]
                            dxs = xno_sb[:, 2048 + par * T:2048 + (par + 1) * T]
                            dab = dA_[:, par * 4096:(par + 1) * 4096]
                            if it >= 2:
                                wt("y", "ac", act_done_at[it - 2])
                                wt("y", "ve", dve_done_at[it - 2])
                            ld(dts, dt_scr[et * 128:(et + 1) * 128, sl])
                            ld(dxs, dtx_scr[et * 128:(et + 1) * 128, sl])
                            wt("s", "ld")
                            if it >= 2:
                                wt("s", "ve", dve_done_at[it - 2])
                            for i in range(G):
                                col = et * N + ng * G + i
                                S(lambda e, i=i, col=col, dab=dab, dts=dts: e.activation(
                                    out=dab[:, i * T:(i + 1) * T], in_=dts, func=AF.Exp,
                                    scale=A_sb[:, col:col + 1]), "ac" if i == G - 1 else None)
                            act_done_at.append(cnt["ac"])
                            wt("v", "ac", act_done_at[-1])
                            wt("v", "ld")
                            V(lambda e, dxs=dxs: e.tensor_tensor(
                                out=u_[:].rearrange("p (n t) -> p n t", n=G),
                                in0=dxs.unsqueeze(1).broadcast_to([128, G, T]),
                                in1=Bb[:].rearrange("p (n t) -> p n t", n=G), op=ALU.mult))
                            for i in range(G):
                                col = et * N + ng * G + i
                                V(lambda e, i=i, col=col, dab=dab: e.tensor_tensor_scan(
                                    out=h_[:, i * T:(i + 1) * T], data0=dab[:, i * T:(i + 1) * T],
                                    data1=u_[:, i * T:(i + 1) * T], initial=state[:, col:col + 1],
                                    op0=ALU.mult, op1=ALU.add), "ve" if i == G - 1 else None)
                            tts_done = cnt["ve"]
                            V(lambda e, Cb=Cb: e.tensor_mul(u_[:], h_[:], Cb[:]))
                            V(lambda e: e.tensor_add(u_[:, 0:2 * T], u_[:, 0:2 * T], u_[:, 2 * T:4 * T]))
                            ya = yacc[:, et * T:(et + 1) * T]
                            if ng == 0:
                                if et == 0:
                                    wt("v", "pe", cap["wout_pe"])
                                V(lambda e, ya=ya: e.tensor_add(ya, u_[:, 0:T], u_[:, T:2 * T]), "ve")
                            else:
                                V(lambda e: e.tensor_add(u_[:, 0:T], u_[:, 0:T], u_[:, T:2 * T]))
                                V(lambda e, ya=ya: e.tensor_add(ya, ya, u_[:, 0:T]), "ve")
                            c0 = et * N + ng * G
                            V(lambda e, c0=c0: e.tensor_copy(
                                state[:, c0:c0 + G],
                                h_[:].rearrange("p (n t) -> p n t", n=G)[:, :, T - 1:T].squeeze(2)), "ve")
                            dve_done_at.append(cnt["ve"])
                            scan_it[0] += 1
                            if et == 2 and ng + 1 < NG:
                                emit_bcast(ng + 1, ng_end_ve.get(ng - 1, dve_done_at[-1] if ng == 0 else None))
                        ng_end_ve[ng] = dve_done_at[-1]
                        if nxt is not None:
                            emit_proj1_dve(nxt, ng)
                            emit_proj1_act(nxt, ng)
                    wt("y", "st", cnt["st"])
                    for et in range(NET):
                        wt("y", "ve", cnt["ve"])
                        ld(yxc[:], xc_scr[et * 128:(et + 1) * 128, sl])
                        ld(yz[:], z_scr[et * 128:(et + 1) * 128, sl])
                        ya = yacc[:, et * T:(et + 1) * T]
                        wt("v", "ld")
                        V(lambda e, et=et, ya=ya: e.scalar_tensor_tensor(
                            out=ya, in0=yxc[:], scalar=Dp_sb[:, et:et + 1], in1=ya, op0=ALU.mult, op1=ALU.add))
                        V(lambda e, ya=ya: e.tensor_mul(ya, ya, yz[:]), "ve")
                    if nxt is not None:
                        emit_loop23(nxt)
                    wcvt_ac = []
                    for dti in range(NDT):
                        stg = wstg if dti % 2 == 0 else bcst
                        wt("y", "ve", cnt["ve"])
                        if dti >= 2:
                            wt("y", "ac", wcvt_ac[dti - 2])
                        else:
                            wt("y", "ac", cap["wstg_ac"])
                        wt("y", "pe", cnt["pe"] if dti == 0 else cnt["pe"])
                        for ke in range(NET):
                            ld(stg[:, ke * 128:(ke + 1) * 128], w["WoutT"][ke * 128:(ke + 1) * 128, dti * 128:(dti + 1) * 128])
                        wt("s", "ld")
                        wt("s", "pe", cnt["pe"])
                        S(lambda e, stg=stg: e.activation(out=wout_sb[:], in_=stg[:], func=AF.Copy), "ac")
                        cap["wstg_ac"] = cnt["ac"]
                        wcvt_ac.append(cnt["ac"])
                        for ts2 in range(2):
                            wt("t", "ve")
                            wt("t", "ac", cnt["ac"])
                            wt("t", "st", cnt["st"])
                            for ke in range(NET):
                                TE(lambda e, ke=ke, ts2=ts2: e.matmul(
                                    pB[:, 0:512], wout_sb[:, ke * 128:(ke + 1) * 128],
                                    yacc[:, ke * T + ts2 * 512:ke * T + ts2 * 512 + 512],
                                    start=(ke == 0), stop=(ke == NET - 1)), "pe" if ke == NET - 1 else None)
                            wt("v", "pe")
                            wt("v", "st", cnt["st"])
                            if out_rev:
                                V(lambda e: e.tensor_copy(wrev[:], pB[:, 0:512][:, ::-1]), "ve")
                                slot = HALF - (s * T + ts2 * 512) - 512
                            else:
                                V(lambda e: e.tensor_copy(wrev[:], pB[:, 0:512]), "ve")
                                slot = s * T + ts2 * 512
                            wt("g", "ve")
                            st((mamba_a if out_rev else mamba_b)[dti * 128:(dti + 1) * 128, slot:slot + 512], wrev[:])
                    cap["wout_ve"] = cnt["ve"]
                    cap["wout_pe"] = cnt["pe"]

            mamba_phase("p1", xn1_fwd, out_rev=True)

            wt("g", "ve", cnt["ve"])
            st(cc_in_t.ap(), state[:])
            wt("g", "st")
            gcc = eng.collective_compute(
                "AllReduce", ALU.add, replica_groups=[[0, 1], [2, 3], [4, 5], [6, 7]],
                ins=[cc_in_t.ap().opt()], outs=[cc_out_t.ap().opt()],
            ) if on("g") else None
            bump("cc", 1, gcc, "g")
            wt("g", "cc")
            i2 = eng.dma_start(out=blob2[:], in_=cc_out_t.ap()) if on("g") else None
            bump("st", 16, i2, "g")
            wt("v", "st")
            V(lambda e: e.tensor_sub(state[:], blob2[:], state[:]), "ve")

            mamba_phase("p2", xn1_rev, out_rev=False)

            sslab = ar3[:]
            def hs_ap(ht):
                return ar1[:, ht * T:(ht + 1) * T] if ht < 16 else ar2[:, (ht - 16) * T:(ht - 15) * T]
            for s in range(NSUP):
                sl = slice(s * T, (s + 1) * T)
                wt("y", "ve", cnt["ve"])
                wt("y", "st", cnt["st"])
                wt("y", "pe", cnt["pe"])
                wt("y", "ac", cnt["ac"])
                for dti in range(NDT):
                    rows = slice(dti * 128, (dti + 1) * 128)
                    ld(xn1_sb[:, dti * T:(dti + 1) * T], xn1_rev[rows, sl])
                    ld(xno_sb[:, dti * T:(dti + 1) * T], xno_rev[rows, sl])
                    ld(cvt[:], mamba_a[rows, sl])
                    ld(upt[:], mamba_b[rows, sl])
                    wt("v", "ld")
                    V(lambda e, dti=dti: e.tensor_add(sslab[:, dti * T:(dti + 1) * T], cvt[:], upt[:]), "ve")
                    wt("y", "ve")
                wt("s", "ve")
                for dti in range(NDT):
                    S(lambda e, i=dti: e.activation(out=ar2[:, 8192 + i * T:8192 + (i + 1) * T],
                                                    in_=sslab[:, i * T:(i + 1) * T], func=AF.Square),
                      "ac" if dti == NDT - 1 else None)
                wt("t", "ac")
                for h in range(2):
                    hs = slice(h * 512, (h + 1) * 512)
                    wt("t", "ve", cnt["ve"])
                    for dti in range(NDT):
                        TE(lambda e, i=dti, hs=hs: e.matmul(pR[:, :], ones_cb[:], sslab[:, i * T:(i + 1) * T][:, hs],
                                                           start=(i == 0), stop=(i == NDT - 1)))
                    for dti in range(NDT):
                        TE(lambda e, i=dti, hs=hs: e.matmul(pR2[:, :], ones_cb[:],
                                                            ar2[:, 8192 + i * T:8192 + (i + 1) * T][:, hs],
                                                            start=(i == 0), stop=(i == NDT - 1)),
                           "pe" if dti == NDT - 1 else None)
                    wt("v", "pe")
                    V(lambda e, hs=hs: e.tensor_scalar(out=row_mu[:, hs], in0=pR[:, :], scalar1=1.0 / D, scalar2=None, op0=ALU.mult))
                    V(lambda e, hs=hs: e.tensor_scalar(out=row_u[:, hs], in0=pR2[:, :], scalar1=1.0 / D, scalar2=None, op0=ALU.mult),
                      "ve")
                wt("s", "ve")
                S(lambda e: e.activation(out=row_v, in_=row_mu, func=AF.Square), "ac")
                wt("v", "ac")
                V(lambda e: e.tensor_sub(row_u, row_u, row_v), "ve")
                wt("s", "ve")
                S(lambda e: e.activation(out=row_v, in_=row_u, func=AF.Ln, bias=epsr[:, 0:1]))
                S(lambda e: e.activation(out=row_v, in_=row_v, func=AF.Exp, scale=-0.5), "ac")
                wt("v", "ac")
                wt("v", "ve")
                V(lambda e: e.tensor_mul(row_u, row_mu, row_v), "ve")
                for h in range(2):
                    hs = slice(h * 512, (h + 1) * 512)
                    bcast_row(row_v, hs, bcst[:, 0:1024])
                    bcast_row(row_u, hs, bcst[:, 1024:2048])
                for dti in range(NDT):
                    ss = sslab[:, dti * T:(dti + 1) * T]
                    V(lambda e, ss=ss: e.tensor_mul(ss, ss, bcst[:, 0:1024]))
                    V(lambda e, ss=ss: e.tensor_sub(ss, ss, bcst[:, 1024:2048]))
                    V(lambda e, ss=ss, dti=dti: e.tensor_scalar(
                        out=ss, in0=ss, scalar1=lnp[:, 2 * NDT + dti:2 * NDT + dti + 1],
                        scalar2=lnp[:, 3 * NDT + dti:3 * NDT + dti + 1], op0=ALU.mult, op1=ALU.add))
                    V(lambda e, ss=ss, dti=dti: e.tensor_add(ss, ss, xn1_sb[:, dti * T:(dti + 1) * T]),
                      "ve" if dti == NDT - 1 else None)
                pe_hist = []
                ac_hist = []
                for ht in range(NHT):
                    stg = wstg if ht % 2 == 0 else bcst
                    wsb = wsb0 if ht % 2 == 0 else wsb1
                    if ht >= 2:
                        wt("y", "ve", ac_hist[ht - 2])
                    else:
                        wt("y", "ve", cnt["ve"])
                    for kd in range(NDT):
                        ld(stg[:, kd * 128:(kd + 1) * 128].bitcast(I32), wgT[kd * 128:(kd + 1) * 128, ht * 128:(ht + 1) * 128])
                    for kd in range(NDT):
                        ld(stg[:, 1024 + kd * 128:1024 + (kd + 1) * 128].bitcast(I32),
                           wuT[kd * 128:(kd + 1) * 128, ht * 128:(ht + 1) * 128])
                    wt("v", "ld")
                    if ht >= 2:
                        wt("v", "pe", pe_hist[ht - 2])
                    V(lambda e, stg=stg, wsb=wsb: e.tensor_scalar(out=wsb[:], in0=stg[:].bitcast(I32), scalar1=QS,
                                                                  scalar2=None, op0=ALU.mult), "ve")
                    ac_hist.append(cnt["ve"])
                    wt("t", "ve", cnt["ve"])
                    if ht >= 1:
                        wt("t", "ve", cnt["ve"])
                    for c2 in range(2):
                        for kd in range(NDT):
                            TE(lambda e, kd=kd, c2=c2, wsb=wsb: e.matmul(
                                pA[:, c2 * 512:(c2 + 1) * 512], wsb[:, kd * 128:(kd + 1) * 128],
                                xno_sb[:, kd * T + c2 * 512:kd * T + (c2 + 1) * 512],
                                start=(kd == 0), stop=(kd == NDT - 1)))
                    for c2 in range(2):
                        for kd in range(NDT):
                            TE(lambda e, kd=kd, c2=c2, wsb=wsb: e.matmul(
                                pB[:, c2 * 512:(c2 + 1) * 512], wsb[:, 1024 + kd * 128:1024 + (kd + 1) * 128],
                                xno_sb[:, kd * T + c2 * 512:kd * T + (c2 + 1) * 512],
                                start=(kd == 0), stop=(kd == NDT - 1)),
                               "pe" if (kd == NDT - 1 and c2 == 1) else None)
                    pe_hist.append(cnt["pe"])
                    wt("s", "pe")
                    S(lambda e: e.activation(out=cvt[:], in_=pA[:, :], func=AF.Silu), "ac")
                    wt("v", "ac")
                    V(lambda e: e.tensor_copy(upt[:], pB[:, :]))
                    V(lambda e, ht=ht: e.tensor_mul(hs_ap(ht), cvt[:], upt[:]), "ve")
                if debug and s == 0:
                    wt("g", "ve", cnt["ve"])
                    for ht in range(NHT):
                        st(h_dbg[ht * 128:(ht + 1) * 128, :], hs_ap(ht))
                for dti in range(NDT):
                    stg = wstg if dti % 2 == 0 else bcst
                    wsb = wsb0 if dti % 2 == 0 else wsb1
                    wt("y", "ve", cnt["ve"])
                    wt("y", "pe", cnt["pe"])
                    for kh in range(12):
                        ld(stg[:, kh * 128:(kh + 1) * 128].bitcast(I32),
                           wdT[kh * 128:(kh + 1) * 128, dti * 128:(dti + 1) * 128])
                    stg2 = bcst if dti % 2 == 0 else wstg
                    for kh in range(12, 24):
                        ld(stg2[:, (kh - 12) * 128:(kh - 11) * 128].bitcast(I32),
                           wdT[kh * 128:(kh + 1) * 128, dti * 128:(dti + 1) * 128])
                    wt("v", "ld")
                    V(lambda e, stg=stg, wsb=wsb: e.tensor_scalar(out=wsb[:, 0:1536], in0=stg[:, 0:1536].bitcast(I32),
                                                                  scalar1=QS, scalar2=None, op0=ALU.mult), "ve")
                    wt("t", "ve", cnt["ve"])
                    for c2 in range(2):
                        for kh in range(12):
                            TE(lambda e, kh=kh, c2=c2, wsb=wsb: e.matmul(
                                pX[:, c2 * 512:(c2 + 1) * 512], wsb[:, kh * 128:(kh + 1) * 128],
                                hs_ap(kh)[:, c2 * 512:(c2 + 1) * 512],
                                start=(kh == 0), stop=False),
                               "pe" if (kh == 11 and c2 == 1) else None)
                    wt("v", "pe")
                    V(lambda e, stg2=stg2, wsb=wsb: e.tensor_scalar(out=wsb[:, 0:1536], in0=stg2[:, 0:1536].bitcast(I32),
                                                                    scalar1=QS, scalar2=None, op0=ALU.mult), "ve")
                    wt("t", "ve", cnt["ve"])
                    for c2 in range(2):
                        for kh in range(12):
                            TE(lambda e, kh=kh, c2=c2, wsb=wsb: e.matmul(
                                pX[:, c2 * 512:(c2 + 1) * 512], wsb[:, kh * 128:(kh + 1) * 128],
                                hs_ap(12 + kh)[:, c2 * 512:(c2 + 1) * 512],
                                start=False, stop=(kh == 11)),
                               "pe" if (kh == 11 and c2 == 1) else None)
                    for ts2 in range(2):
                        tss = slice(ts2 * TS3, (ts2 + 1) * TS3)
                        wt("y", "st", cnt["st"])
                        ld(ot[:], x_loc[dti * 128:(dti + 1) * 128, sl][:, tss])
                        wt("s", "pe")
                        wt("s", "ld")
                        wt("s", "ve", cnt["ve"])
                        S(lambda e, ts2=ts2: e.activation(out=sgt[:], in_=pX[:, ts2 * TS3:(ts2 + 1) * TS3], func=AF.Silu), "ac")
                        wt("v", "ac")
                        wt("v", "st", cnt["st"])
                        if debug and s == 0:
                            wt("g", "ac", cnt["ac"])
                            st(g_dbg[dti * 128:(dti + 1) * 128, tss], sgt[:])
                        V(lambda e, dti=dti, tss=tss: e.tensor_mul(sgt[:], sgt[:], sslab[:, dti * T:(dti + 1) * T][:, tss]))
                        V(lambda e: e.tensor_add(ot[:], ot[:], sgt[:]), "ve")
                        wt("g", "ve")
                        st(out_feat[dti * 128:(dti + 1) * 128, sl][:, tss], ot[:])
                        wt("v", "st")

            wt("g", "st", cnt["st"])

        @block.vector
        def _(e):
            prog("v", e)

        @block.scalar
        def _(e):
            prog("s", e)

        @block.tensor
        def _(e):
            prog("t", e)

        @block.gpsimd
        def _(e):
            prog("g", e)

        @block.sync
        def _(e):
            prog("y", e)

    return nc


def _prep_core_inputs(inputs, b, role):
    x = np.asarray(inputs["L0_attn_x"])
    if role == 0:
        x_loc = np.ascontiguousarray(x[b, :HALF][::-1].T)
    else:
        x_loc = np.ascontiguousarray(x[b, HALF:].T)

    def mamba_set(m):
        pre = f"L0_attn_{m}_"
        return dict(
            WinT=np.ascontiguousarray(np.asarray(inputs[pre + "W_in"]).T),
            WxpT=np.ascontiguousarray(np.asarray(inputs[pre + "W_xp"]).T),
            WdtT=np.ascontiguousarray(np.asarray(inputs[pre + "W_dt"]).T),
            WoutT=np.ascontiguousarray(np.asarray(inputs[pre + "W_out"]).T),
            Alog=np.ascontiguousarray(np.asarray(inputs[pre + "A_log"]).reshape(NET, 128, N).transpose(1, 0, 2).reshape(128, NET * N)),
            convw=np.ascontiguousarray(np.asarray(inputs[pre + "conv_w"]).reshape(NET, 128, 4).transpose(1, 0, 2).reshape(128, NET * 4)),
            convb=np.ascontiguousarray(np.asarray(inputs[pre + "conv_b"]).reshape(NET, 128).T),
            dtb=np.ascontiguousarray(np.asarray(inputs[pre + "dt_b"]).reshape(NET, 128).T),
            Dp=np.ascontiguousarray(np.asarray(inputs[pre + "D"]).reshape(NET, 128).T),
        )

    m1, m2 = mamba_set("m1"), mamba_set("m2")
    p1, p2 = (m1, m2) if role == 0 else (m2, m1)
    im = {"x_loc": x_loc}
    for ph, mm in (("p1", p1), ("p2", p2)):
        for k, v in mm.items():
            im[f"{ph}_{k}"] = v
    for k, src in (("ln1w", "L0_attn_ln1_w"), ("ln1b", "L0_attn_ln1_b"),
                   ("ln2w", "L0_attn_ln2_w"), ("ln2b", "L0_attn_ln2_b")):
        im[k] = np.ascontiguousarray(np.asarray(inputs[src]).reshape(NDT, 128).T)
    im["wgT"] = np.ascontiguousarray(np.asarray(inputs["L0_ffn_w_gate"]).T)
    im["wuT"] = np.ascontiguousarray(np.asarray(inputs["L0_ffn_w_up"]).T)
    im["wdT"] = np.ascontiguousarray(np.asarray(inputs["L0_ffn_w_down"]).T)
    return im


def kernel(**inputs):
    nc = build_program()
    in_maps = [_prep_core_inputs(inputs, r // 2, r % 2) for r in range(8)]
    res = run_bass_kernel_spmd(nc, in_maps, core_ids=list(range(8)))
    out = np.empty((4, 8192, 1024), dtype=np.float32)
    for r in range(8):
        of = np.asarray(res.results[r]["out_feat"])
        b, role = r // 2, r % 2
        if role == 0:
            out[b, :HALF] = of.T[::-1]
        else:
            out[b, HALF:] = of.T
    return out
